# version 13
# baseline (speedup 1.0000x reference)
import numpy as np
import concourse.bass as bass
import concourse.mybir as mybir
from concourse.bass_utils import run_bass_kernel_spmd

F32 = mybir.dt.float32
AF = mybir.ActivationFunctionType
OP = mybir.AluOpType
AX = mybir.AxisListType

B, T, D = 2, 8192, 1024
DI, HN, HD, DS = 2048, 16, 128, 64
HPC, CPC = 4, 512
QT, CAP, E = 2048, 2048, 8
TC1, HALO = 1024, 128
NK1 = T // TC1
NT1 = (TC1 + HALO) // 128
TW = TC1 + HALO
TC2 = 1024
NK2 = T // TC2
NBLK = 2
NNB = DS // NBLK
NST = 4
NDA = 3
NWB = 3
EPS = 1e-5

_DEBUG = False
_PHASES = 4


class Sched:
    def __init__(self):
        self.steps = []
        self.cnt = {}
        self.own = {}
        self.order_sems = {}
        self.live_cnt = {}

    def tok(self, sem):
        k = id(sem)
        return (sem, self.cnt.get(k, 0), self.own.get(k))

    def step(self, engine, fn, waits=(), inc=None, amt=1, dma=False, drain=False,
             chain=True):
        w = []
        for t in waits:
            if t is None:
                continue
            sem, val, owner = t
            if val <= 0:
                continue
            w.append((sem, val, owner))
        tokv = None
        post = None
        if inc is not None:
            k = id(inc)
            self.cnt[k] = self.cnt.get(k, 0) + amt
            self.own[k] = None if dma else engine
            tokv = (inc, self.cnt[k], self.own[k])
            post = self.cnt[k]
        self.steps.append((engine, fn, w, inc, amt, chain, post))
        return tokv

    def dma(self, engine, fn, inc, waits=()):
        return self.step(engine, fn, waits=waits, inc=inc, amt=16, dma=True)

    def replay(self, engine, eng):
        hi = {}
        osem = self.order_sems.get(engine)
        ocnt = 0
        prev_upd = None
        for (e, fn, waits, inc, amt, chain, post) in self.steps:
            if e != engine:
                continue
            if osem is not None and chain and prev_upd is not None:
                psem, pval = prev_upd
                k = id(psem)
                if hi.get(k, 0) < pval:
                    hi[k] = pval
                    eng.wait_ge(psem, pval)
            bysem = {}
            for (sem, val, owner) in waits:
                k = id(sem)
                if val > bysem.get(k, (None, 0))[1]:
                    bysem[k] = (sem, val)
            for k, (sem, val) in bysem.items():
                if hi.get(k, 0) >= val:
                    continue
                hi[k] = val
                eng.wait_ge(sem, val)
            inst = fn(eng)
            if inc is not None:
                assert inst is not None, "inc step returned no instruction"
                inst.then_inc(inc, amt)
                prev_upd = (inc, post)
            elif osem is not None:
                assert inst is not None, f"{engine} step returned no instruction"
                ocnt += 1
                inst.then_inc(osem, 1)
                prev_upd = (osem, ocnt)


def build():
    nc = bass.Bass("TRN2", target_bir_lowering=False)
    dbg = "ExternalOutput" if _DEBUG else "Internal"
    di = lambda n, s: nc.dram_tensor(n, s, F32, kind="ExternalInput").ap()
    dd = lambda n, s, k="Internal": nc.dram_tensor(n, s, F32, kind=k).ap()

    x_b = di("x_b", [T, D])
    x_q = di("x_q", [QT, D])
    w_inT = di("w_inT", [D, 2 * CPC])
    w_dtT = di("w_dtT", [CPC, 144])
    w_conv = di("w_conv", [128, 16])
    b_conv = di("b_conv", [128, 4])
    a_log = di("a_log", [HD, HPC * DS])
    d_col = di("d_col", [HD, HPC])
    dtbias = di("dtbias", [HPC, 1])
    b_dt = di("b_dt", [144, 1])
    gamma = di("gamma", [HD, HPC])
    beta = di("beta", [HD, HPC])
    w_outT = di("w_outT", [CPC, D])
    we = di("we", [D, D])
    be = di("be", [1, D])
    wg = di("wg", [D, E])
    eye = di("eye", [128, 128])
    qsel = di("qsel", [128, NK1])
    sel16 = di("sel16", [16, 4])
    step8 = di("step8", [8, 1])

    out_c = nc.dram_tensor("out_c", [QT, D], F32, kind="ExternalOutput").ap()
    x4_hbm = dd("x4_hbm", [CPC, T], dbg)
    z_hbm = dd("z_hbm", [CPC, T], dbg)
    y_hbm = dd("y_hbm", [CPC, T], dbg)
    xp_in = dd("xp_in", [144, T])
    xp_out = dd("xp_out", [144, T])
    dt_dram = dd("dt_dram", [HPC, T], dbg)
    bb_dram = dd("bb_dram", [DS, T], dbg)
    cc_dram = dd("cc_dram", [DS, T], dbg)
    sta_in = dd("sta_in", [2, T])
    sta_out = dd("sta_out", [2, T])
    mu_dram = dd("mu_dram", [1, T], dbg)
    rs_dram = dd("rs_dram", [1, T], dbg)
    loc_dram = dd("loc_dram", [E, QT], dbg)
    base_dram = dd("base_dram", [1, E], dbg)
    tot_in = dd("tot_in", [1, E])
    tot_out = dd("tot_out", [8, E])
    s_in = dd("s_in", [T, D])
    s_out = dd("s_out", [QT, D])

    ctx = []
    def sb(name, shape):
        cm = nc.sbuf_tensor(name, shape, F32)
        t = cm.__enter__(); ctx.append(cm); return t
    def pst(name, shape):
        cm = nc.psum_tensor(name, shape, F32)
        t = cm.__enter__(); ctx.append(cm); return t
    def mksem(name):
        cm = nc.semaphore(name)
        s = cm.__enter__(); ctx.append(cm); return s

    t_eye = sb("t_eye", [128, 128])
    t_ones = sb("t_ones", [128, 8])
    t_wdtT = sb("t_wdtT", [128, 4 * 144])
    t_wcv = sb("t_wcv", [128, 16])
    t_bcv = sb("t_bcv", [128, 4])
    t_a = sb("t_a", [128, HPC * DS])
    t_d = sb("t_d", [128, HPC])
    t_gb = sb("t_gb", [128, 2 * HPC])
    t_dtb = sb("t_dtb", [HPC, 1])
    t_bdt = sb("t_bdt", [128, 2])
    t_wg = sb("t_wg", [128, 8 * E])
    t_qsel = sb("t_qsel", [128, NK1])
    t_sel16 = sb("t_sel16", [16, 4])
    t_step8 = sb("t_step8", [8, 1])
    t_carry = sb("t_carry", [128, HPC * DS])
    t_mask = sb("t_mask", [128, 16 * E])
    t_gate = sb("t_gate", [128, 16])
    t_scale = sb("t_scale", [128, 16])
    t_maskT = sb("t_maskT", [8, QT])
    t_loc = sb("t_loc", [8, QT])
    t_tot = sb("t_tot", [8, 1])
    t_totL = sb("t_totL", [8, 8])
    t_small = sb("t_small", [128, 64])
    t_eps = sb("t_eps", [128, 1])
    AW = 43008
    arena = sb("arena", [128, AW])
    A = lambda off, w: arena[:, off:off + w]
    psb = [pst(f"psb{i}", [128, 512]) for i in range(8)]

    S = Sched()
    S.order_sems = {}
    g = {}
    for n in ("s_ld", "s_x0", "s_x1", "s_stat", "s_ev", "s_mm", "s_rtr",
              "s_cc", "s_pre", "s_p2k", "s_bc0", "s_bc1", "s_dA", "s_scan",
              "s_w", "s_ymm", "s_yev", "s_out", "s_norm", "s_tp",
              "s_spx4", "s_spz", "s_spxp", "s_sprt", "s_spd1", "s_spy",
              "s_spst", "s_sps0", "s_sps1", "s_spmu", "s_pay", "s_pby",
              "s_pbz", "s_pbm", "s_p4w", "s_p4b", "s_p4t", "s_p4bs",
              "s_p4bd", "s_xq0", "s_xq1", "s_p4s", "s_p4x", "s_dbg", "s_pam",
              "s_evs", "s_spy1", "s_u", "s_cy", "s_x4g", "s_dtbs",
              "s_pby1", "s_pbz1", "s_pbm1", "s_xq2", "s_xq3",
              "s_out1", "s_p4s1", "s_p4x1", "s_pay1", "s_wek", "s_wek1",
              "s_bcg0", "s_bcg1"):
        g[n] = mksem(n)

    S.order_sems = {"vector": mksem("ord_v"), "scalar": mksem("ord_s")}
    bank_free = [None] * 8
    _dbg_t = {}

    def mm_group(bank, mms, evac_fn, evac_engine, extra_mm_waits=(),
                 extra_evac_waits=(), evac_drain=False, evac_chain=False):
        mm_tok = None
        for i, fnm in enumerate(mms):
            w = list(extra_mm_waits) if i == 0 else []
            if i == 0 and bank_free[bank] is not None:
                w.append(bank_free[bank])
            mm_tok = S.step("tensor", fnm, waits=w,
                            inc=(g["s_mm"] if i == len(mms) - 1 else None))
        evt = S.step(evac_engine, evac_fn,
                     waits=[mm_tok] + list(extra_evac_waits), inc=g["s_ev"],
                     chain=evac_chain)
        bank_free[bank] = evt
        return evt, mm_tok

    o_xn = 0
    o_xz = o_xn + 8 * TW
    o_zb = o_xz + 4 * TW
    o_cv = o_zb + TW
    o_scr = o_cv + TC1
    o_xld = o_scr + 1024
    o_xp = o_xld + 2 * 1024
    o_xnorm = o_xp + 2 * TC1
    o_scr2 = o_xnorm + 2 * 1024
    o_win = o_scr2 + 1024
    o_end1 = o_win + 8 * 1024
    assert o_end1 <= AW, o_end1
    o_dtb, o_dtx, o_x4 = 0, 4 * TC2, 8 * TC2
    o_bb = 12 * TC2
    o_cb = o_bb + 2 * NBLK * TC2
    o_dA = o_cb + 2 * NBLK * TC2
    o_u = o_dA + NDA * TC2
    o_st2 = o_u + 4 * TC2
    o_w = o_st2 + NST * TC2
    o_dx4 = o_w + NWB * 2 * TC2
    o_yev = o_dx4 + TC2
    o_end2 = o_yev + 2 * TC2
    assert o_end2 <= AW, o_end2
    o_xpl = o_dA
    o_dto = o_st2
    o3_y, o3_z, o3_g = 0, 4 * TC2, 8 * TC2
    o3_mu, o3_rs = 12 * TC2, 13 * TC2
    o3_wo = 14 * TC2
    o3_s = o3_wo + 4 * 1024
    o3_sr = o3_s + 2 * 1024
    o3_end = o3_sr + T + 2048
    assert o3_end <= AW, o3_end
    o3_y2 = o3_end
    o3_z2 = o3_y2 + 4 * TC2
    o3_mu2 = o3_z2 + 4 * TC2
    o3_rs2 = o3_mu2 + TC2
    assert o3_rs2 + TC2 <= AW
    o4_xnT = 0
    o4_eo = o4_xnT + 8 * QT
    o4_we = o4_eo + 4 * 1024
    o4_s = o4_we + 8 * 1024
    o4_x = o4_s + 2 * 1024
    o4_be = o4_x + 2 * 1024
    o4_bs = o4_be + 1024
    o4_lt = o4_bs + 8
    o4_xq = o4_lt + 16 * E
    o4_g2 = o4_xq + 4 * 1024
    o4_scr = o4_g2 + 2 * 1024
    o4_end = o4_scr + 2 * 1024
    assert o4_end <= AW, o4_end

    for fn in (
        lambda e: e.dma_start(out=t_eye[:], in_=eye),
        lambda e: e.dma_start(out=A(o_win, 8 * 1024).rearrange("p (k c) -> p k c", k=8), in_=w_inT.rearrange("(k p) c -> p k c", p=128)),
        lambda e: e.dma_start(out=t_wdtT[:].rearrange("p (k c) -> p k c", k=4), in_=w_dtT.rearrange("(k p) c -> p k c", p=128)),
        lambda e: e.dma_start(out=t_wcv[:], in_=w_conv),
        lambda e: e.dma_start(out=t_bcv[:], in_=b_conv),
        lambda e: e.dma_start(out=t_a[:], in_=a_log),
        lambda e: e.dma_start(out=t_d[:], in_=d_col),
        lambda e: e.dma_start(out=t_gb[:, 0:HPC], in_=gamma),
        lambda e: e.dma_start(out=t_gb[:, HPC:2 * HPC], in_=beta),
        lambda e: e.dma_start(out=t_dtb[:], in_=dtbias),
        lambda e: e.dma_start(out=t_bdt[:, 0:1], in_=b_dt[0:128, :]),
        lambda e: e.dma_start(out=t_bdt[0:16, 1:2], in_=b_dt[128:144, :]),
        lambda e: e.dma_start(out=t_wg[:].rearrange("p (k c) -> p k c", k=8), in_=wg.rearrange("(k p) c -> p k c", p=128)),
        lambda e: e.dma_start(out=t_qsel[:], in_=qsel),
        lambda e: e.dma_start(out=t_sel16[:], in_=sel16),
        lambda e: e.dma_start(out=t_step8[:], in_=step8),
    ):
        S.dma("sync", fn, g["s_ld"])
    all_ld = S.tok(g["s_ld"])

    t_a2 = A(o_yev, HPC * DS)
    ta = S.step("scalar", lambda e: e.activation(t_a2, t_a[:], AF.Exp),
                waits=[all_ld], inc=g["s_pre"])
    a_ready = S.step("scalar", lambda e: e.mul(t_a[:], t_a2, -1.0),
                     waits=[ta], inc=g["s_pre"])
    S.step("vector", lambda e: e.memset(t_ones[:], 1.0))
    S.step("vector", lambda e: e.memset(t_eps[:], EPS))
    S.step("vector", lambda e: e.memset(t_carry[:], 0.0))
    S.step("vector", lambda e: e.memset(t_mask[:], 0.0))
    S.step("vector", lambda e: e.memset(t_gate[:], 0.0))

    def ln_tile(tl_tok, xb, scr, scrB, xnd, dbase, dstride, jcol, tpbanks,
                stats_wait, first_pe_wait=None, xnd_wait=None):
        t1 = S.step("scalar", lambda e, xb=xb, scr=scr: e.activation(
            A(scr, 1024), A(xb, 1024), AF.Copy, accum_out=t_small[:, 0:1]),
            waits=[tl_tok, stats_wait], inc=g["s_stat"], chain=False)
        t2 = S.step("scalar", lambda e, xb=xb, scrB=scrB: e.activation(
            A(scrB, 1024), A(xb, 1024), AF.Square, accum_out=t_small[:, 1:2]),
            waits=[t1], inc=g["s_stat"], chain=False)
        S.step("vector", lambda e: e.tensor_scalar_mul(
            t_small[:, 2:3], t_small[:, 0:1], 1.0 / D), waits=[t2], drain=True)
        S.step("vector", lambda e: e.tensor_scalar_mul(
            t_small[:, 3:4], t_small[:, 1:2], 1.0 / D), drain=True)
        S.step("vector", lambda e: e.tensor_tensor(
            t_small[:, 4:5], t_small[:, 2:3], t_small[:, 2:3], OP.mult),
            drain=True)
        t3 = S.step("vector", lambda e: e.tensor_tensor(
            t_small[:, 5:6], t_small[:, 3:4], t_small[:, 4:5], OP.subtract),
            inc=g["s_norm"], drain=True)
        t4 = S.step("scalar", lambda e: e.activation(
            t_small[:, 6:7], t_small[:, 5:6], AF.Sqrt, bias=t_eps[:, 0:1]),
            waits=[t3], inc=g["s_stat"], chain=False)
        S.step("vector", lambda e: e.reciprocal(
            t_small[:, 7:8], t_small[:, 6:7]), waits=[t4], drain=True)
        tn = S.step("vector", lambda e, xb=xb, xnd=xnd: e.tensor_scalar(
            A(xnd, 1024), A(xb, 1024), t_small[:, 2:3], t_small[:, 7:8],
            OP.subtract, OP.mult), waits=[xnd_wait], inc=g["s_norm"])
        last_pe, last_ev = None, None
        for kk in range(8):
            bank = tpbanks[kk % 2]
            wv = [tn]
            if bank_free[bank] is not None:
                wv.append(bank_free[bank])
            if kk == 0 and first_pe_wait is not None:
                wv.append(first_pe_wait)
            last_pe = S.step("tensor", lambda e, xnd=xnd, kk=kk, bank=bank: e.transpose(
                psb[bank][:, 0:128], A(xnd + kk * 128, 128), t_eye[:]),
                waits=wv, inc=g["s_tp"])
            last_ev = S.step("scalar", lambda e, kk=kk, bank=bank,
                             dbase=dbase, dstride=dstride, jcol=jcol: e.copy(
                arena[:, dbase + kk * dstride + jcol:dbase + kk * dstride + jcol + 128],
                psb[bank][:, 0:128]), waits=[last_pe], inc=g["s_evs"],
                chain=False)
            bank_free[bank] = last_ev
        return tn, last_pe, last_ev

    tp_pe, chunk_pe_done, tn_toks = {}, {}, {}
    x4_sp, xp_sp, z_sp = {}, {}, {}
    conv_done = {}
    dt_pe_last = {}
    z_sp_prev = None
    prev_tn = None
    NCHN = [(0, 512), (512, 512), (1024, 128)]

    for k in range(NK1):
        evs = []
        for j in range(NT1):
            tok0 = k * TC1 - HALO + j * 128
            xb = o_xld + (j % 2) * 1024
            if tok0 < 0:
                tz = None
                for kk in range(8):
                    tz = S.step("vector", lambda e, kk=kk, j=j: e.memset(
                        arena[:, o_xn + kk * TW + j * 128:o_xn + kk * TW + j * 128 + 128], 0.0),
                        inc=g["s_norm"])
                evs.append(tz)
                tp_pe[(k, j)] = None
                continue
            w = []
            if j >= 2:
                w.append(tn_toks.get((k, j - 2)))
            elif k > 0:
                w.append(tn_toks.get((k - 1, NT1 - 1 - (1 - j))))
            if j >= 2:
                w.append(tp_pe.get((k, j - 2)))
            elif k > 0:
                w.append(tp_pe.get((k - 1, NT1 - 1 - (1 - j))))
            tl = S.dma("sync", lambda e, xb=xb, tok0=tok0: e.dma_start(
                out=A(xb, 1024), in_=x_b[tok0:tok0 + 128, :]),
                g["s_x0" if j % 2 == 0 else "s_x1"], waits=w)
            fpw = chunk_pe_done.get(k - 1) if (k > 0 and j == 0) else None
            tn, lpe, lev = ln_tile(tl, xb, o_scr, o_scr2,
                                   o_xnorm + (j % 2) * 1024,
                                   o_xn, TW, j * 128, (6, 7),
                                   prev_tn, first_pe_wait=fpw)
            prev_tn = tn
            tn_toks[(k, j)] = tn
            tp_pe[(k, j)] = lpe
            evs.append(lev)

        pe_last = None
        for m in range(8):
            z_ev_group = []
            for (ni, (noff, nw)) in enumerate(NCHN):
                bank = ni
                mms = []
                for kd in range(8):
                    def fmms(e, m=m, kd=kd, noff=noff, nw=nw, bank=bank):
                        return e.matmul(
                            psb[bank][:, 0:nw],
                            arena[:, o_win + kd * 1024 + m * 128:o_win + kd * 1024 + (m + 1) * 128],
                            arena[:, o_xn + kd * TW + noff:o_xn + kd * TW + noff + nw],
                            start=(kd == 0), stop=(kd == 7))
                    mms.append(fmms)
                extra_ev = []
                if m < 4:
                    def fev(e, m=m, noff=noff, nw=nw, bank=bank):
                        return e.tensor_copy(
                            arena[:, o_xz + m * TW + noff:o_xz + m * TW + noff + nw],
                            psb[bank][:, 0:nw])
                    if k > 0:
                        extra_ev = [dt_pe_last.get(k - 1), x4_sp.get((k - 1, 3)),
                                    conv_done.get((k - 1, m))]
                else:
                    def fev(e, noff=noff, nw=nw, bank=bank):
                        return e.tensor_copy(
                            arena[:, o_zb + noff:o_zb + noff + nw],
                            psb[bank][:, 0:nw])
                    extra_ev = [z_sp_prev]
                ev, mmt = mm_group(bank, mms, fev, "vector",
                                   extra_mm_waits=evs + [all_ld],
                                   extra_evac_waits=extra_ev)
                pe_last = mmt
                if m >= 4:
                    z_ev_group.append(ev)
            if m >= 4:
                z_sp_prev = S.dma("sync", lambda e, k=k, m=m: e.dma_start(
                    out=z_hbm[(m - 4) * 128:(m - 3) * 128, k * TC1:(k + 1) * TC1],
                    in_=A(o_zb + HALO, TC1)), g["s_spz"], waits=z_ev_group)
                z_sp[(k, m - 4)] = z_sp_prev

        for ct in range(4):
            xzoff = o_xz + ct * TW
            cw = [conv_done.get((k, ct - 1)) if ct > 0 else conv_done.get((k - 1, 3))]
            t1 = S.step("vector", lambda e, xzoff=xzoff, ct=ct: e.tensor_scalar(
                A(o_cv, TC1), arena[:, xzoff + HALO - 3:xzoff + HALO - 3 + TC1],
                t_wcv[:, ct * 4:ct * 4 + 1], t_bcv[:, ct:ct + 1],
                OP.mult, OP.add), waits=cw)
            for tap in range(1, 4):
                t1 = S.step("vector", lambda e, xzoff=xzoff, ct=ct, tap=tap: e.scalar_tensor_tensor(
                    A(o_cv, TC1),
                    arena[:, xzoff + HALO - 3 + tap:xzoff + HALO - 3 + tap + TC1],
                    t_wcv[:, ct * 4 + tap:ct * 4 + tap + 1],
                    A(o_cv, TC1), OP.mult, OP.add),
                    inc=(g["s_norm"] if tap == 3 else None))
            tsl = S.step("scalar", lambda e, xzoff=xzoff: e.activation(
                arena[:, xzoff + HALO:xzoff + HALO + TC1], A(o_cv, TC1), AF.Silu),
                waits=[t1], inc=g["s_stat"])
            conv_done[(k, ct)] = tsl
            x4_sp[(k, ct)] = S.dma("sync", lambda e, k=k, ct=ct, xzoff=xzoff: e.dma_start(
                out=x4_hbm[ct * 128:(ct + 1) * 128, k * TC1:(k + 1) * TC1],
                in_=arena[:, xzoff + HALO:xzoff + HALO + TC1]), g["s_spx4"],
                waits=[tsl])

        dt_ev = []
        for (mi, (p0, pw)) in enumerate([(0, 128), (128, 16)]):
            for (ni, noff) in enumerate(range(0, TC1, 512)):
                bank = 5
                mms = []
                for kd in range(4):
                    def fdt(e, mi=mi, p0=p0, pw=pw, kd=kd, noff=noff, bank=bank):
                        return e.matmul(
                            psb[bank][0:pw, 0:512],
                            t_wdtT[:, kd * 144 + p0:kd * 144 + p0 + pw],
                            arena[:, o_xz + kd * TW + HALO + noff:
                                  o_xz + kd * TW + HALO + noff + 512],
                            start=(kd == 0), stop=(kd == 3))
                    mms.append(fdt)
                def fdtev(e, mi=mi, pw=pw, noff=noff, bank=bank):
                    col = t_bdt[0:128, 0:1] if mi == 0 else t_bdt[0:16, 1:2]
                    return e.tensor_scalar(
                        arena[0:pw, o_xp + mi * TC1 + noff:o_xp + mi * TC1 + noff + 512],
                        psb[bank][0:pw, 0:512], col, None, OP.add)
                exw = [xp_sp.get((k - 1, 1))] if k > 0 else []
                ev, mmt = mm_group(bank, mms, fdtev, "vector",
                                   extra_mm_waits=[conv_done[(k, c2)] for c2 in range(4)],
                                   extra_evac_waits=exw)
                dt_ev.append(ev)
                dt_pe_last[k] = mmt
        xp_sp[(k, 0)] = S.dma("sync", lambda e, k=k: e.dma_start(
            out=xp_in[0:128, k * TC1:(k + 1) * TC1], in_=A(o_xp, TC1)),
            g["s_spxp"], waits=dt_ev)
        xp_sp[(k, 1)] = S.dma("sync", lambda e, k=k: e.dma_start(
            out=xp_in[128:144, k * TC1:(k + 1) * TC1],
            in_=arena[0:16, o_xp + TC1:o_xp + 2 * TC1]), g["s_spxp"], waits=dt_ev)

        rt_prev_ev = None
        rpe = pe_last
        for jt in range(TC1 // 128):
            m = (k % 2) * 8 + jt
            bank = 5
            mms = []
            for kd in range(8):
                def frt(e, jt=jt, kd=kd, bank=bank):
                    return e.matmul(
                        psb[bank][:, 0:E],
                        arena[:, o_xn + kd * TW + HALO + jt * 128:
                              o_xn + kd * TW + HALO + (jt + 1) * 128],
                        t_wg[:, kd * E:(kd + 1) * E],
                        start=(kd == 0), stop=(kd == 7))
                mms.append(frt)
            def frtev(e):
                return e.tensor_copy(t_small[:, 8:16], psb[5][:, 0:E])
            ev, mmt = mm_group(bank, mms, frtev, "vector",
                               extra_mm_waits=evs + [all_ld],
                               extra_evac_waits=[rt_prev_ev], evac_chain=True)
            rpe = mmt
            if _DEBUG and k == 0 and jt == 0:
                if "dbg_lg" not in _dbg_t:
                    _dbg_t["dbg_lg"] = dd("dbg_lg", [128, 64], "ExternalOutput")
                S.dma("sync", lambda e: e.dma_start(
                    out=_dbg_t["dbg_lg"][:, 0:8], in_=t_small[:, 8:16]),
                    g["s_dbg"], waits=[ev])
            S.step("vector", lambda e: e.reduce_max(
                t_small[:, 16:17], t_small[:, 8:16], axis=AX.X), drain=True)
            t1 = S.step("vector", lambda e: e.tensor_scalar_mul(
                t_small[:, 17:18], t_small[:, 16:17], -1.0), inc=g["s_norm"],
                drain=True)
            t2 = S.step("scalar", lambda e: e.activation(
                t_small[:, 24:32], t_small[:, 8:16], AF.Exp,
                bias=t_small[:, 17:18]), waits=[t1], inc=g["s_stat"])
            t3 = S.step("vector", lambda e: e.reduce_sum(
                t_small[:, 18:19], t_small[:, 24:32], axis=AX.X), waits=[t2],
                drain=True)
            S.step("vector", lambda e: e.reciprocal(
                t_small[:, 19:20], t_small[:, 18:19]), drain=True)
            S.step("vector", lambda e, m=m, k=k: e.scalar_tensor_tensor(
                t_gate[:, m:m + 1], t_small[:, 19:20], t_qsel[:, k:k + 1],
                t_gate[:, m:m + 1], OP.mult, OP.add), drain=True)
            S.step("vector", lambda e: e.tensor_scalar(
                t_small[:, 32:40], t_small[:, 8:16], t_small[:, 16:17], None,
                OP.is_equal), drain=True)
            S.step("vector", lambda e: e.tensor_tensor_scan(
                t_small[:, 40:48], t_ones[:, 0:8], t_small[:, 32:40], 0.0,
                OP.mult, OP.add), drain=True)
            S.step("vector", lambda e: e.tensor_scalar(
                t_small[:, 56:64], t_small[:, 40:48], 1.0, None, OP.is_equal),
                drain=True)
            S.step("vector", lambda e: e.tensor_tensor(
                t_small[:, 48:56], t_small[:, 32:40], t_small[:, 56:64],
                OP.mult), drain=True)
            rt_prev_ev = S.step("vector", lambda e, m=m, k=k: e.scalar_tensor_tensor(
                t_mask[:, m * E:(m + 1) * E], t_small[:, 48:56],
                t_qsel[:, k:k + 1], t_mask[:, m * E:(m + 1) * E],
                OP.mult, OP.add), inc=g["s_norm"], drain=True)
        chunk_pe_done[k] = rpe

    mlast = S.tok(g["s_norm"])
    mt_prev = None
    for m in range(16):
        wv = [mlast]
        if bank_free[6] is not None:
            wv.append(bank_free[6])
        pt = S.step("tensor", lambda e, m=m: e.transpose(
            psb[6][0:E, 0:128], t_mask[:, m * E:(m + 1) * E], t_eye[:]),
            waits=wv, inc=g["s_tp"])
        mt_prev = S.step("vector", lambda e, m=m: e.tensor_copy(
            t_maskT[:, m * 128:(m + 1) * 128], psb[6][0:E, 0:128]),
            waits=[pt], inc=g["s_ev"], drain=True)
        bank_free[6] = mt_prev
    if _DEBUG:
        dbg_mask = dd("dbg_mask", [128, 16 * E], "ExternalOutput")
        dbg_maskT = dd("dbg_maskT", [8, QT], "ExternalOutput")
        dbg_gate = dd("dbg_gate", [128, 16], "ExternalOutput")
        S.dma("sync", lambda e: e.dma_start(out=dbg_mask, in_=t_mask[:]),
              g["s_dbg"], waits=[mlast, mt_prev])
        S.dma("sync", lambda e: e.dma_start(out=dbg_maskT, in_=t_maskT[:, :]),
              g["s_dbg"], waits=[mt_prev])
        S.dma("sync", lambda e: e.dma_start(out=dbg_gate, in_=t_gate[:]),
              g["s_dbg"], waits=[mlast])
    S.step("vector", lambda e: e.tensor_tensor_scan(
        t_loc[:, :], t_maskT[:, :], t_maskT[:, :], 0.0, OP.add, OP.bypass),
        waits=[mt_prev], drain=True)
    ttot = S.step("vector", lambda e: e.tensor_copy(
        t_tot[:, 0:1], t_loc[:, QT - 1:QT]), inc=g["s_rtr"], drain=True)
    S.dma("sync", lambda e: e.dma_start(out=loc_dram, in_=t_loc[:, :]),
          g["s_sprt"], waits=[ttot])
    S.dma("sync", lambda e: e.dma_start(
        out=tot_in.rearrange("one e -> e one"), in_=t_tot[:, 0:1],
        allow_slow_non_contiguous=True), g["s_sprt"], waits=[ttot])
    ld1 = ld2 = S.tok(g["s_sprt"])

    if _PHASES < 2:
        return _finish(nc, S, ctx)

    grp_quad = [[0, 1, 2, 3], [4, 5, 6, 7]]
    grp_all = [[0, 1, 2, 3, 4, 5, 6, 7]]
    xp_ready = [xp_sp[(k, i)] for k in range(NK1) for i in range(2)]
    ar1 = S.step("gpsimd", lambda e: e.collective_compute(
        "AllReduce", OP.add, ins=[xp_in], outs=[xp_out],
        replica_groups=grp_quad), waits=xp_ready, inc=g["s_cc"])
    if _DEBUG:
        dbg_xpo = dd("dbg_xpo", [144, T], "ExternalOutput")
        S.dma("sync", lambda e: e.dma_start(out=dbg_xpo, in_=xp_out),
              g["s_dbg"], waits=[ar1])
    ag1 = S.step("gpsimd", lambda e: e.collective_compute(
        "AllGather", OP.bypass, ins=[tot_in], outs=[tot_out],
        replica_groups=grp_all), waits=[ld2], inc=g["s_cc"])

    stage_toks = []
    prev_sp = []
    prev_e2 = None
    for kk in range(4):
        c0 = kk * 2048
        wv = [ar1] + prev_sp
        l1 = S.dma("sync", lambda e, c0=c0: e.dma_start(
            out=arena[0:16, o_xpl:o_xpl + 2048], in_=xp_out[0:16, c0:c0 + 2048]),
            g["s_p2k"], waits=wv)
        selev = []
        for (ni, noff) in enumerate(range(0, 2048, 512)):
            def fsel(e, noff=noff):
                return e.matmul(
                    psb[7][0:4, 0:512], t_sel16[:, :],
                    arena[0:16, o_xpl + noff:o_xpl + noff + 512],
                    start=True, stop=True)
            def fselev(e, noff=noff):
                return e.tensor_copy(
                    arena[0:4, o_dto + noff:o_dto + noff + 512],
                    psb[7][0:4, 0:512])
            ev, _ = mm_group(7, [fsel], fselev, "vector",
                             extra_mm_waits=[l1, all_ld],
                             extra_evac_waits=[prev_e2])
            selev.append(ev)
        e1 = S.step("scalar", lambda e: e.activation(
            arena[0:4, o_dto:o_dto + 2048], arena[0:4, o_dto:o_dto + 2048],
            AF.Exp, bias=t_dtb[:, 0:1]), waits=selev, inc=g["s_stat"])
        prev_e2 = S.step("scalar", lambda e: e.activation(
            arena[0:4, o_dto:o_dto + 2048], arena[0:4, o_dto:o_dto + 2048],
            AF.Ln, bias=1.0), inc=g["s_stat"])
        d1 = S.dma("sync", lambda e, c0=c0: e.dma_start(
            out=dt_dram[:, c0:c0 + 2048], in_=arena[0:4, o_dto:o_dto + 2048]),
            g["s_spd1"], waits=[prev_e2])
        prev_sp = [d1]
        stage_toks += [d1]
    stage_toks.append(ar1)

    dA_toks, scan_toks, w_toks, mm_toks = {}, {}, {}, {}
    dump_toks = {}
    x4r, dtbr, dtxr = {}, {}, {}
    yev_toks, ysp_toks = {}, {}

    def itix(k, nb, n, h):
        return ((k * NNB + nb) * NBLK + n) * 4 + h

    I_LAST = NK2 * NNB * NBLK * 4 - 1

    x4_all = S.tok(g["s_spx4"])
    z_all0 = S.tok(g["s_spz"])
    x_all0 = [S.tok(g["s_x0"]), S.tok(g["s_x1"])]

    NIT = NNB * NBLK * 4
    bc_tok = {}
    b_tok = {}
    wpair = {}
    u_tok, cy_tok = {}, {}

    def gi_parts(gi):
        k, r = divmod(gi, NIT)
        nb, r2 = divmod(r, 4 * NBLK)
        h, n = divmod(r2, NBLK)
        return k, nb, n, h

    def emit_bc(k, nb):
        par = nb % 2
        gnb = k * NNB + nb
        if gnb < 2:
            wv = list(stage_toks) + [z_all0, x4_all] + x_all0
        else:
            pgi = (gnb - 2) * 4 * NBLK + 4 * NBLK - 1
            wv = [u_tok[pgi], wpair[pgi]]
        b_tok[gnb] = S.dma("sync", lambda e, k=k, nb=nb, par=par: e.dma_start(
            out=A(o_bb + par * NBLK * TC2, NBLK * TC2).rearrange(
                "p (n t) -> p n t", n=NBLK),
            in_=xp_out[16 + nb * NBLK:16 + (nb + 1) * NBLK,
                       k * TC2:(k + 1) * TC2][None, :, :]
            .broadcast_to([128, NBLK, TC2])),
            g["s_bc0" if par == 0 else "s_bc1"], waits=wv)
        bc_tok[gnb] = S.dma("gpsimd", lambda e, k=k, nb=nb, par=par: e.dma_start(
            out=A(o_cb + par * NBLK * TC2, NBLK * TC2).rearrange(
                "p (n t) -> p n t", n=NBLK),
            in_=xp_out[80 + nb * NBLK:80 + (nb + 1) * NBLK,
                       k * TC2:(k + 1) * TC2][None, :, :]
            .broadcast_to([128, NBLK, TC2])),
            g["s_bcg0" if par == 0 else "s_bcg1"], waits=wv)

    def emit_u(gi):
        k, nb, n, h = gi_parts(gi)
        par = nb % 2
        u_tok[gi] = S.step("vector", lambda e, h=h, n=n, par=par, gi=gi: e.tensor_tensor(
            A(o_u + (gi % 4) * TC2, TC2), A(o_dtx + h * TC2, TC2),
            A(o_bb + par * NBLK * TC2 + n * TC2, TC2), OP.mult),
            waits=[b_tok[k * NNB + nb], dtxr[(k, 3)],
                   scan_toks.get(gi - 4)], inc=g["s_u"], chain=False)

    pend_carry = []
    for k in range(NK2):
        for h in range(4):
            wvx = list(stage_toks) + [x4_all]
            wvd = list(stage_toks) + [x4_all]
            if k > 0:
                wvx.append(yev_toks[(k - 1, h)])
                wvd.append(dA_toks[k * NIT - 1])
            S.dma("gpsimd", lambda e, k=k, h=h: e.dma_start(
                out=A(o_x4 + h * TC2, TC2),
                in_=x4_hbm[h * 128:(h + 1) * 128, k * TC2:(k + 1) * TC2]),
                g["s_x4g"], waits=wvx)
            S.dma("scalar", lambda e, k=k, h=h: e.dma_start(
                out=A(o_dtb + h * TC2, TC2),
                in_=dt_dram[h:h + 1, k * TC2:(k + 1) * TC2].broadcast_to([128, TC2])),
                g["s_dtbs"], waits=wvd)
        chunk_ld_x = S.tok(g["s_x4g"])
        chunk_ld_d = S.tok(g["s_dtbs"])
        for h in range(4):
            dtxr[(k, h)] = S.step("vector", lambda e, h=h: e.tensor_tensor(
                A(o_dtx + h * TC2, TC2), A(o_dtb + h * TC2, TC2),
                A(o_x4 + h * TC2, TC2), OP.mult),
                waits=[chunk_ld_x, chunk_ld_d], inc=g["s_norm"], chain=False)
            dtbr[(k, h)] = chunk_ld_d
        emit_bc(k, 0)
        emit_bc(k, 1)
        for (pc_h, pc_nn, pc_gi) in pend_carry:
            cy_tok[pc_gi] = S.step("vector", lambda e, pc_h=pc_h, pc_nn=pc_nn, pc_gi=pc_gi: e.tensor_copy(
                t_carry[:, pc_h * DS + pc_nn:pc_h * DS + pc_nn + 1],
                arena[:, o_st2 + (pc_gi % NST) * TC2 + TC2 - 1:
                      o_st2 + (pc_gi % NST) * TC2 + TC2]),
                waits=[scan_toks[pc_gi]], inc=g["s_cy"], chain=False)
        pend_carry = []
        emit_u(k * NIT + 0)
        emit_u(k * NIT + 1)
        emit_u(k * NIT + 2)
        for i in range(NIT):
            gi = k * NIT + i
            _, nb, n, h = gi_parts(gi)
            nn = nb * NBLK + n
            if i % (4 * NBLK) == 0 and nb >= 1 and nb + 1 < NNB:
                emit_bc(k, nb + 1)
            wv = [chunk_ld_d, a_ready]
            if gi >= NDA:
                wv.append(scan_toks.get(gi - NDA))
            dA_toks[gi] = S.step("scalar", lambda e, h=h, nn=nn, gi=gi: e.activation(
                A(o_dA + (gi % NDA) * TC2, TC2), A(o_dtb + h * TC2, TC2),
                AF.Exp, scale=t_a[:, h * DS + nn:h * DS + nn + 1]),
                waits=wv, inc=g["s_dA"], chain=False)
            wv = [dA_toks[gi], u_tok[gi], cy_tok.get(gi - NST)]
            if gi >= NST:
                wv.append(wpair.get((gi - NST) | 1))
            scan_toks[gi] = S.step("vector", lambda e, h=h, nn=nn, gi=gi: e.tensor_tensor_scan(
                A(o_st2 + (gi % NST) * TC2, TC2),
                A(o_dA + (gi % NDA) * TC2, TC2),
                A(o_u + (gi % 4) * TC2, TC2),
                t_carry[:, h * DS + nn:h * DS + nn + 1],
                OP.mult, OP.add), waits=wv, inc=g["s_scan"], chain=False)
            if i + 3 < NIT:
                emit_u(gi + 3)
            if i >= 1:
                _, pnb, pn, ph = gi_parts(gi - 1)
                pnn = pnb * NBLK + pn
                cy_tok[gi - 1] = S.step("vector", lambda e, ph=ph, pnn=pnn, gi=gi: e.tensor_copy(
                    t_carry[:, ph * DS + pnn:ph * DS + pnn + 1],
                    arena[:, o_st2 + ((gi - 1) % NST) * TC2 + TC2 - 1:
                          o_st2 + ((gi - 1) % NST) * TC2 + TC2]),
                    waits=[scan_toks[gi - 1]], inc=g["s_cy"], chain=False)
            if i == NIT - 1:
                pend_carry.append((h, nn, gi))
            if i % 2 == 1:
                pbase = gi - 1
                slot = pbase % NST
                wbuf = (pbase // 2) % NWB
                par = nb % 2
                wv = [scan_toks[gi], bc_tok[k * NNB + nb]]
                if pbase >= 2 * NWB:
                    wv.append(mm_toks.get((pbase - 2 * NWB) | 1))
                wpair[gi] = S.step("vector", lambda e, par=par, slot=slot, wbuf=wbuf: e.tensor_tensor(
                    A(o_w + wbuf * 2 * TC2, 2 * TC2),
                    A(o_st2 + slot * TC2, 2 * TC2),
                    A(o_cb + par * NBLK * TC2, 2 * TC2),
                    OP.mult), waits=wv, inc=g["s_w"], chain=False)
                first = (nb == 0 and n == 1)
                last = (nb == NNB - 1 and n == 1)
                for sub in range(4):
                    bank = 2 * h + (sub % 2)
                    wv = [wpair[gi], all_ld]
                    if first and sub < 2:
                        if k == 0:
                            wv.append(bank_free[bank])
                        else:
                            wv.append(yev_toks[(k - 1, h)])
                    mm_toks[gi] = S.step(
                        "tensor", lambda e, sub=sub, bank=bank, wbuf=wbuf,
                        first=first, last=last: e.matmul(
                            psb[bank][:, 0:512], t_eye[:],
                            arena[:, o_w + wbuf * 2 * TC2 + sub * 512:
                                  o_w + wbuf * 2 * TC2 + sub * 512 + 512],
                            start=(first and sub < 2), stop=(last and sub >= 2)),
                        waits=wv, inc=(g["s_ymm"] if sub == 3 else None))
        for h in range(4):
            jj = k * 4 + h
            tok_i = k * NIT + ((NNB - 1) * 4 + h) * NBLK + 1
            S.step("vector", lambda e, h=h: e.tensor_scalar(
                A(o_dx4, TC2), A(o_x4 + h * TC2, TC2), t_d[:, h:h + 1], None,
                OP.mult), waits=[mm_toks[tok_i]])
            wv = [ysp_toks.get(jj - 2)]
            S.step("vector", lambda e, h=h, jj=jj: e.tensor_tensor(
                A(o_yev + (jj % 2) * TC2, 512), psb[2 * h][:, 0:512],
                A(o_dx4, 512), OP.add), waits=wv)
            y2 = S.step("vector", lambda e, h=h, jj=jj: e.tensor_tensor(
                arena[:, o_yev + (jj % 2) * TC2 + 512:o_yev + (jj % 2) * TC2 + 1024],
                psb[2 * h + 1][:, 0:512],
                arena[:, o_dx4 + 512:o_dx4 + 1024], OP.add), inc=g["s_yev"])
            yev_toks[(k, h)] = y2
            ysp_toks[jj] = S.dma("sync", lambda e, h=h, k=k, jj=jj: e.dma_start(
                out=y_hbm[h * 128:(h + 1) * 128, k * TC2:(k + 1) * TC2],
                in_=A(o_yev + (jj % 2) * TC2, TC2)),
                g["s_spy" if jj % 2 == 0 else "s_spy1"], waits=[y2])

    if _PHASES < 3:
        return _finish(nc, S, ctx)

    p2_drain = [scan_toks[I_LAST], wpair[I_LAST]] + \
        [yev_toks[(NK2 - 1, hh)] for hh in range(4)]
    y_all_sp = S.tok(g["s_spy"])
    y_all_sp1 = S.tok(g["s_spy1"])

    st_ev = {}
    sp_done = {}
    for k in range(NK2):
        pay = o3_y if k % 2 == 0 else o3_y2
        pag = o3_g if k % 2 == 0 else o3_z
        spay = g["s_pay" if k % 2 == 0 else "s_pay1"]
        yld = {}
        for h in range(4):
            wv = [y_all_sp, y_all_sp1]
            if k < 2:
                wv += p2_drain
            else:
                wv.append(st_ev[(k - 2, 1, 1)])
            yld[h] = S.dma("sync", lambda e, k=k, h=h, pay=pay: e.dma_start(
                out=A(pay + h * TC2, TC2),
                in_=y_hbm[h * 128:(h + 1) * 128, k * TC2:(k + 1) * TC2]),
                spay, waits=wv)
        yall = S.tok(spay)
        yld = {h: yall for h in range(4)}
        sq = {}
        for h in range(4):
            sqw = [yld[h]]
            if k < 2:
                sqw += p2_drain
            else:
                sqw.append(st_ev[(k - 2, 1, 1)])
            sq[h] = S.step("scalar", lambda e, h=h, pay=pay, pag=pag: e.activation(
                A(pag + h * TC2, TC2), A(pay + h * TC2, TC2), AF.Square),
                waits=sqw, inc=g["s_stat"], chain=False)
        for (si, src_off) in ((0, pay), (1, pag)):
            for half in range(2):
                bank = 4 + si * 2 + half
                mms = []
                for h in range(4):
                    def fst(e, h=h, src_off=src_off, half=half, bank=bank):
                        return e.matmul(
                            psb[bank][0:1, 0:512], t_ones[:, 0:1],
                            A(src_off + h * TC2 + half * 512, 512),
                            start=(h == 0), stop=(h == 3))
                    mms.append(fst)
                def fstev(e, si=si, half=half, k=k, bank=bank):
                    off = o3_sr + si * 4096 + (k % 4) * 1024 + half * 512
                    return e.tensor_copy(arena[0:1, off:off + 512],
                                         psb[bank][0:1, 0:512])
                exw = [sp_done[(k - 4, si)]] if k >= 4 else []
                st_ev[(k, si, half)], _ = mm_group(
                    bank, mms, fstev, "vector",
                    extra_mm_waits=[sq[h2] for h2 in range(4)] + list(yld.values()),
                    extra_evac_waits=exw)
        if k % 4 == 3:
            deps = [st_ev[(kk2, si, hh)] for kk2 in range(k - 3, k + 1)
                    for si in range(2) for hh in range(2)]
            for si in range(2):
                S.dma("sync", lambda e, k=k, si=si: e.dma_start(
                    out=sta_in[si:si + 1, (k - 3) * TC2:(k + 1) * TC2],
                    in_=arena[0:1, o3_sr + si * 4096:o3_sr + si * 4096 + 4096]),
                    g["s_spst"], waits=deps)
            tpair = S.tok(g["s_spst"])
            for kk2 in range(k - 3, k + 1):
                for si in range(2):
                    sp_done[(kk2, si)] = tpair

    ar2 = S.step("gpsimd", lambda e: e.collective_compute(
        "AllReduce", OP.add, ins=[sta_in], outs=[sta_out],
        replica_groups=grp_quad),
        waits=[sp_done[(NK2 - 1, 0)], sp_done[(NK2 - 1, 1)]], inc=g["s_cc"])
    m1 = m2 = None
    for c4 in range(4):
        c0 = c4 * 2048
        oA, oB, oC, oD, oE = (o3_sr, o3_sr + 2048, o3_sr + 4096,
                              o3_sr + 6144, o3_sr + 8192)
        la = S.dma("sync", lambda e, c0=c0, oA=oA: e.dma_start(
            out=arena[0:1, oA:oA + 2048], in_=sta_out[0:1, c0:c0 + 2048]),
            g["s_pam"], waits=[ar2, st_ev[(NK2 - 1, 1, 1)], m1, m2])
        lb = S.dma("sync", lambda e, c0=c0, oB=oB: e.dma_start(
            out=arena[0:1, oB:oB + 2048], in_=sta_out[1:2, c0:c0 + 2048]),
            g["s_pam"], waits=[ar2, m1, m2])
        la = S.tok(g["s_pam"])
        lb = la
        S.step("vector", lambda e, oA=oA, oD=oD: e.tensor_scalar_mul(
            arena[0:1, oD:oD + 2048], arena[0:1, oA:oA + 2048], 1.0 / DI),
            waits=[la])
        S.step("vector", lambda e, oB=oB, oC=oC: e.tensor_scalar_mul(
            arena[0:1, oC:oC + 2048], arena[0:1, oB:oB + 2048], 1.0 / DI),
            waits=[lb], drain=True)
        S.step("vector", lambda e, oA=oA, oD=oD: e.tensor_tensor(
            arena[0:1, oA:oA + 2048], arena[0:1, oD:oD + 2048],
            arena[0:1, oD:oD + 2048], OP.mult), drain=True)
        v1 = S.step("vector", lambda e, oA=oA, oB=oB, oC=oC: e.tensor_tensor(
            arena[0:1, oB:oB + 2048], arena[0:1, oC:oC + 2048],
            arena[0:1, oA:oA + 2048], OP.subtract), inc=g["s_norm"],
            drain=True)
        v2 = S.step("scalar", lambda e, oB=oB, oC=oC: e.activation(
            arena[0:1, oC:oC + 2048], arena[0:1, oB:oB + 2048], AF.Sqrt,
            bias=t_eps[0:1, 0:1]), waits=[v1], inc=g["s_stat"])
        v3 = S.step("vector", lambda e, oC=oC, oE=oE: e.reciprocal(
            arena[0:1, oE:oE + 2048], arena[0:1, oC:oC + 2048]),
            waits=[v2], inc=g["s_norm"], drain=True)
        m1 = S.dma("sync", lambda e, c0=c0, oD=oD: e.dma_start(
            out=mu_dram[0:1, c0:c0 + 2048], in_=arena[0:1, oD:oD + 2048]),
            g["s_spmu"], waits=[v3])
        m2 = S.dma("sync", lambda e, c0=c0, oE=oE: e.dma_start(
            out=rs_dram[0:1, c0:c0 + 2048], in_=arena[0:1, oE:oE + 2048]),
            g["s_spmu"], waits=[v3])

    wol = S.dma("sync", lambda e: e.dma_start(
        out=A(o3_wo, 4 * 1024).rearrange("p (g c) -> p g c", g=4),
        in_=w_outT.rearrange("(g p) c -> p g c", p=128)), g["s_p4w"])
    z_all = S.tok(g["s_spz"])
    mu_all = S.tok(g["s_spmu"])
    sev = {}
    s_sp_tok = {}
    pb_done = {}
    gt_hist = {}
    wel_a = wel_b = None
    for k in range(NK2):
        if k == 2:
            wel_a = S.dma("gpsimd", lambda e: e.dma_start(
                out=A(o4_we, 4 * 1024).rearrange("p (k c) -> p k c", k=4),
                in_=we[0:512, :].rearrange("(k p) c -> p k c", p=128)),
                g["s_wek"], waits=[m1, m2])
            wel_b = S.dma("gpsimd", lambda e: e.dma_start(
                out=A(o4_we + 4 * 1024, 4 * 1024).rearrange("p (k c) -> p k c", k=4),
                in_=we[512:1024, :].rearrange("(k p) c -> p k c", p=128)),
                g["s_wek1"], waits=[m1, m2])
        oy = o3_y if k % 2 == 0 else o3_y2
        oz = o3_z if k % 2 == 0 else o3_z2
        om = o3_mu if k % 2 == 0 else o3_mu2
        ors = o3_rs if k % 2 == 0 else o3_rs2
        sm = g["s_pbm" if k % 2 == 0 else "s_pbm1"]
        sy = g["s_pby" if k % 2 == 0 else "s_pby1"]
        sz = g["s_pbz" if k % 2 == 0 else "s_pbz1"]
        mw = [mu_all, gt_hist.get(k - 2)]
        mub = S.dma("sync", lambda e, k=k, om=om: e.dma_start(
            out=A(om, TC2),
            in_=mu_dram[0:1, k * TC2:(k + 1) * TC2].broadcast_to([128, TC2])),
            sm, waits=mw)
        S.dma("sync", lambda e, k=k, ors=ors: e.dma_start(
            out=A(ors, TC2),
            in_=rs_dram[0:1, k * TC2:(k + 1) * TC2].broadcast_to([128, TC2])),
            sm, waits=mw)
        mub = S.tok(sm)
        rsb = mub
        gtok = {}
        for h in range(4):
            S.dma("gpsimd", lambda e, k=k, h=h, oy=oy: e.dma_start(
                out=A(oy + h * TC2, TC2),
                in_=y_hbm[h * 128:(h + 1) * 128, k * TC2:(k + 1) * TC2]),
                sy, waits=[y_all_sp, y_all_sp1, pb_done.get(k - 2)])
            S.dma("scalar", lambda e, k=k, h=h, oz=oz: e.dma_start(
                out=A(oz + h * TC2, TC2),
                in_=z_hbm[h * 128:(h + 1) * 128, k * TC2:(k + 1) * TC2]),
                sz, waits=[z_all, gt_hist.get(k - 2)])
        pby_tok = S.tok(sy)
        pbz_tok = S.tok(sz)
        for h in range(4):
            gw = [pby_tok, mub, pb_done.get(k - 2)]
            S.step("vector", lambda e, h=h, oy=oy, om=om: e.tensor_tensor(
                A(o3_g + h * TC2, TC2), A(oy + h * TC2, TC2), A(om, TC2),
                OP.subtract), waits=gw)
            S.step("vector", lambda e, h=h, oy=oy, ors=ors: e.tensor_tensor(
                A(oy + h * TC2, TC2), A(o3_g + h * TC2, TC2), A(ors, TC2),
                OP.mult), waits=[rsb])
            S.step("vector", lambda e, h=h, oy=oy: e.tensor_scalar(
                A(o3_g + h * TC2, TC2), A(oy + h * TC2, TC2),
                t_gb[:, h:h + 1], t_gb[:, HPC + h:HPC + h + 1],
                OP.mult, OP.add), inc=g["s_norm"])
            sg = S.step("scalar", lambda e, h=h, oz=oz: e.activation(
                A(oz + h * TC2, TC2), A(oz + h * TC2, TC2), AF.Sigmoid),
                waits=[pbz_tok], inc=g["s_stat"], chain=False)
            gtok[h] = S.step("vector", lambda e, h=h, oy=oy, oz=oz: e.tensor_tensor(
                A(oy + h * TC2, TC2), A(o3_g + h * TC2, TC2),
                A(oz + h * TC2, TC2), OP.mult), waits=[sg], inc=g["s_norm"])
        gt_hist[k] = gtok[3]
        for m in range(TC2 // 128):
            for half in range(2):
                bank = half + 2 * (m % 2)
                mms = []
                for cg in range(4):
                    def fop(e, m=m, cg=cg, half=half, bank=bank, oy=oy):
                        return e.matmul(
                            psb[bank][:, 0:512],
                            arena[:, oy + cg * TC2 + m * 128:
                                  oy + cg * TC2 + (m + 1) * 128],
                            A(o3_wo + cg * 1024 + half * 512, 512),
                            start=(cg == 0), stop=(cg == 3))
                    mms.append(fop)
                def fopev(e, m=m, half=half, bank=bank):
                    return e.tensor_copy(
                        A(o3_s + (m % 2) * 1024 + half * 512, 512),
                        psb[bank][:, 0:512])
                ms = k * (TC2 // 128) + m
                sev[(k, m, half)], _ = mm_group(
                    bank, mms, fopev, "vector",
                    extra_mm_waits=[gtok[h2] for h2 in range(4)] + [wol],
                    extra_evac_waits=[s_sp_tok.get(ms - 2)])
            s_sp_tok[ms] = S.dma("sync", lambda e, k=k, m=m: e.dma_start(
                out=s_in[k * TC2 + m * 128:k * TC2 + (m + 1) * 128, :],
                in_=A(o3_s + (m % 2) * 1024, 1024)),
                g["s_sps0" if m % 2 == 0 else "s_sps1"],
                waits=[sev[(k, m, 0)], sev[(k, m, 1)]])
        pb_done[k] = sev[(k, TC2 // 128 - 1, 1)]

    p3_last_ev = sev[(NK2 - 1, TC2 // 128 - 1, 1)]
    rs1 = S.step("gpsimd", lambda e: e.collective_compute(
        "ReduceScatter", OP.add, ins=[s_in], outs=[s_out],
        replica_groups=grp_quad),
        waits=[S.tok(g["s_sps0"]), S.tok(g["s_sps1"])], inc=g["s_cc"])
    if _DEBUG:
        dbg_so = dd("dbg_so", [QT, D], "ExternalOutput")
        S.dma("sync", lambda e: e.dma_start(out=dbg_so, in_=s_out),
              g["s_dbg"], waits=[rs1])
        dbg_to = dd("dbg_to", [8, E], "ExternalOutput")
        S.dma("sync", lambda e: e.dma_start(out=dbg_to, in_=tot_out),
              g["s_dbg"], waits=[ag1])

    if _PHASES < 4:
        return _finish(nc, S, ctx)

    sps0_tok = S.tok(g["s_sps0"])
    sps1_tok = S.tok(g["s_sps1"])
    bel = S.dma("sync", lambda e: e.dma_start(
        out=A(o4_be, 1024), in_=be.broadcast_to([128, 1024])), g["s_p4b"],
        waits=[p3_last_ev])
    prev_tn4 = None
    tn4, lpe4, xn_lev = {}, {}, {}
    for j in range(16):
        xb = o4_xq + (j % 4) * 1024
        w = [tn4.get(j - 4)] if j >= 4 else [p3_last_ev]
        tl = S.dma("sync", lambda e, xb=xb, j=j: e.dma_start(
            out=A(xb, 1024), in_=x_q[j * 128:(j + 1) * 128, :]),
            g[("s_xq0", "s_xq1", "s_xq2", "s_xq3")[j % 4]], waits=w)
        fpw = p3_last_ev if j == 0 else None
        tn, lpe, lev = ln_tile(tl, xb, o4_scr, o4_scr + 1024,
                               o4_g2 + (j % 2) * 1024,
                               o4_xnT, QT, j * 128, (6, 7),
                               prev_tn4, first_pe_wait=fpw,
                               xnd_wait=lpe4.get(j - 2))
        prev_tn4 = tn
        tn4[j], lpe4[j], xn_lev[j] = tn, lpe, lev

    ttl = S.dma("sync", lambda e: e.dma_start(out=t_totL[:], in_=tot_out),
                g["s_p4t"], waits=[ag1])
    def fb(e):
        return e.matmul(psb[5][0:1, 0:E], t_step8[:], t_totL[:],
                        start=True, stop=True)
    def fbev(e):
        inst = e.tensor_copy(t_small[0:1, 36:44], psb[5][0:1, 0:E])
        return inst
    bev, _ = mm_group(5, [fb], fbev, "vector", extra_mm_waits=[ttl, all_ld])
    bd = S.dma("sync", lambda e: e.dma_start(
        out=base_dram, in_=t_small[0:1, 36:44]), g["s_p4bd"], waits=[bev])
    bb2 = S.dma("sync", lambda e: e.dma_start(
        out=A(o4_bs, 8), in_=base_dram.broadcast_to([128, 8])), g["s_p4bs"],
        waits=[bd])
    ltl = None
    for m in range(16):
        wv = [ld1]
        if bank_free[5] is not None:
            wv.append(bank_free[5])
        pt = S.step("tensor", lambda e, m=m: e.transpose(
            psb[5][:, 0:E], t_loc[:, m * 128:(m + 1) * 128], t_eye[0:E, 0:E]),
            waits=wv, inc=g["s_tp"])
        ltl = S.step("vector", lambda e, m=m: e.tensor_copy(
            arena[:, o4_lt + m * E:o4_lt + (m + 1) * E], psb[5][:, 0:E]),
            waits=[pt], inc=g["s_ev"], drain=True)
        bank_free[5] = ltl
    sc_tok = {}
    for m in range(16):
        S.step("vector", lambda e, m=m: e.tensor_tensor(
            t_small[:, 24:32], arena[:, o4_lt + m * 8:o4_lt + (m + 1) * 8],
            A(o4_bs, 8), OP.add), waits=[bb2, ltl], drain=True)
        S.step("vector", lambda e: e.tensor_scalar(
            t_small[:, 44:52], t_small[:, 24:32], float(CAP), None, OP.is_le),
            drain=True)
        S.step("vector", lambda e, m=m: e.tensor_tensor(
            t_small[:, 24:32], t_small[:, 44:52],
            t_mask[:, m * E:(m + 1) * E], OP.mult), drain=True)
        S.step("vector", lambda e: e.reduce_sum(
            t_small[:, 32:33], t_small[:, 24:32], axis=AX.X), drain=True)
        sc_tok[m] = S.step("vector", lambda e, m=m: e.tensor_tensor(
            t_scale[:, m:m + 1], t_small[:, 32:33], t_gate[:, m:m + 1],
            OP.mult), inc=g["s_norm"], drain=True)

    out_hist = {}
    lpe_last = lpe4[15]
    for m in range(16):
        par = m % 2
        par4 = m % 4
        evs4 = []
        for half in range(2):
            bank = half + 2 * (m % 2)
            mm_tok = None
            for kd in range(8):
                w = [wel_a if kd < 4 else wel_b]
                if kd == 0:
                    w.append(xn_lev[m])
                    if bank_free[bank] is not None:
                        w.append(bank_free[bank])
                mm_tok = S.step(
                    "tensor", lambda e, m=m, kd=kd, half=half, bank=bank: e.matmul(
                        psb[bank][:, 0:512],
                        arena[:, o4_xnT + kd * QT + m * 128:
                              o4_xnT + kd * QT + (m + 1) * 128],
                        A(o4_we + kd * 1024 + half * 512, 512),
                        start=(kd == 0), stop=(kd == 7)),
                    waits=w, inc=(g["s_mm"] if kd == 7 else None))
            def fev(e, par4=par4, half=half, bank=bank):
                return e.tensor_copy(
                    A(o4_eo + par4 * 1024 + half * 512, 512), psb[bank][:, 0:512])
            ev = S.step("vector", fev,
                        waits=[mm_tok, out_hist.get(m - 4),
                               sps0_tok, sps1_tok], inc=g["s_ev"])
            bank_free[bank] = ev
            evs4.append(ev)
        t1w = [bel] + evs4 + ([lpe_last] if m < 2 else [])
        t1 = S.step("vector", lambda e, par=par, par4=par4: e.tensor_tensor(
            A(o4_g2 + par * 1024, 1024), A(o4_eo + par4 * 1024, 1024),
            A(o4_be, 1024), OP.add), waits=t1w, inc=g["s_norm"])
        t2 = S.step("scalar", lambda e, m=m, par=par, par4=par4: e.activation(
            A(o4_eo + par4 * 1024, 1024), A(o4_g2 + par * 1024, 1024), AF.Silu,
            scale=t_scale[:, m:m + 1]), waits=[t1, sc_tok[m]], inc=g["s_stat"],
            chain=False)
        sl = S.dma("gpsimd", lambda e, m=m, par=par: e.dma_start(
            out=A(o4_s + par * 1024, 1024), in_=s_out[m * 128:(m + 1) * 128, :]),
            g["s_p4s" if par == 0 else "s_p4s1"], waits=[rs1, out_hist.get(m - 2)])
        xlw = [out_hist.get(m - 2)] + ([p3_last_ev] if m < 2 else [])
        xl = S.dma("scalar", lambda e, m=m, par=par: e.dma_start(
            out=A(o4_x + par * 1024, 1024), in_=x_q[m * 128:(m + 1) * 128, :]),
            g["s_p4x" if par == 0 else "s_p4x1"], waits=xlw)
        t3 = S.step("vector", lambda e, par=par, par4=par4: e.tensor_tensor(
            A(o4_g2 + par * 1024, 1024), A(o4_eo + par4 * 1024, 1024),
            A(o4_s + par * 1024, 1024), OP.mult), waits=[t2, sl], inc=g["s_norm"])
        t4 = S.step("vector", lambda e, par=par, par4=par4: e.tensor_tensor(
            A(o4_eo + par4 * 1024, 1024), A(o4_g2 + par * 1024, 1024),
            A(o4_x + par * 1024, 1024), OP.add), waits=[xl], inc=g["s_norm"])
        out_hist[m] = S.dma("sync", lambda e, m=m, par4=par4: e.dma_start(
            out=out_c[m * 128:(m + 1) * 128, :], in_=A(o4_eo + par4 * 1024, 1024)),
            g["s_out" if par == 0 else "s_out1"], waits=[t4])

    S.step("vector", lambda e: e.engine_nop(),
           waits=[out_hist[14], out_hist[15]])
    return _finish(nc, S, ctx)


def _finish(nc, S, ctx):
    with nc.Block() as block:
        @block.sync
        def _(eng):
            S.replay("sync", eng)

        @block.scalar
        def _(eng):
            S.replay("scalar", eng)

        @block.vector
        def _(eng):
            S.replay("vector", eng)

        @block.tensor
        def _(eng):
            S.replay("tensor", eng)

        @block.gpsimd
        def _(eng):
            S.replay("gpsimd", eng)

    for cm in reversed(ctx):
        cm.__exit__(None, None, None)
    return nc


def host_inputs(core, inputs):
    b, q = core // 4, core % 4
    ch0 = CPC * q
    W_in = np.asarray(inputs["L0_attn_W_in"], np.float32)
    m = {}
    xb = np.ascontiguousarray(np.asarray(
        inputs["L0_attn_hidden_states"], np.float32)[b])
    m["x_b"] = xb
    m["x_q"] = np.ascontiguousarray(xb[q * QT:(q + 1) * QT])
    wx = W_in[ch0:ch0 + CPC]
    wz = W_in[DI + ch0:DI + ch0 + CPC]
    m["w_inT"] = np.ascontiguousarray(np.concatenate([wx, wz], 0).T)
    m["w_dtT"] = np.ascontiguousarray(
        np.asarray(inputs["L0_attn_W_dt"], np.float32)[:, ch0:ch0 + CPC].T)
    m["w_conv"] = np.ascontiguousarray(
        np.asarray(inputs["L0_attn_W_conv"], np.float32).reshape(DI, 4)[ch0:ch0 + CPC]
        .reshape(4, 128, 4).transpose(1, 0, 2).reshape(128, 16))
    m["b_conv"] = np.ascontiguousarray(
        np.asarray(inputs["L0_attn_b_conv"], np.float32)[ch0:ch0 + CPC].reshape(4, 128).T)
    al = np.asarray(inputs["L0_attn_A_log"], np.float32).reshape(HN, HD, DS)[4 * q:4 * q + 4]
    m["a_log"] = np.ascontiguousarray(al.transpose(1, 0, 2).reshape(HD, HPC * DS))
    m["d_col"] = np.ascontiguousarray(
        np.asarray(inputs["L0_attn_D"], np.float32).reshape(HN, HD)[4 * q:4 * q + 4].T)
    m["dtbias"] = np.ascontiguousarray(
        np.asarray(inputs["L0_attn_dt_bias"], np.float32)[4 * q:4 * q + 4].reshape(HPC, 1))
    bdt = np.asarray(inputs["L0_attn_b_dt"], np.float32).reshape(144, 1)
    m["b_dt"] = np.ascontiguousarray(bdt if q == 0 else np.zeros_like(bdt))
    m["gamma"] = np.ascontiguousarray(
        np.asarray(inputs["L0_attn_ln_gamma"], np.float32).reshape(HN, HD)[4 * q:4 * q + 4].T)
    m["beta"] = np.ascontiguousarray(
        np.asarray(inputs["L0_attn_ln_beta"], np.float32).reshape(HN, HD)[4 * q:4 * q + 4].T)
    m["w_outT"] = np.ascontiguousarray(
        np.asarray(inputs["L0_attn_W_out"], np.float32)[:, ch0:ch0 + CPC].T)
    m["we"] = np.ascontiguousarray(np.asarray(inputs["L0_ffn_We"], np.float32))
    m["be"] = np.ascontiguousarray(
        np.asarray(inputs["L0_ffn_be"], np.float32).reshape(1, D))
    m["wg"] = np.ascontiguousarray(np.asarray(inputs["L0_ffn_Wg"], np.float32))
    m["eye"] = np.eye(128, dtype=np.float32)
    qs = np.zeros((128, NK1), np.float32)
    qs[:, 2 * q] = 1.0
    qs[:, 2 * q + 1] = 1.0
    m["qsel"] = qs
    s16 = np.zeros((16, 4), np.float32)
    for hh in range(4):
        s16[4 * q + hh, hh] = 1.0
    m["sel16"] = s16
    st8 = np.zeros((8, 1), np.float32)
    st8[0:core, 0] = 1.0
    m["step8"] = st8
    return m


def kernel(**inputs):
    nc = build()
    in_maps = [host_inputs(c, inputs) for c in range(8)]
    res = run_bass_kernel_spmd(nc, in_maps, core_ids=list(range(8)))
    out = np.empty((B, T, D), np.float32)
    for c in range(8):
        b, q = c // 4, c % 4
        out[b, q * QT:(q + 1) * QT] = np.asarray(res.results[c]["out_c"])
    return out


# version 14
# speedup vs baseline: 1.0013x; 1.0013x over previous
import numpy as np
import concourse.bass as bass
import concourse.mybir as mybir
from concourse.bass_utils import run_bass_kernel_spmd

F32 = mybir.dt.float32
AF = mybir.ActivationFunctionType
OP = mybir.AluOpType
AX = mybir.AxisListType

B, T, D = 2, 8192, 1024
DI, HN, HD, DS = 2048, 16, 128, 64
HPC, CPC = 4, 512
QT, CAP, E = 2048, 2048, 8
TC1, HALO = 1024, 128
NK1 = T // TC1
NT1 = (TC1 + HALO) // 128
TW = TC1 + HALO
TC2 = 1024
NK2 = T // TC2
NBLK = 2
NNB = DS // NBLK
NST = 4
NDA = 3
NWB = 3
EPS = 1e-5

_DEBUG = False
_PHASES = 4


class Sched:
    def __init__(self):
        self.steps = []
        self.cnt = {}
        self.own = {}
        self.order_sems = {}
        self.live_cnt = {}

    def tok(self, sem):
        k = id(sem)
        return (sem, self.cnt.get(k, 0), self.own.get(k))

    def step(self, engine, fn, waits=(), inc=None, amt=1, dma=False, drain=False,
             chain=True):
        w = []
        for t in waits:
            if t is None:
                continue
            sem, val, owner = t
            if val <= 0:
                continue
            w.append((sem, val, owner))
        tokv = None
        post = None
        if inc is not None:
            k = id(inc)
            self.cnt[k] = self.cnt.get(k, 0) + amt
            self.own[k] = None if dma else engine
            tokv = (inc, self.cnt[k], self.own[k])
            post = self.cnt[k]
        self.steps.append((engine, fn, w, inc, amt, chain, post))
        return tokv

    def dma(self, engine, fn, inc, waits=()):
        return self.step(engine, fn, waits=waits, inc=inc, amt=16, dma=True)

    def replay(self, engine, eng):
        hi = {}
        osem = self.order_sems.get(engine)
        ocnt = 0
        prev_upd = None
        for (e, fn, waits, inc, amt, chain, post) in self.steps:
            if e != engine:
                continue
            if osem is not None and chain and prev_upd is not None:
                psem, pval = prev_upd
                k = id(psem)
                if hi.get(k, 0) < pval:
                    hi[k] = pval
                    eng.wait_ge(psem, pval)
            bysem = {}
            for (sem, val, owner) in waits:
                k = id(sem)
                if val > bysem.get(k, (None, 0))[1]:
                    bysem[k] = (sem, val)
            for k, (sem, val) in bysem.items():
                if hi.get(k, 0) >= val:
                    continue
                hi[k] = val
                eng.wait_ge(sem, val)
            inst = fn(eng)
            if inc is not None:
                assert inst is not None, "inc step returned no instruction"
                inst.then_inc(inc, amt)
                prev_upd = (inc, post)
            elif osem is not None:
                assert inst is not None, f"{engine} step returned no instruction"
                ocnt += 1
                inst.then_inc(osem, 1)
                prev_upd = (osem, ocnt)


def build():
    nc = bass.Bass("TRN2", target_bir_lowering=False)
    dbg = "ExternalOutput" if _DEBUG else "Internal"
    di = lambda n, s: nc.dram_tensor(n, s, F32, kind="ExternalInput").ap()
    dd = lambda n, s, k="Internal": nc.dram_tensor(n, s, F32, kind=k).ap()

    x_b = di("x_b", [T, D])
    x_q = di("x_q", [QT, D])
    w_inT = di("w_inT", [D, 2 * CPC])
    w_dtT = di("w_dtT", [CPC, 144])
    w_conv = di("w_conv", [128, 16])
    b_conv = di("b_conv", [128, 4])
    a_log = di("a_log", [HD, HPC * DS])
    d_col = di("d_col", [HD, HPC])
    dtbias = di("dtbias", [HPC, 1])
    b_dt = di("b_dt", [144, 1])
    gamma = di("gamma", [HD, HPC])
    beta = di("beta", [HD, HPC])
    w_outT = di("w_outT", [CPC, D])
    we = di("we", [D, D])
    be = di("be", [1, D])
    wg = di("wg", [D, E])
    eye = di("eye", [128, 128])
    qsel = di("qsel", [128, NK1])
    sel16 = di("sel16", [16, 4])
    step8 = di("step8", [8, 1])

    out_c = nc.dram_tensor("out_c", [QT, D], F32, kind="ExternalOutput").ap()
    x4_hbm = dd("x4_hbm", [CPC, T], dbg)
    z_hbm = dd("z_hbm", [CPC, T], dbg)
    y_hbm = dd("y_hbm", [CPC, T], dbg)
    xp_in = dd("xp_in", [144, T])
    xp_out = dd("xp_out", [144, T])
    dt_dram = dd("dt_dram", [HPC, T], dbg)
    bb_dram = dd("bb_dram", [DS, T], dbg)
    cc_dram = dd("cc_dram", [DS, T], dbg)
    sta_in = dd("sta_in", [2, T])
    sta_out = dd("sta_out", [2, T])
    mu_dram = dd("mu_dram", [1, T], dbg)
    rs_dram = dd("rs_dram", [1, T], dbg)
    loc_dram = dd("loc_dram", [E, QT], dbg)
    base_dram = dd("base_dram", [1, E], dbg)
    tot_in = dd("tot_in", [1, E])
    tot_out = dd("tot_out", [8, E])
    s_in = dd("s_in", [T, D])
    s_out = dd("s_out", [QT, D])

    ctx = []
    def sb(name, shape):
        cm = nc.sbuf_tensor(name, shape, F32)
        t = cm.__enter__(); ctx.append(cm); return t
    def pst(name, shape):
        cm = nc.psum_tensor(name, shape, F32)
        t = cm.__enter__(); ctx.append(cm); return t
    def mksem(name):
        cm = nc.semaphore(name)
        s = cm.__enter__(); ctx.append(cm); return s

    t_eye = sb("t_eye", [128, 128])
    t_ones = sb("t_ones", [128, 8])
    t_wdtT = sb("t_wdtT", [128, 4 * 144])
    t_wcv = sb("t_wcv", [128, 16])
    t_bcv = sb("t_bcv", [128, 4])
    t_a = sb("t_a", [128, HPC * DS])
    t_d = sb("t_d", [128, HPC])
    t_gb = sb("t_gb", [128, 2 * HPC])
    t_dtb = sb("t_dtb", [HPC, 1])
    t_bdt = sb("t_bdt", [128, 2])
    t_wg = sb("t_wg", [128, 8 * E])
    t_qsel = sb("t_qsel", [128, NK1])
    t_sel16 = sb("t_sel16", [16, 4])
    t_step8 = sb("t_step8", [8, 1])
    t_carry = sb("t_carry", [128, HPC * DS])
    t_mask = sb("t_mask", [128, 16 * E])
    t_gate = sb("t_gate", [128, 16])
    t_scale = sb("t_scale", [128, 16])
    t_maskT = sb("t_maskT", [8, QT])
    t_loc = sb("t_loc", [8, QT])
    t_tot = sb("t_tot", [8, 1])
    t_totL = sb("t_totL", [8, 8])
    t_small = sb("t_small", [128, 64])
    t_eps = sb("t_eps", [128, 1])
    AW = 43008
    arena = sb("arena", [128, AW])
    A = lambda off, w: arena[:, off:off + w]
    psb = [pst(f"psb{i}", [128, 512]) for i in range(8)]

    S = Sched()
    S.order_sems = {}
    g = {}
    for n in ("s_ld", "s_x0", "s_x1", "s_stat", "s_ev", "s_mm", "s_rtr",
              "s_cc", "s_pre", "s_p2k", "s_bc0", "s_bc1", "s_dA", "s_scan",
              "s_w", "s_ymm", "s_yev", "s_out", "s_norm", "s_tp",
              "s_spx4", "s_spz", "s_spxp", "s_sprt", "s_spd1", "s_spy",
              "s_spst", "s_sps0", "s_sps1", "s_spmu", "s_pay", "s_pby",
              "s_pbz", "s_pbm", "s_p4w", "s_p4b", "s_p4t", "s_p4bs",
              "s_p4bd", "s_xq0", "s_xq1", "s_p4s", "s_p4x", "s_dbg", "s_pam",
              "s_evs", "s_spy1", "s_u", "s_cy", "s_x4g", "s_dtbs",
              "s_pby1", "s_pbz1", "s_pbm1", "s_xq2", "s_xq3",
              "s_out1", "s_p4s1", "s_p4x1", "s_pay1", "s_wek", "s_wek1"):
        g[n] = mksem(n)

    S.order_sems = {"vector": mksem("ord_v"), "scalar": mksem("ord_s")}
    bank_free = [None] * 8
    _dbg_t = {}

    def mm_group(bank, mms, evac_fn, evac_engine, extra_mm_waits=(),
                 extra_evac_waits=(), evac_drain=False, evac_chain=False):
        mm_tok = None
        for i, fnm in enumerate(mms):
            w = list(extra_mm_waits) if i == 0 else []
            if i == 0 and bank_free[bank] is not None:
                w.append(bank_free[bank])
            mm_tok = S.step("tensor", fnm, waits=w,
                            inc=(g["s_mm"] if i == len(mms) - 1 else None))
        evt = S.step(evac_engine, evac_fn,
                     waits=[mm_tok] + list(extra_evac_waits), inc=g["s_ev"],
                     chain=evac_chain)
        bank_free[bank] = evt
        return evt, mm_tok

    o_xn = 0
    o_xz = o_xn + 8 * TW
    o_zb = o_xz + 4 * TW
    o_cv = o_zb + TW
    o_scr = o_cv + TC1
    o_xld = o_scr + 1024
    o_xp = o_xld + 2 * 1024
    o_xnorm = o_xp + 2 * TC1
    o_scr2 = o_xnorm + 2 * 1024
    o_win = o_scr2 + 1024
    o_end1 = o_win + 8 * 1024
    assert o_end1 <= AW, o_end1
    o_dtb, o_dtx, o_x4 = 0, 4 * TC2, 8 * TC2
    o_bb = 12 * TC2
    o_cb = o_bb + 2 * NBLK * TC2
    o_dA = o_cb + 2 * NBLK * TC2
    o_u = o_dA + NDA * TC2
    o_st2 = o_u + 4 * TC2
    o_w = o_st2 + NST * TC2
    o_dx4 = o_w + NWB * 2 * TC2
    o_yev = o_dx4 + TC2
    o_end2 = o_yev + 2 * TC2
    assert o_end2 <= AW, o_end2
    o_xpl = o_dA
    o_dto = o_st2
    o3_y, o3_z, o3_g = 0, 4 * TC2, 8 * TC2
    o3_mu, o3_rs = 12 * TC2, 13 * TC2
    o3_wo = 14 * TC2
    o3_s = o3_wo + 4 * 1024
    o3_sr = o3_s + 2 * 1024
    o3_end = o3_sr + T + 2048
    assert o3_end <= AW, o3_end
    o3_y2 = o3_end
    o3_z2 = o3_y2 + 4 * TC2
    o3_mu2 = o3_z2 + 4 * TC2
    o3_rs2 = o3_mu2 + TC2
    assert o3_rs2 + TC2 <= AW
    o4_xnT = 0
    o4_eo = o4_xnT + 8 * QT
    o4_we = o4_eo + 4 * 1024
    o4_s = o4_we + 8 * 1024
    o4_x = o4_s + 2 * 1024
    o4_be = o4_x + 2 * 1024
    o4_bs = o4_be + 1024
    o4_lt = o4_bs + 8
    o4_xq = o4_lt + 16 * E
    o4_g2 = o4_xq + 4 * 1024
    o4_scr = o4_g2 + 2 * 1024
    o4_end = o4_scr + 2 * 1024
    assert o4_end <= AW, o4_end

    for fn in (
        lambda e: e.dma_start(out=t_eye[:], in_=eye),
        lambda e: e.dma_start(out=A(o_win, 8 * 1024).rearrange("p (k c) -> p k c", k=8), in_=w_inT.rearrange("(k p) c -> p k c", p=128)),
        lambda e: e.dma_start(out=t_wdtT[:].rearrange("p (k c) -> p k c", k=4), in_=w_dtT.rearrange("(k p) c -> p k c", p=128)),
        lambda e: e.dma_start(out=t_wcv[:], in_=w_conv),
        lambda e: e.dma_start(out=t_bcv[:], in_=b_conv),
        lambda e: e.dma_start(out=t_a[:], in_=a_log),
        lambda e: e.dma_start(out=t_d[:], in_=d_col),
        lambda e: e.dma_start(out=t_gb[:, 0:HPC], in_=gamma),
        lambda e: e.dma_start(out=t_gb[:, HPC:2 * HPC], in_=beta),
        lambda e: e.dma_start(out=t_dtb[:], in_=dtbias),
        lambda e: e.dma_start(out=t_bdt[:, 0:1], in_=b_dt[0:128, :]),
        lambda e: e.dma_start(out=t_bdt[0:16, 1:2], in_=b_dt[128:144, :]),
        lambda e: e.dma_start(out=t_wg[:].rearrange("p (k c) -> p k c", k=8), in_=wg.rearrange("(k p) c -> p k c", p=128)),
        lambda e: e.dma_start(out=t_qsel[:], in_=qsel),
        lambda e: e.dma_start(out=t_sel16[:], in_=sel16),
        lambda e: e.dma_start(out=t_step8[:], in_=step8),
    ):
        S.dma("sync", fn, g["s_ld"])
    all_ld = S.tok(g["s_ld"])

    t_a2 = A(o_yev, HPC * DS)
    ta = S.step("scalar", lambda e: e.activation(t_a2, t_a[:], AF.Exp),
                waits=[all_ld], inc=g["s_pre"])
    a_ready = S.step("scalar", lambda e: e.mul(t_a[:], t_a2, -1.0),
                     waits=[ta], inc=g["s_pre"])
    S.step("vector", lambda e: e.memset(t_ones[:], 1.0))
    S.step("vector", lambda e: e.memset(t_eps[:], EPS))
    S.step("vector", lambda e: e.memset(t_carry[:], 0.0))
    S.step("vector", lambda e: e.memset(t_mask[:], 0.0))
    S.step("vector", lambda e: e.memset(t_gate[:], 0.0))

    def ln_tile(tl_tok, xb, scr, scrB, xnd, dbase, dstride, jcol, tpbanks,
                stats_wait, first_pe_wait=None, xnd_wait=None):
        t1 = S.step("scalar", lambda e, xb=xb, scr=scr: e.activation(
            A(scr, 1024), A(xb, 1024), AF.Copy, accum_out=t_small[:, 0:1]),
            waits=[tl_tok, stats_wait], inc=g["s_stat"], chain=False)
        t2 = S.step("scalar", lambda e, xb=xb, scrB=scrB: e.activation(
            A(scrB, 1024), A(xb, 1024), AF.Square, accum_out=t_small[:, 1:2]),
            waits=[t1], inc=g["s_stat"], chain=False)
        S.step("vector", lambda e: e.tensor_scalar_mul(
            t_small[:, 2:3], t_small[:, 0:1], 1.0 / D), waits=[t2], drain=True)
        S.step("vector", lambda e: e.tensor_scalar_mul(
            t_small[:, 3:4], t_small[:, 1:2], 1.0 / D), drain=True)
        S.step("vector", lambda e: e.tensor_tensor(
            t_small[:, 4:5], t_small[:, 2:3], t_small[:, 2:3], OP.mult),
            drain=True)
        t3 = S.step("vector", lambda e: e.tensor_tensor(
            t_small[:, 5:6], t_small[:, 3:4], t_small[:, 4:5], OP.subtract),
            inc=g["s_norm"], drain=True)
        t4 = S.step("scalar", lambda e: e.activation(
            t_small[:, 6:7], t_small[:, 5:6], AF.Sqrt, bias=t_eps[:, 0:1]),
            waits=[t3], inc=g["s_stat"], chain=False)
        S.step("vector", lambda e: e.reciprocal(
            t_small[:, 7:8], t_small[:, 6:7]), waits=[t4], drain=True)
        tn = S.step("vector", lambda e, xb=xb, xnd=xnd: e.tensor_scalar(
            A(xnd, 1024), A(xb, 1024), t_small[:, 2:3], t_small[:, 7:8],
            OP.subtract, OP.mult), waits=[xnd_wait], inc=g["s_norm"])
        last_pe, last_ev = None, None
        for kk in range(8):
            bank = tpbanks[kk % 2]
            wv = [tn]
            if bank_free[bank] is not None:
                wv.append(bank_free[bank])
            if kk == 0 and first_pe_wait is not None:
                wv.append(first_pe_wait)
            last_pe = S.step("tensor", lambda e, xnd=xnd, kk=kk, bank=bank: e.transpose(
                psb[bank][:, 0:128], A(xnd + kk * 128, 128), t_eye[:]),
                waits=wv, inc=g["s_tp"])
            last_ev = S.step("scalar", lambda e, kk=kk, bank=bank,
                             dbase=dbase, dstride=dstride, jcol=jcol: e.copy(
                arena[:, dbase + kk * dstride + jcol:dbase + kk * dstride + jcol + 128],
                psb[bank][:, 0:128]), waits=[last_pe], inc=g["s_evs"],
                chain=False)
            bank_free[bank] = last_ev
        return tn, last_pe, last_ev

    tp_pe, chunk_pe_done, tn_toks = {}, {}, {}
    x4_sp, xp_sp, z_sp = {}, {}, {}
    conv_done = {}
    dt_pe_last = {}
    z_sp_prev = None
    prev_tn = None
    NCHN = [(0, 512), (512, 512), (1024, 128)]

    for k in range(NK1):
        evs = []
        for j in range(NT1):
            tok0 = k * TC1 - HALO + j * 128
            xb = o_xld + (j % 2) * 1024
            if tok0 < 0:
                tz = None
                for kk in range(8):
                    tz = S.step("vector", lambda e, kk=kk, j=j: e.memset(
                        arena[:, o_xn + kk * TW + j * 128:o_xn + kk * TW + j * 128 + 128], 0.0),
                        inc=g["s_norm"])
                evs.append(tz)
                tp_pe[(k, j)] = None
                continue
            w = []
            if j >= 2:
                w.append(tn_toks.get((k, j - 2)))
            elif k > 0:
                w.append(tn_toks.get((k - 1, NT1 - 1 - (1 - j))))
            if j >= 2:
                w.append(tp_pe.get((k, j - 2)))
            elif k > 0:
                w.append(tp_pe.get((k - 1, NT1 - 1 - (1 - j))))
            tl = S.dma("sync", lambda e, xb=xb, tok0=tok0: e.dma_start(
                out=A(xb, 1024), in_=x_b[tok0:tok0 + 128, :]),
                g["s_x0" if j % 2 == 0 else "s_x1"], waits=w)
            fpw = chunk_pe_done.get(k - 1) if (k > 0 and j == 0) else None
            tn, lpe, lev = ln_tile(tl, xb, o_scr, o_scr2,
                                   o_xnorm + (j % 2) * 1024,
                                   o_xn, TW, j * 128, (6, 7),
                                   prev_tn, first_pe_wait=fpw)
            prev_tn = tn
            tn_toks[(k, j)] = tn
            tp_pe[(k, j)] = lpe
            evs.append(lev)

        pe_last = None
        for m in range(8):
            z_ev_group = []
            for (ni, (noff, nw)) in enumerate(NCHN):
                bank = ni
                mms = []
                for kd in range(8):
                    def fmms(e, m=m, kd=kd, noff=noff, nw=nw, bank=bank):
                        return e.matmul(
                            psb[bank][:, 0:nw],
                            arena[:, o_win + kd * 1024 + m * 128:o_win + kd * 1024 + (m + 1) * 128],
                            arena[:, o_xn + kd * TW + noff:o_xn + kd * TW + noff + nw],
                            start=(kd == 0), stop=(kd == 7))
                    mms.append(fmms)
                extra_ev = []
                if m < 4:
                    def fev(e, m=m, noff=noff, nw=nw, bank=bank):
                        return e.tensor_copy(
                            arena[:, o_xz + m * TW + noff:o_xz + m * TW + noff + nw],
                            psb[bank][:, 0:nw])
                    if k > 0:
                        extra_ev = [dt_pe_last.get(k - 1), x4_sp.get((k - 1, 3)),
                                    conv_done.get((k - 1, m))]
                else:
                    def fev(e, noff=noff, nw=nw, bank=bank):
                        return e.tensor_copy(
                            arena[:, o_zb + noff:o_zb + noff + nw],
                            psb[bank][:, 0:nw])
                    extra_ev = [z_sp_prev]
                ev, mmt = mm_group(bank, mms, fev, "vector",
                                   extra_mm_waits=evs + [all_ld],
                                   extra_evac_waits=extra_ev)
                pe_last = mmt
                if m >= 4:
                    z_ev_group.append(ev)
            if m >= 4:
                z_sp_prev = S.dma("sync", lambda e, k=k, m=m: e.dma_start(
                    out=z_hbm[(m - 4) * 128:(m - 3) * 128, k * TC1:(k + 1) * TC1],
                    in_=A(o_zb + HALO, TC1)), g["s_spz"], waits=z_ev_group)
                z_sp[(k, m - 4)] = z_sp_prev

        for ct in range(4):
            xzoff = o_xz + ct * TW
            cw = [conv_done.get((k, ct - 1)) if ct > 0 else conv_done.get((k - 1, 3))]
            t1 = S.step("vector", lambda e, xzoff=xzoff, ct=ct: e.tensor_scalar(
                A(o_cv, TC1), arena[:, xzoff + HALO - 3:xzoff + HALO - 3 + TC1],
                t_wcv[:, ct * 4:ct * 4 + 1], t_bcv[:, ct:ct + 1],
                OP.mult, OP.add), waits=cw)
            for tap in range(1, 4):
                t1 = S.step("vector", lambda e, xzoff=xzoff, ct=ct, tap=tap: e.scalar_tensor_tensor(
                    A(o_cv, TC1),
                    arena[:, xzoff + HALO - 3 + tap:xzoff + HALO - 3 + tap + TC1],
                    t_wcv[:, ct * 4 + tap:ct * 4 + tap + 1],
                    A(o_cv, TC1), OP.mult, OP.add),
                    inc=(g["s_norm"] if tap == 3 else None))
            tsl = S.step("scalar", lambda e, xzoff=xzoff: e.activation(
                arena[:, xzoff + HALO:xzoff + HALO + TC1], A(o_cv, TC1), AF.Silu),
                waits=[t1], inc=g["s_stat"])
            conv_done[(k, ct)] = tsl
            x4_sp[(k, ct)] = S.dma("sync", lambda e, k=k, ct=ct, xzoff=xzoff: e.dma_start(
                out=x4_hbm[ct * 128:(ct + 1) * 128, k * TC1:(k + 1) * TC1],
                in_=arena[:, xzoff + HALO:xzoff + HALO + TC1]), g["s_spx4"],
                waits=[tsl])

        dt_ev = []
        for (mi, (p0, pw)) in enumerate([(0, 128), (128, 16)]):
            for (ni, noff) in enumerate(range(0, TC1, 512)):
                bank = 5
                mms = []
                for kd in range(4):
                    def fdt(e, mi=mi, p0=p0, pw=pw, kd=kd, noff=noff, bank=bank):
                        return e.matmul(
                            psb[bank][0:pw, 0:512],
                            t_wdtT[:, kd * 144 + p0:kd * 144 + p0 + pw],
                            arena[:, o_xz + kd * TW + HALO + noff:
                                  o_xz + kd * TW + HALO + noff + 512],
                            start=(kd == 0), stop=(kd == 3))
                    mms.append(fdt)
                def fdtev(e, mi=mi, pw=pw, noff=noff, bank=bank):
                    col = t_bdt[0:128, 0:1] if mi == 0 else t_bdt[0:16, 1:2]
                    return e.tensor_scalar(
                        arena[0:pw, o_xp + mi * TC1 + noff:o_xp + mi * TC1 + noff + 512],
                        psb[bank][0:pw, 0:512], col, None, OP.add)
                exw = [xp_sp.get((k - 1, 1))] if k > 0 else []
                ev, mmt = mm_group(bank, mms, fdtev, "vector",
                                   extra_mm_waits=[conv_done[(k, c2)] for c2 in range(4)],
                                   extra_evac_waits=exw)
                dt_ev.append(ev)
                dt_pe_last[k] = mmt
        xp_sp[(k, 0)] = S.dma("sync", lambda e, k=k: e.dma_start(
            out=xp_in[0:128, k * TC1:(k + 1) * TC1], in_=A(o_xp, TC1)),
            g["s_spxp"], waits=dt_ev)
        xp_sp[(k, 1)] = S.dma("sync", lambda e, k=k: e.dma_start(
            out=xp_in[128:144, k * TC1:(k + 1) * TC1],
            in_=arena[0:16, o_xp + TC1:o_xp + 2 * TC1]), g["s_spxp"], waits=dt_ev)

        rt_prev_ev = None
        rpe = pe_last
        for jt in range(TC1 // 128):
            m = (k % 2) * 8 + jt
            bank = 5
            mms = []
            for kd in range(8):
                def frt(e, jt=jt, kd=kd, bank=bank):
                    return e.matmul(
                        psb[bank][:, 0:E],
                        arena[:, o_xn + kd * TW + HALO + jt * 128:
                              o_xn + kd * TW + HALO + (jt + 1) * 128],
                        t_wg[:, kd * E:(kd + 1) * E],
                        start=(kd == 0), stop=(kd == 7))
                mms.append(frt)
            def frtev(e):
                return e.tensor_copy(t_small[:, 8:16], psb[5][:, 0:E])
            ev, mmt = mm_group(bank, mms, frtev, "vector",
                               extra_mm_waits=evs + [all_ld],
                               extra_evac_waits=[rt_prev_ev], evac_chain=True)
            rpe = mmt
            if _DEBUG and k == 0 and jt == 0:
                if "dbg_lg" not in _dbg_t:
                    _dbg_t["dbg_lg"] = dd("dbg_lg", [128, 64], "ExternalOutput")
                S.dma("sync", lambda e: e.dma_start(
                    out=_dbg_t["dbg_lg"][:, 0:8], in_=t_small[:, 8:16]),
                    g["s_dbg"], waits=[ev])
            S.step("vector", lambda e: e.reduce_max(
                t_small[:, 16:17], t_small[:, 8:16], axis=AX.X), drain=True)
            t1 = S.step("vector", lambda e: e.tensor_scalar_mul(
                t_small[:, 17:18], t_small[:, 16:17], -1.0), inc=g["s_norm"],
                drain=True)
            t2 = S.step("scalar", lambda e: e.activation(
                t_small[:, 24:32], t_small[:, 8:16], AF.Exp,
                bias=t_small[:, 17:18]), waits=[t1], inc=g["s_stat"])
            t3 = S.step("vector", lambda e: e.reduce_sum(
                t_small[:, 18:19], t_small[:, 24:32], axis=AX.X), waits=[t2],
                drain=True)
            S.step("vector", lambda e: e.reciprocal(
                t_small[:, 19:20], t_small[:, 18:19]), drain=True)
            S.step("vector", lambda e, m=m, k=k: e.scalar_tensor_tensor(
                t_gate[:, m:m + 1], t_small[:, 19:20], t_qsel[:, k:k + 1],
                t_gate[:, m:m + 1], OP.mult, OP.add), drain=True)
            S.step("vector", lambda e: e.tensor_scalar(
                t_small[:, 32:40], t_small[:, 8:16], t_small[:, 16:17], None,
                OP.is_equal), drain=True)
            S.step("vector", lambda e: e.tensor_tensor_scan(
                t_small[:, 40:48], t_ones[:, 0:8], t_small[:, 32:40], 0.0,
                OP.mult, OP.add), drain=True)
            S.step("vector", lambda e: e.tensor_scalar(
                t_small[:, 56:64], t_small[:, 40:48], 1.0, None, OP.is_equal),
                drain=True)
            S.step("vector", lambda e: e.tensor_tensor(
                t_small[:, 48:56], t_small[:, 32:40], t_small[:, 56:64],
                OP.mult), drain=True)
            rt_prev_ev = S.step("vector", lambda e, m=m, k=k: e.scalar_tensor_tensor(
                t_mask[:, m * E:(m + 1) * E], t_small[:, 48:56],
                t_qsel[:, k:k + 1], t_mask[:, m * E:(m + 1) * E],
                OP.mult, OP.add), inc=g["s_norm"], drain=True)
        chunk_pe_done[k] = rpe

    mlast = S.tok(g["s_norm"])
    mt_prev = None
    for m in range(16):
        wv = [mlast]
        if bank_free[6] is not None:
            wv.append(bank_free[6])
        pt = S.step("tensor", lambda e, m=m: e.transpose(
            psb[6][0:E, 0:128], t_mask[:, m * E:(m + 1) * E], t_eye[:]),
            waits=wv, inc=g["s_tp"])
        mt_prev = S.step("vector", lambda e, m=m: e.tensor_copy(
            t_maskT[:, m * 128:(m + 1) * 128], psb[6][0:E, 0:128]),
            waits=[pt], inc=g["s_ev"], drain=True)
        bank_free[6] = mt_prev
    if _DEBUG:
        dbg_mask = dd("dbg_mask", [128, 16 * E], "ExternalOutput")
        dbg_maskT = dd("dbg_maskT", [8, QT], "ExternalOutput")
        dbg_gate = dd("dbg_gate", [128, 16], "ExternalOutput")
        S.dma("sync", lambda e: e.dma_start(out=dbg_mask, in_=t_mask[:]),
              g["s_dbg"], waits=[mlast, mt_prev])
        S.dma("sync", lambda e: e.dma_start(out=dbg_maskT, in_=t_maskT[:, :]),
              g["s_dbg"], waits=[mt_prev])
        S.dma("sync", lambda e: e.dma_start(out=dbg_gate, in_=t_gate[:]),
              g["s_dbg"], waits=[mlast])
    S.step("vector", lambda e: e.tensor_tensor_scan(
        t_loc[:, :], t_maskT[:, :], t_maskT[:, :], 0.0, OP.add, OP.bypass),
        waits=[mt_prev], drain=True)
    ttot = S.step("vector", lambda e: e.tensor_copy(
        t_tot[:, 0:1], t_loc[:, QT - 1:QT]), inc=g["s_rtr"], drain=True)
    S.dma("sync", lambda e: e.dma_start(out=loc_dram, in_=t_loc[:, :]),
          g["s_sprt"], waits=[ttot])
    S.dma("sync", lambda e: e.dma_start(
        out=tot_in.rearrange("one e -> e one"), in_=t_tot[:, 0:1],
        allow_slow_non_contiguous=True), g["s_sprt"], waits=[ttot])
    ld1 = ld2 = S.tok(g["s_sprt"])

    if _PHASES < 2:
        return _finish(nc, S, ctx)

    grp_quad = [[0, 1, 2, 3], [4, 5, 6, 7]]
    grp_all = [[0, 1, 2, 3, 4, 5, 6, 7]]
    xp_ready = [xp_sp[(k, i)] for k in range(NK1) for i in range(2)]
    ar1 = S.step("gpsimd", lambda e: e.collective_compute(
        "AllReduce", OP.add, ins=[xp_in], outs=[xp_out],
        replica_groups=grp_quad), waits=xp_ready, inc=g["s_cc"])
    if _DEBUG:
        dbg_xpo = dd("dbg_xpo", [144, T], "ExternalOutput")
        S.dma("sync", lambda e: e.dma_start(out=dbg_xpo, in_=xp_out),
              g["s_dbg"], waits=[ar1])
    ag1 = S.step("gpsimd", lambda e: e.collective_compute(
        "AllGather", OP.bypass, ins=[tot_in], outs=[tot_out],
        replica_groups=grp_all), waits=[ld2], inc=g["s_cc"])

    stage_toks = []
    prev_sp = []
    prev_e2 = None
    for kk in range(4):
        c0 = kk * 2048
        wv = [ar1] + prev_sp
        l1 = S.dma("sync", lambda e, c0=c0: e.dma_start(
            out=arena[0:16, o_xpl:o_xpl + 2048], in_=xp_out[0:16, c0:c0 + 2048]),
            g["s_p2k"], waits=wv)
        selev = []
        for (ni, noff) in enumerate(range(0, 2048, 512)):
            def fsel(e, noff=noff):
                return e.matmul(
                    psb[7][0:4, 0:512], t_sel16[:, :],
                    arena[0:16, o_xpl + noff:o_xpl + noff + 512],
                    start=True, stop=True)
            def fselev(e, noff=noff):
                return e.tensor_copy(
                    arena[0:4, o_dto + noff:o_dto + noff + 512],
                    psb[7][0:4, 0:512])
            ev, _ = mm_group(7, [fsel], fselev, "vector",
                             extra_mm_waits=[l1, all_ld],
                             extra_evac_waits=[prev_e2])
            selev.append(ev)
        e1 = S.step("scalar", lambda e: e.activation(
            arena[0:4, o_dto:o_dto + 2048], arena[0:4, o_dto:o_dto + 2048],
            AF.Exp, bias=t_dtb[:, 0:1]), waits=selev, inc=g["s_stat"])
        prev_e2 = S.step("scalar", lambda e: e.activation(
            arena[0:4, o_dto:o_dto + 2048], arena[0:4, o_dto:o_dto + 2048],
            AF.Ln, bias=1.0), inc=g["s_stat"])
        d1 = S.dma("sync", lambda e, c0=c0: e.dma_start(
            out=dt_dram[:, c0:c0 + 2048], in_=arena[0:4, o_dto:o_dto + 2048]),
            g["s_spd1"], waits=[prev_e2])
        prev_sp = [d1]
        stage_toks += [d1]
    stage_toks.append(ar1)

    dA_toks, scan_toks, w_toks, mm_toks = {}, {}, {}, {}
    dump_toks = {}
    x4r, dtbr, dtxr = {}, {}, {}
    yev_toks, ysp_toks = {}, {}

    def itix(k, nb, n, h):
        return ((k * NNB + nb) * NBLK + n) * 4 + h

    I_LAST = NK2 * NNB * NBLK * 4 - 1

    x4_all = S.tok(g["s_spx4"])
    z_all0 = S.tok(g["s_spz"])
    x_all0 = [S.tok(g["s_x0"]), S.tok(g["s_x1"])]

    NIT = NNB * NBLK * 4
    bc_tok = {}
    wpair = {}
    u_tok, cy_tok = {}, {}

    def gi_parts(gi):
        k, r = divmod(gi, NIT)
        nb, r2 = divmod(r, 4 * NBLK)
        h, n = divmod(r2, NBLK)
        return k, nb, n, h

    def emit_bc(k, nb):
        par = nb % 2
        gnb = k * NNB + nb
        if gnb < 2:
            wv = list(stage_toks) + [z_all0, x4_all] + x_all0
        else:
            pgi = (gnb - 2) * 4 * NBLK + 4 * NBLK - 1
            wv = [u_tok[pgi], wpair[pgi]]
        bcs = g["s_bc0" if par == 0 else "s_bc1"]
        S.dma("sync", lambda e, k=k, nb=nb, par=par: e.dma_start(
            out=A(o_bb + par * NBLK * TC2, NBLK * TC2).rearrange(
                "p (n t) -> p n t", n=NBLK),
            in_=xp_out[16 + nb * NBLK:16 + (nb + 1) * NBLK,
                       k * TC2:(k + 1) * TC2][None, :, :]
            .broadcast_to([128, NBLK, TC2])), bcs, waits=wv)
        bc_tok[gnb] = S.dma("sync", lambda e, k=k, nb=nb, par=par: e.dma_start(
            out=A(o_cb + par * NBLK * TC2, NBLK * TC2).rearrange(
                "p (n t) -> p n t", n=NBLK),
            in_=xp_out[80 + nb * NBLK:80 + (nb + 1) * NBLK,
                       k * TC2:(k + 1) * TC2][None, :, :]
            .broadcast_to([128, NBLK, TC2])), bcs, waits=wv)

    def emit_u(gi):
        k, nb, n, h = gi_parts(gi)
        par = nb % 2
        u_tok[gi] = S.step("vector", lambda e, h=h, n=n, par=par, gi=gi: e.tensor_tensor(
            A(o_u + (gi % 4) * TC2, TC2), A(o_dtx + h * TC2, TC2),
            A(o_bb + par * NBLK * TC2 + n * TC2, TC2), OP.mult),
            waits=[bc_tok[k * NNB + nb], dtxr[(k, 3)],
                   scan_toks.get(gi - 4)], inc=g["s_u"], chain=False)

    pend_carry = []
    for k in range(NK2):
        for h in range(4):
            wvx = list(stage_toks) + [x4_all]
            wvd = list(stage_toks) + [x4_all]
            if k > 0:
                wvx.append(yev_toks[(k - 1, h)])
                wvd.append(dA_toks[k * NIT - 1])
            S.dma("gpsimd", lambda e, k=k, h=h: e.dma_start(
                out=A(o_x4 + h * TC2, TC2),
                in_=x4_hbm[h * 128:(h + 1) * 128, k * TC2:(k + 1) * TC2]),
                g["s_x4g"], waits=wvx)
            S.dma("scalar", lambda e, k=k, h=h: e.dma_start(
                out=A(o_dtb + h * TC2, TC2),
                in_=dt_dram[h:h + 1, k * TC2:(k + 1) * TC2].broadcast_to([128, TC2])),
                g["s_dtbs"], waits=wvd)
        chunk_ld_x = S.tok(g["s_x4g"])
        chunk_ld_d = S.tok(g["s_dtbs"])
        for h in range(4):
            dtxr[(k, h)] = S.step("vector", lambda e, h=h: e.tensor_tensor(
                A(o_dtx + h * TC2, TC2), A(o_dtb + h * TC2, TC2),
                A(o_x4 + h * TC2, TC2), OP.mult),
                waits=[chunk_ld_x, chunk_ld_d], inc=g["s_norm"], chain=False)
            dtbr[(k, h)] = chunk_ld_d
        emit_bc(k, 0)
        emit_bc(k, 1)
        for (pc_h, pc_nn, pc_gi) in pend_carry:
            cy_tok[pc_gi] = S.step("vector", lambda e, pc_h=pc_h, pc_nn=pc_nn, pc_gi=pc_gi: e.tensor_copy(
                t_carry[:, pc_h * DS + pc_nn:pc_h * DS + pc_nn + 1],
                arena[:, o_st2 + (pc_gi % NST) * TC2 + TC2 - 1:
                      o_st2 + (pc_gi % NST) * TC2 + TC2]),
                waits=[scan_toks[pc_gi]], inc=g["s_cy"], chain=False)
        pend_carry = []
        emit_u(k * NIT + 0)
        emit_u(k * NIT + 1)
        emit_u(k * NIT + 2)
        for i in range(NIT):
            gi = k * NIT + i
            _, nb, n, h = gi_parts(gi)
            nn = nb * NBLK + n
            if i % (4 * NBLK) == 0 and nb >= 1 and nb + 1 < NNB:
                emit_bc(k, nb + 1)
            wv = [chunk_ld_d, a_ready]
            if gi >= NDA:
                wv.append(scan_toks.get(gi - NDA))
            dA_toks[gi] = S.step("scalar", lambda e, h=h, nn=nn, gi=gi: e.activation(
                A(o_dA + (gi % NDA) * TC2, TC2), A(o_dtb + h * TC2, TC2),
                AF.Exp, scale=t_a[:, h * DS + nn:h * DS + nn + 1]),
                waits=wv, inc=g["s_dA"], chain=False)
            wv = [dA_toks[gi], u_tok[gi], cy_tok.get(gi - NST)]
            if gi >= NST:
                wv.append(wpair.get((gi - NST) | 1))
            scan_toks[gi] = S.step("vector", lambda e, h=h, nn=nn, gi=gi: e.tensor_tensor_scan(
                A(o_st2 + (gi % NST) * TC2, TC2),
                A(o_dA + (gi % NDA) * TC2, TC2),
                A(o_u + (gi % 4) * TC2, TC2),
                t_carry[:, h * DS + nn:h * DS + nn + 1],
                OP.mult, OP.add), waits=wv, inc=g["s_scan"], chain=False)
            if i + 3 < NIT:
                emit_u(gi + 3)
            if i >= 1:
                _, pnb, pn, ph = gi_parts(gi - 1)
                pnn = pnb * NBLK + pn
                cy_tok[gi - 1] = S.step("vector", lambda e, ph=ph, pnn=pnn, gi=gi: e.tensor_copy(
                    t_carry[:, ph * DS + pnn:ph * DS + pnn + 1],
                    arena[:, o_st2 + ((gi - 1) % NST) * TC2 + TC2 - 1:
                          o_st2 + ((gi - 1) % NST) * TC2 + TC2]),
                    waits=[scan_toks[gi - 1]], inc=g["s_cy"], chain=False)
            if i == NIT - 1:
                pend_carry.append((h, nn, gi))
            if i % 2 == 1:
                pbase = gi - 1
                slot = pbase % NST
                wbuf = (pbase // 2) % NWB
                par = nb % 2
                wv = [scan_toks[gi], bc_tok[k * NNB + nb]]
                if pbase >= 2 * NWB:
                    wv.append(mm_toks.get((pbase - 2 * NWB) | 1))
                wpair[gi] = S.step("vector", lambda e, par=par, slot=slot, wbuf=wbuf: e.tensor_tensor(
                    A(o_w + wbuf * 2 * TC2, 2 * TC2),
                    A(o_st2 + slot * TC2, 2 * TC2),
                    A(o_cb + par * NBLK * TC2, 2 * TC2),
                    OP.mult), waits=wv, inc=g["s_w"], chain=False)
                first = (nb == 0 and n == 1)
                last = (nb == NNB - 1 and n == 1)
                for sub in range(4):
                    bank = 2 * h + (sub % 2)
                    wv = [wpair[gi], all_ld]
                    if first and sub < 2:
                        if k == 0:
                            wv.append(bank_free[bank])
                        else:
                            wv.append(yev_toks[(k - 1, h)])
                    mm_toks[gi] = S.step(
                        "tensor", lambda e, sub=sub, bank=bank, wbuf=wbuf,
                        first=first, last=last: e.matmul(
                            psb[bank][:, 0:512], t_eye[:],
                            arena[:, o_w + wbuf * 2 * TC2 + sub * 512:
                                  o_w + wbuf * 2 * TC2 + sub * 512 + 512],
                            start=(first and sub < 2), stop=(last and sub >= 2)),
                        waits=wv, inc=(g["s_ymm"] if sub == 3 else None))
        for h in range(4):
            jj = k * 4 + h
            tok_i = k * NIT + ((NNB - 1) * 4 + h) * NBLK + 1
            S.step("vector", lambda e, h=h: e.tensor_scalar(
                A(o_dx4, TC2), A(o_x4 + h * TC2, TC2), t_d[:, h:h + 1], None,
                OP.mult), waits=[mm_toks[tok_i]])
            wv = [ysp_toks.get(jj - 2)]
            S.step("vector", lambda e, h=h, jj=jj: e.tensor_tensor(
                A(o_yev + (jj % 2) * TC2, 512), psb[2 * h][:, 0:512],
                A(o_dx4, 512), OP.add), waits=wv)
            y2 = S.step("vector", lambda e, h=h, jj=jj: e.tensor_tensor(
                arena[:, o_yev + (jj % 2) * TC2 + 512:o_yev + (jj % 2) * TC2 + 1024],
                psb[2 * h + 1][:, 0:512],
                arena[:, o_dx4 + 512:o_dx4 + 1024], OP.add), inc=g["s_yev"])
            yev_toks[(k, h)] = y2
            ysp_toks[jj] = S.dma("sync", lambda e, h=h, k=k, jj=jj: e.dma_start(
                out=y_hbm[h * 128:(h + 1) * 128, k * TC2:(k + 1) * TC2],
                in_=A(o_yev + (jj % 2) * TC2, TC2)),
                g["s_spy" if jj % 2 == 0 else "s_spy1"], waits=[y2])

    if _PHASES < 3:
        return _finish(nc, S, ctx)

    p2_drain = [scan_toks[I_LAST], wpair[I_LAST]] + \
        [yev_toks[(NK2 - 1, hh)] for hh in range(4)]
    y_all_sp = S.tok(g["s_spy"])
    y_all_sp1 = S.tok(g["s_spy1"])

    st_ev = {}
    sp_done = {}
    for k in range(NK2):
        pay = o3_y if k % 2 == 0 else o3_y2
        pag = o3_g if k % 2 == 0 else o3_z
        spay = g["s_pay" if k % 2 == 0 else "s_pay1"]
        yld = {}
        for h in range(4):
            wv = [y_all_sp, y_all_sp1]
            if k < 2:
                wv += p2_drain
            else:
                wv.append(st_ev[(k - 2, 1, 1)])
            yld[h] = S.dma("sync", lambda e, k=k, h=h, pay=pay: e.dma_start(
                out=A(pay + h * TC2, TC2),
                in_=y_hbm[h * 128:(h + 1) * 128, k * TC2:(k + 1) * TC2]),
                spay, waits=wv)
        yall = S.tok(spay)
        yld = {h: yall for h in range(4)}
        sq = {}
        for h in range(4):
            sqw = [yld[h]]
            if k < 2:
                sqw += p2_drain
            else:
                sqw.append(st_ev[(k - 2, 1, 1)])
            sq[h] = S.step("scalar", lambda e, h=h, pay=pay, pag=pag: e.activation(
                A(pag + h * TC2, TC2), A(pay + h * TC2, TC2), AF.Square),
                waits=sqw, inc=g["s_stat"], chain=False)
        for (si, src_off) in ((0, pay), (1, pag)):
            for half in range(2):
                bank = 4 + si * 2 + half
                mms = []
                for h in range(4):
                    def fst(e, h=h, src_off=src_off, half=half, bank=bank):
                        return e.matmul(
                            psb[bank][0:1, 0:512], t_ones[:, 0:1],
                            A(src_off + h * TC2 + half * 512, 512),
                            start=(h == 0), stop=(h == 3))
                    mms.append(fst)
                def fstev(e, si=si, half=half, k=k, bank=bank):
                    off = o3_sr + si * 4096 + (k % 4) * 1024 + half * 512
                    return e.tensor_copy(arena[0:1, off:off + 512],
                                         psb[bank][0:1, 0:512])
                exw = [sp_done[(k - 4, si)]] if k >= 4 else []
                st_ev[(k, si, half)], _ = mm_group(
                    bank, mms, fstev, "vector",
                    extra_mm_waits=[sq[h2] for h2 in range(4)] + list(yld.values()),
                    extra_evac_waits=exw)
        if k % 4 == 3:
            deps = [st_ev[(kk2, si, hh)] for kk2 in range(k - 3, k + 1)
                    for si in range(2) for hh in range(2)]
            for si in range(2):
                S.dma("sync", lambda e, k=k, si=si: e.dma_start(
                    out=sta_in[si:si + 1, (k - 3) * TC2:(k + 1) * TC2],
                    in_=arena[0:1, o3_sr + si * 4096:o3_sr + si * 4096 + 4096]),
                    g["s_spst"], waits=deps)
            tpair = S.tok(g["s_spst"])
            for kk2 in range(k - 3, k + 1):
                for si in range(2):
                    sp_done[(kk2, si)] = tpair

    ar2 = S.step("gpsimd", lambda e: e.collective_compute(
        "AllReduce", OP.add, ins=[sta_in], outs=[sta_out],
        replica_groups=grp_quad),
        waits=[sp_done[(NK2 - 1, 0)], sp_done[(NK2 - 1, 1)]], inc=g["s_cc"])
    m1 = m2 = None
    for c4 in range(4):
        c0 = c4 * 2048
        oA, oB, oC, oD, oE = (o3_sr, o3_sr + 2048, o3_sr + 4096,
                              o3_sr + 6144, o3_sr + 8192)
        la = S.dma("sync", lambda e, c0=c0, oA=oA: e.dma_start(
            out=arena[0:1, oA:oA + 2048], in_=sta_out[0:1, c0:c0 + 2048]),
            g["s_pam"], waits=[ar2, st_ev[(NK2 - 1, 1, 1)], m1, m2])
        lb = S.dma("sync", lambda e, c0=c0, oB=oB: e.dma_start(
            out=arena[0:1, oB:oB + 2048], in_=sta_out[1:2, c0:c0 + 2048]),
            g["s_pam"], waits=[ar2, m1, m2])
        la = S.tok(g["s_pam"])
        lb = la
        S.step("vector", lambda e, oA=oA, oD=oD: e.tensor_scalar_mul(
            arena[0:1, oD:oD + 2048], arena[0:1, oA:oA + 2048], 1.0 / DI),
            waits=[la])
        S.step("vector", lambda e, oB=oB, oC=oC: e.tensor_scalar_mul(
            arena[0:1, oC:oC + 2048], arena[0:1, oB:oB + 2048], 1.0 / DI),
            waits=[lb], drain=True)
        S.step("vector", lambda e, oA=oA, oD=oD: e.tensor_tensor(
            arena[0:1, oA:oA + 2048], arena[0:1, oD:oD + 2048],
            arena[0:1, oD:oD + 2048], OP.mult), drain=True)
        v1 = S.step("vector", lambda e, oA=oA, oB=oB, oC=oC: e.tensor_tensor(
            arena[0:1, oB:oB + 2048], arena[0:1, oC:oC + 2048],
            arena[0:1, oA:oA + 2048], OP.subtract), inc=g["s_norm"],
            drain=True)
        v2 = S.step("scalar", lambda e, oB=oB, oC=oC: e.activation(
            arena[0:1, oC:oC + 2048], arena[0:1, oB:oB + 2048], AF.Sqrt,
            bias=t_eps[0:1, 0:1]), waits=[v1], inc=g["s_stat"])
        v3 = S.step("vector", lambda e, oC=oC, oE=oE: e.reciprocal(
            arena[0:1, oE:oE + 2048], arena[0:1, oC:oC + 2048]),
            waits=[v2], inc=g["s_norm"], drain=True)
        m1 = S.dma("sync", lambda e, c0=c0, oD=oD: e.dma_start(
            out=mu_dram[0:1, c0:c0 + 2048], in_=arena[0:1, oD:oD + 2048]),
            g["s_spmu"], waits=[v3])
        m2 = S.dma("sync", lambda e, c0=c0, oE=oE: e.dma_start(
            out=rs_dram[0:1, c0:c0 + 2048], in_=arena[0:1, oE:oE + 2048]),
            g["s_spmu"], waits=[v3])

    wol = S.dma("sync", lambda e: e.dma_start(
        out=A(o3_wo, 4 * 1024).rearrange("p (g c) -> p g c", g=4),
        in_=w_outT.rearrange("(g p) c -> p g c", p=128)), g["s_p4w"])
    z_all = S.tok(g["s_spz"])
    mu_all = S.tok(g["s_spmu"])
    sev = {}
    s_sp_tok = {}
    pb_done = {}
    gt_hist = {}
    wel_a = wel_b = None
    for k in range(NK2):
        if k == 2:
            wel_a = S.dma("gpsimd", lambda e: e.dma_start(
                out=A(o4_we, 4 * 1024).rearrange("p (k c) -> p k c", k=4),
                in_=we[0:512, :].rearrange("(k p) c -> p k c", p=128)),
                g["s_wek"], waits=[m1, m2])
            wel_b = S.dma("gpsimd", lambda e: e.dma_start(
                out=A(o4_we + 4 * 1024, 4 * 1024).rearrange("p (k c) -> p k c", k=4),
                in_=we[512:1024, :].rearrange("(k p) c -> p k c", p=128)),
                g["s_wek1"], waits=[m1, m2])
        oy = o3_y if k % 2 == 0 else o3_y2
        oz = o3_z if k % 2 == 0 else o3_z2
        om = o3_mu if k % 2 == 0 else o3_mu2
        ors = o3_rs if k % 2 == 0 else o3_rs2
        sm = g["s_pbm" if k % 2 == 0 else "s_pbm1"]
        sy = g["s_pby" if k % 2 == 0 else "s_pby1"]
        sz = g["s_pbz" if k % 2 == 0 else "s_pbz1"]
        mw = [mu_all, gt_hist.get(k - 2)]
        mub = S.dma("sync", lambda e, k=k, om=om: e.dma_start(
            out=A(om, TC2),
            in_=mu_dram[0:1, k * TC2:(k + 1) * TC2].broadcast_to([128, TC2])),
            sm, waits=mw)
        S.dma("sync", lambda e, k=k, ors=ors: e.dma_start(
            out=A(ors, TC2),
            in_=rs_dram[0:1, k * TC2:(k + 1) * TC2].broadcast_to([128, TC2])),
            sm, waits=mw)
        mub = S.tok(sm)
        rsb = mub
        gtok = {}
        for h in range(4):
            S.dma("gpsimd", lambda e, k=k, h=h, oy=oy: e.dma_start(
                out=A(oy + h * TC2, TC2),
                in_=y_hbm[h * 128:(h + 1) * 128, k * TC2:(k + 1) * TC2]),
                sy, waits=[y_all_sp, y_all_sp1, pb_done.get(k - 2)])
            S.dma("scalar", lambda e, k=k, h=h, oz=oz: e.dma_start(
                out=A(oz + h * TC2, TC2),
                in_=z_hbm[h * 128:(h + 1) * 128, k * TC2:(k + 1) * TC2]),
                sz, waits=[z_all, gt_hist.get(k - 2)])
        pby_tok = S.tok(sy)
        pbz_tok = S.tok(sz)
        for h in range(4):
            gw = [pby_tok, mub, pb_done.get(k - 2)]
            S.step("vector", lambda e, h=h, oy=oy, om=om: e.tensor_tensor(
                A(o3_g + h * TC2, TC2), A(oy + h * TC2, TC2), A(om, TC2),
                OP.subtract), waits=gw)
            S.step("vector", lambda e, h=h, oy=oy, ors=ors: e.tensor_tensor(
                A(oy + h * TC2, TC2), A(o3_g + h * TC2, TC2), A(ors, TC2),
                OP.mult), waits=[rsb])
            S.step("vector", lambda e, h=h, oy=oy: e.tensor_scalar(
                A(o3_g + h * TC2, TC2), A(oy + h * TC2, TC2),
                t_gb[:, h:h + 1], t_gb[:, HPC + h:HPC + h + 1],
                OP.mult, OP.add), inc=g["s_norm"])
            sg = S.step("scalar", lambda e, h=h, oz=oz: e.activation(
                A(oz + h * TC2, TC2), A(oz + h * TC2, TC2), AF.Sigmoid),
                waits=[pbz_tok], inc=g["s_stat"], chain=False)
            gtok[h] = S.step("vector", lambda e, h=h, oy=oy, oz=oz: e.tensor_tensor(
                A(oy + h * TC2, TC2), A(o3_g + h * TC2, TC2),
                A(oz + h * TC2, TC2), OP.mult), waits=[sg], inc=g["s_norm"])
        gt_hist[k] = gtok[3]
        for m in range(TC2 // 128):
            for half in range(2):
                bank = half + 2 * (m % 2)
                mms = []
                for cg in range(4):
                    def fop(e, m=m, cg=cg, half=half, bank=bank, oy=oy):
                        return e.matmul(
                            psb[bank][:, 0:512],
                            arena[:, oy + cg * TC2 + m * 128:
                                  oy + cg * TC2 + (m + 1) * 128],
                            A(o3_wo + cg * 1024 + half * 512, 512),
                            start=(cg == 0), stop=(cg == 3))
                    mms.append(fop)
                def fopev(e, m=m, half=half, bank=bank):
                    return e.tensor_copy(
                        A(o3_s + (m % 2) * 1024 + half * 512, 512),
                        psb[bank][:, 0:512])
                ms = k * (TC2 // 128) + m
                sev[(k, m, half)], _ = mm_group(
                    bank, mms, fopev, "vector",
                    extra_mm_waits=[gtok[h2] for h2 in range(4)] + [wol],
                    extra_evac_waits=[s_sp_tok.get(ms - 2)])
            s_sp_tok[ms] = S.dma("sync", lambda e, k=k, m=m: e.dma_start(
                out=s_in[k * TC2 + m * 128:k * TC2 + (m + 1) * 128, :],
                in_=A(o3_s + (m % 2) * 1024, 1024)),
                g["s_sps0" if m % 2 == 0 else "s_sps1"],
                waits=[sev[(k, m, 0)], sev[(k, m, 1)]])
        pb_done[k] = sev[(k, TC2 // 128 - 1, 1)]

    p3_last_ev = sev[(NK2 - 1, TC2 // 128 - 1, 1)]
    rs1 = S.step("gpsimd", lambda e: e.collective_compute(
        "ReduceScatter", OP.add, ins=[s_in], outs=[s_out],
        replica_groups=grp_quad),
        waits=[S.tok(g["s_sps0"]), S.tok(g["s_sps1"])], inc=g["s_cc"])
    if _DEBUG:
        dbg_so = dd("dbg_so", [QT, D], "ExternalOutput")
        S.dma("sync", lambda e: e.dma_start(out=dbg_so, in_=s_out),
              g["s_dbg"], waits=[rs1])
        dbg_to = dd("dbg_to", [8, E], "ExternalOutput")
        S.dma("sync", lambda e: e.dma_start(out=dbg_to, in_=tot_out),
              g["s_dbg"], waits=[ag1])

    if _PHASES < 4:
        return _finish(nc, S, ctx)

    sps0_tok = S.tok(g["s_sps0"])
    sps1_tok = S.tok(g["s_sps1"])
    bel = S.dma("sync", lambda e: e.dma_start(
        out=A(o4_be, 1024), in_=be.broadcast_to([128, 1024])), g["s_p4b"],
        waits=[p3_last_ev])
    prev_tn4 = None
    tn4, lpe4, xn_lev = {}, {}, {}
    for j in range(16):
        xb = o4_xq + (j % 4) * 1024
        w = [tn4.get(j - 4)] if j >= 4 else [p3_last_ev]
        tl = S.dma("sync", lambda e, xb=xb, j=j: e.dma_start(
            out=A(xb, 1024), in_=x_q[j * 128:(j + 1) * 128, :]),
            g[("s_xq0", "s_xq1", "s_xq2", "s_xq3")[j % 4]], waits=w)
        fpw = p3_last_ev if j == 0 else None
        tn, lpe, lev = ln_tile(tl, xb, o4_scr, o4_scr + 1024,
                               o4_g2 + (j % 2) * 1024,
                               o4_xnT, QT, j * 128, (6, 7),
                               prev_tn4, first_pe_wait=fpw,
                               xnd_wait=lpe4.get(j - 2))
        prev_tn4 = tn
        tn4[j], lpe4[j], xn_lev[j] = tn, lpe, lev

    ttl = S.dma("sync", lambda e: e.dma_start(out=t_totL[:], in_=tot_out),
                g["s_p4t"], waits=[ag1])
    def fb(e):
        return e.matmul(psb[5][0:1, 0:E], t_step8[:], t_totL[:],
                        start=True, stop=True)
    def fbev(e):
        inst = e.tensor_copy(t_small[0:1, 36:44], psb[5][0:1, 0:E])
        return inst
    bev, _ = mm_group(5, [fb], fbev, "vector", extra_mm_waits=[ttl, all_ld])
    bd = S.dma("sync", lambda e: e.dma_start(
        out=base_dram, in_=t_small[0:1, 36:44]), g["s_p4bd"], waits=[bev])
    bb2 = S.dma("sync", lambda e: e.dma_start(
        out=A(o4_bs, 8), in_=base_dram.broadcast_to([128, 8])), g["s_p4bs"],
        waits=[bd])
    ltl = None
    for m in range(16):
        wv = [ld1]
        if bank_free[5] is not None:
            wv.append(bank_free[5])
        pt = S.step("tensor", lambda e, m=m: e.transpose(
            psb[5][:, 0:E], t_loc[:, m * 128:(m + 1) * 128], t_eye[0:E, 0:E]),
            waits=wv, inc=g["s_tp"])
        ltl = S.step("vector", lambda e, m=m: e.tensor_copy(
            arena[:, o4_lt + m * E:o4_lt + (m + 1) * E], psb[5][:, 0:E]),
            waits=[pt], inc=g["s_ev"], drain=True)
        bank_free[5] = ltl
    sc_tok = {}
    for m in range(16):
        S.step("vector", lambda e, m=m: e.tensor_tensor(
            t_small[:, 24:32], arena[:, o4_lt + m * 8:o4_lt + (m + 1) * 8],
            A(o4_bs, 8), OP.add), waits=[bb2, ltl], drain=True)
        S.step("vector", lambda e: e.tensor_scalar(
            t_small[:, 44:52], t_small[:, 24:32], float(CAP), None, OP.is_le),
            drain=True)
        S.step("vector", lambda e, m=m: e.tensor_tensor(
            t_small[:, 24:32], t_small[:, 44:52],
            t_mask[:, m * E:(m + 1) * E], OP.mult), drain=True)
        S.step("vector", lambda e: e.reduce_sum(
            t_small[:, 32:33], t_small[:, 24:32], axis=AX.X), drain=True)
        sc_tok[m] = S.step("vector", lambda e, m=m: e.tensor_tensor(
            t_scale[:, m:m + 1], t_small[:, 32:33], t_gate[:, m:m + 1],
            OP.mult), inc=g["s_norm"], drain=True)

    out_hist = {}
    lpe_last = lpe4[15]
    for m in range(16):
        par = m % 2
        par4 = m % 4
        evs4 = []
        for half in range(2):
            bank = half + 2 * (m % 2)
            mm_tok = None
            for kd in range(8):
                w = [wel_a if kd < 4 else wel_b]
                if kd == 0:
                    w.append(xn_lev[m])
                    if bank_free[bank] is not None:
                        w.append(bank_free[bank])
                mm_tok = S.step(
                    "tensor", lambda e, m=m, kd=kd, half=half, bank=bank: e.matmul(
                        psb[bank][:, 0:512],
                        arena[:, o4_xnT + kd * QT + m * 128:
                              o4_xnT + kd * QT + (m + 1) * 128],
                        A(o4_we + kd * 1024 + half * 512, 512),
                        start=(kd == 0), stop=(kd == 7)),
                    waits=w, inc=(g["s_mm"] if kd == 7 else None))
            def fev(e, par4=par4, half=half, bank=bank):
                return e.tensor_copy(
                    A(o4_eo + par4 * 1024 + half * 512, 512), psb[bank][:, 0:512])
            ev = S.step("vector", fev,
                        waits=[mm_tok, out_hist.get(m - 4),
                               sps0_tok, sps1_tok], inc=g["s_ev"])
            bank_free[bank] = ev
            evs4.append(ev)
        t1w = [bel] + evs4 + ([lpe_last] if m < 2 else [])
        t1 = S.step("vector", lambda e, par=par, par4=par4: e.tensor_tensor(
            A(o4_g2 + par * 1024, 1024), A(o4_eo + par4 * 1024, 1024),
            A(o4_be, 1024), OP.add), waits=t1w, inc=g["s_norm"])
        t2 = S.step("scalar", lambda e, m=m, par=par, par4=par4: e.activation(
            A(o4_eo + par4 * 1024, 1024), A(o4_g2 + par * 1024, 1024), AF.Silu,
            scale=t_scale[:, m:m + 1]), waits=[t1, sc_tok[m]], inc=g["s_stat"],
            chain=False)
        sl = S.dma("gpsimd", lambda e, m=m, par=par: e.dma_start(
            out=A(o4_s + par * 1024, 1024), in_=s_out[m * 128:(m + 1) * 128, :]),
            g["s_p4s" if par == 0 else "s_p4s1"], waits=[rs1, out_hist.get(m - 2)])
        xlw = [out_hist.get(m - 2)] + ([p3_last_ev] if m < 2 else [])
        xl = S.dma("scalar", lambda e, m=m, par=par: e.dma_start(
            out=A(o4_x + par * 1024, 1024), in_=x_q[m * 128:(m + 1) * 128, :]),
            g["s_p4x" if par == 0 else "s_p4x1"], waits=xlw)
        t3 = S.step("vector", lambda e, par=par, par4=par4: e.tensor_tensor(
            A(o4_g2 + par * 1024, 1024), A(o4_eo + par4 * 1024, 1024),
            A(o4_s + par * 1024, 1024), OP.mult), waits=[t2, sl], inc=g["s_norm"])
        t4 = S.step("vector", lambda e, par=par, par4=par4: e.tensor_tensor(
            A(o4_eo + par4 * 1024, 1024), A(o4_g2 + par * 1024, 1024),
            A(o4_x + par * 1024, 1024), OP.add), waits=[xl], inc=g["s_norm"])
        out_hist[m] = S.dma("sync" if par == 0 else "scalar",
                            lambda e, m=m, par4=par4: e.dma_start(
            out=out_c[m * 128:(m + 1) * 128, :], in_=A(o4_eo + par4 * 1024, 1024)),
            g["s_out" if par == 0 else "s_out1"], waits=[t4])

    S.step("vector", lambda e: e.engine_nop(),
           waits=[out_hist[14], out_hist[15]])
    return _finish(nc, S, ctx)


def _finish(nc, S, ctx):
    with nc.Block() as block:
        @block.sync
        def _(eng):
            S.replay("sync", eng)

        @block.scalar
        def _(eng):
            S.replay("scalar", eng)

        @block.vector
        def _(eng):
            S.replay("vector", eng)

        @block.tensor
        def _(eng):
            S.replay("tensor", eng)

        @block.gpsimd
        def _(eng):
            S.replay("gpsimd", eng)

    for cm in reversed(ctx):
        cm.__exit__(None, None, None)
    return nc


def host_inputs(core, inputs):
    b, q = core // 4, core % 4
    ch0 = CPC * q
    W_in = np.asarray(inputs["L0_attn_W_in"], np.float32)
    m = {}
    xb = np.ascontiguousarray(np.asarray(
        inputs["L0_attn_hidden_states"], np.float32)[b])
    m["x_b"] = xb
    m["x_q"] = np.ascontiguousarray(xb[q * QT:(q + 1) * QT])
    wx = W_in[ch0:ch0 + CPC]
    wz = W_in[DI + ch0:DI + ch0 + CPC]
    m["w_inT"] = np.ascontiguousarray(np.concatenate([wx, wz], 0).T)
    m["w_dtT"] = np.ascontiguousarray(
        np.asarray(inputs["L0_attn_W_dt"], np.float32)[:, ch0:ch0 + CPC].T)
    m["w_conv"] = np.ascontiguousarray(
        np.asarray(inputs["L0_attn_W_conv"], np.float32).reshape(DI, 4)[ch0:ch0 + CPC]
        .reshape(4, 128, 4).transpose(1, 0, 2).reshape(128, 16))
    m["b_conv"] = np.ascontiguousarray(
        np.asarray(inputs["L0_attn_b_conv"], np.float32)[ch0:ch0 + CPC].reshape(4, 128).T)
    al = np.asarray(inputs["L0_attn_A_log"], np.float32).reshape(HN, HD, DS)[4 * q:4 * q + 4]
    m["a_log"] = np.ascontiguousarray(al.transpose(1, 0, 2).reshape(HD, HPC * DS))
    m["d_col"] = np.ascontiguousarray(
        np.asarray(inputs["L0_attn_D"], np.float32).reshape(HN, HD)[4 * q:4 * q + 4].T)
    m["dtbias"] = np.ascontiguousarray(
        np.asarray(inputs["L0_attn_dt_bias"], np.float32)[4 * q:4 * q + 4].reshape(HPC, 1))
    bdt = np.asarray(inputs["L0_attn_b_dt"], np.float32).reshape(144, 1)
    m["b_dt"] = np.ascontiguousarray(bdt if q == 0 else np.zeros_like(bdt))
    m["gamma"] = np.ascontiguousarray(
        np.asarray(inputs["L0_attn_ln_gamma"], np.float32).reshape(HN, HD)[4 * q:4 * q + 4].T)
    m["beta"] = np.ascontiguousarray(
        np.asarray(inputs["L0_attn_ln_beta"], np.float32).reshape(HN, HD)[4 * q:4 * q + 4].T)
    m["w_outT"] = np.ascontiguousarray(
        np.asarray(inputs["L0_attn_W_out"], np.float32)[:, ch0:ch0 + CPC].T)
    m["we"] = np.ascontiguousarray(np.asarray(inputs["L0_ffn_We"], np.float32))
    m["be"] = np.ascontiguousarray(
        np.asarray(inputs["L0_ffn_be"], np.float32).reshape(1, D))
    m["wg"] = np.ascontiguousarray(np.asarray(inputs["L0_ffn_Wg"], np.float32))
    m["eye"] = np.eye(128, dtype=np.float32)
    qs = np.zeros((128, NK1), np.float32)
    qs[:, 2 * q] = 1.0
    qs[:, 2 * q + 1] = 1.0
    m["qsel"] = qs
    s16 = np.zeros((16, 4), np.float32)
    for hh in range(4):
        s16[4 * q + hh, hh] = 1.0
    m["sel16"] = s16
    st8 = np.zeros((8, 1), np.float32)
    st8[0:core, 0] = 1.0
    m["step8"] = st8
    return m


def kernel(**inputs):
    nc = build()
    in_maps = [host_inputs(c, inputs) for c in range(8)]
    res = run_bass_kernel_spmd(nc, in_maps, core_ids=list(range(8)))
    out = np.empty((B, T, D), np.float32)
    for c in range(8):
        b, q = c // 4, c % 4
        out[b, q * QT:(q + 1) * QT] = np.asarray(res.results[c]["out_c"])
    return out


# version 15
# speedup vs baseline: 1.0044x; 1.0030x over previous
import numpy as np
import concourse.bass as bass
import concourse.mybir as mybir
from concourse.bass_utils import run_bass_kernel_spmd

F32 = mybir.dt.float32
AF = mybir.ActivationFunctionType
OP = mybir.AluOpType
AX = mybir.AxisListType

B, T, D = 2, 8192, 1024
DI, HN, HD, DS = 2048, 16, 128, 64
HPC, CPC = 4, 512
QT, CAP, E = 2048, 2048, 8
TC1, HALO = 1024, 128
NK1 = T // TC1
NT1 = (TC1 + HALO) // 128
TW = TC1 + HALO
TC2 = 1024
NK2 = T // TC2
NBLK = 2
NNB = DS // NBLK
NST = 4
NDA = 3
NWB = 3
EPS = 1e-5

_DEBUG = False
_PHASES = 4


class Sched:
    def __init__(self):
        self.steps = []
        self.cnt = {}
        self.own = {}
        self.order_sems = {}
        self.live_cnt = {}

    def tok(self, sem):
        k = id(sem)
        return (sem, self.cnt.get(k, 0), self.own.get(k))

    def step(self, engine, fn, waits=(), inc=None, amt=1, dma=False, drain=False,
             chain=True):
        w = []
        for t in waits:
            if t is None:
                continue
            sem, val, owner = t
            if val <= 0:
                continue
            w.append((sem, val, owner))
        tokv = None
        post = None
        if inc is not None:
            k = id(inc)
            self.cnt[k] = self.cnt.get(k, 0) + amt
            self.own[k] = None if dma else engine
            tokv = (inc, self.cnt[k], self.own[k])
            post = self.cnt[k]
        self.steps.append((engine, fn, w, inc, amt, chain, post))
        return tokv

    def dma(self, engine, fn, inc, waits=()):
        return self.step(engine, fn, waits=waits, inc=inc, amt=16, dma=True)

    def replay(self, engine, eng):
        hi = {}
        osem = self.order_sems.get(engine)
        ocnt = 0
        prev_upd = None
        for (e, fn, waits, inc, amt, chain, post) in self.steps:
            if e != engine:
                continue
            if osem is not None and chain and prev_upd is not None:
                psem, pval = prev_upd
                k = id(psem)
                if hi.get(k, 0) < pval:
                    hi[k] = pval
                    eng.wait_ge(psem, pval)
            bysem = {}
            for (sem, val, owner) in waits:
                k = id(sem)
                if val > bysem.get(k, (None, 0))[1]:
                    bysem[k] = (sem, val)
            for k, (sem, val) in bysem.items():
                if hi.get(k, 0) >= val:
                    continue
                hi[k] = val
                eng.wait_ge(sem, val)
            inst = fn(eng)
            if inc is not None:
                assert inst is not None, "inc step returned no instruction"
                inst.then_inc(inc, amt)
                prev_upd = (inc, post)
            elif osem is not None:
                assert inst is not None, f"{engine} step returned no instruction"
                ocnt += 1
                inst.then_inc(osem, 1)
                prev_upd = (osem, ocnt)


def build():
    nc = bass.Bass("TRN2", target_bir_lowering=False)
    dbg = "ExternalOutput" if _DEBUG else "Internal"
    di = lambda n, s: nc.dram_tensor(n, s, F32, kind="ExternalInput").ap()
    dd = lambda n, s, k="Internal": nc.dram_tensor(n, s, F32, kind=k).ap()

    x_b = di("x_b", [T, D])
    x_q = di("x_q", [QT, D])
    w_inT = di("w_inT", [D, 2 * CPC])
    w_dtT = di("w_dtT", [CPC, 144])
    w_conv = di("w_conv", [128, 16])
    b_conv = di("b_conv", [128, 4])
    a_log = di("a_log", [HD, HPC * DS])
    d_col = di("d_col", [HD, HPC])
    dtbias = di("dtbias", [HPC, 1])
    b_dt = di("b_dt", [144, 1])
    gamma = di("gamma", [HD, HPC])
    beta = di("beta", [HD, HPC])
    w_outT = di("w_outT", [CPC, D])
    we = di("we", [D, D])
    be = di("be", [1, D])
    wg = di("wg", [D, E])
    eye = di("eye", [128, 128])
    qsel = di("qsel", [128, NK1])
    sel16 = di("sel16", [16, 4])
    step8 = di("step8", [8, 1])

    out_c = nc.dram_tensor("out_c", [QT, D], F32, kind="ExternalOutput").ap()
    x4_hbm = dd("x4_hbm", [CPC, T], dbg)
    z_hbm = dd("z_hbm", [CPC, T], dbg)
    y_hbm = dd("y_hbm", [CPC, T], dbg)
    xp_in = dd("xp_in", [144, T])
    xp_out = dd("xp_out", [144, T])
    dt_dram = dd("dt_dram", [HPC, T], dbg)
    bb_dram = dd("bb_dram", [DS, T], dbg)
    cc_dram = dd("cc_dram", [DS, T], dbg)
    sta_in = dd("sta_in", [2, T])
    sta_out = dd("sta_out", [2, T])
    mu_dram = dd("mu_dram", [1, T], dbg)
    rs_dram = dd("rs_dram", [1, T], dbg)
    loc_dram = dd("loc_dram", [E, QT], dbg)
    base_dram = dd("base_dram", [1, E], dbg)
    tot_in = dd("tot_in", [1, E])
    tot_out = dd("tot_out", [8, E])
    s_in = dd("s_in", [T, D])
    s_out = dd("s_out", [QT, D])

    ctx = []
    def sb(name, shape):
        cm = nc.sbuf_tensor(name, shape, F32)
        t = cm.__enter__(); ctx.append(cm); return t
    def pst(name, shape):
        cm = nc.psum_tensor(name, shape, F32)
        t = cm.__enter__(); ctx.append(cm); return t
    def mksem(name):
        cm = nc.semaphore(name)
        s = cm.__enter__(); ctx.append(cm); return s

    t_eye = sb("t_eye", [128, 128])
    t_ones = sb("t_ones", [128, 8])
    t_wdtT = sb("t_wdtT", [128, 4 * 144])
    t_wcv = sb("t_wcv", [128, 16])
    t_bcv = sb("t_bcv", [128, 4])
    t_a = sb("t_a", [128, HPC * DS])
    t_d = sb("t_d", [128, HPC])
    t_gb = sb("t_gb", [128, 2 * HPC])
    t_dtb = sb("t_dtb", [HPC, 1])
    t_bdt = sb("t_bdt", [128, 2])
    t_wg = sb("t_wg", [128, 8 * E])
    t_qsel = sb("t_qsel", [128, NK1])
    t_sel16 = sb("t_sel16", [16, 4])
    t_step8 = sb("t_step8", [8, 1])
    t_carry = sb("t_carry", [128, HPC * DS])
    t_mask = sb("t_mask", [128, 16 * E])
    t_gate = sb("t_gate", [128, 16])
    t_scale = sb("t_scale", [128, 16])
    t_maskT = sb("t_maskT", [8, QT])
    t_loc = sb("t_loc", [8, QT])
    t_tot = sb("t_tot", [8, 1])
    t_totL = sb("t_totL", [8, 8])
    t_small = sb("t_small", [128, 64])
    t_eps = sb("t_eps", [128, 1])
    AW = 43008
    arena = sb("arena", [128, AW])
    A = lambda off, w: arena[:, off:off + w]
    psb = [pst(f"psb{i}", [128, 512]) for i in range(8)]

    S = Sched()
    S.order_sems = {}
    g = {}
    for n in ("s_ld", "s_x0", "s_x1", "s_stat", "s_ev", "s_mm", "s_rtr",
              "s_cc", "s_pre", "s_p2k", "s_bc0", "s_bc1", "s_dA", "s_scan",
              "s_w", "s_ymm", "s_yev", "s_out", "s_norm", "s_tp",
              "s_spx4", "s_spz", "s_spxp", "s_sprt", "s_spd1", "s_spy",
              "s_spst", "s_sps0", "s_sps1", "s_spmu", "s_pay", "s_pby",
              "s_pbz", "s_pbm", "s_p4w", "s_p4b", "s_p4t", "s_p4bs",
              "s_p4bd", "s_xq0", "s_xq1", "s_p4s", "s_p4x", "s_dbg", "s_pam",
              "s_evs", "s_spy1", "s_u", "s_cy", "s_x4g", "s_dtbs",
              "s_pby1", "s_pbz1", "s_pbm1", "s_xq2", "s_xq3",
              "s_out1", "s_p4s1", "s_p4x1", "s_pay1", "s_wek", "s_wek1",
              "s_x4h0", "s_x4h1", "s_x4h2", "s_x4h3",
              "s_dth0", "s_dth1", "s_dth2", "s_dth3"):
        g[n] = mksem(n)

    S.order_sems = {"vector": mksem("ord_v"), "scalar": mksem("ord_s")}
    bank_free = [None] * 8
    _dbg_t = {}

    def mm_group(bank, mms, evac_fn, evac_engine, extra_mm_waits=(),
                 extra_evac_waits=(), evac_drain=False, evac_chain=False):
        mm_tok = None
        for i, fnm in enumerate(mms):
            w = list(extra_mm_waits) if i == 0 else []
            if i == 0 and bank_free[bank] is not None:
                w.append(bank_free[bank])
            mm_tok = S.step("tensor", fnm, waits=w,
                            inc=(g["s_mm"] if i == len(mms) - 1 else None))
        evt = S.step(evac_engine, evac_fn,
                     waits=[mm_tok] + list(extra_evac_waits), inc=g["s_ev"],
                     chain=evac_chain)
        bank_free[bank] = evt
        return evt, mm_tok

    o_xn = 0
    o_xz = o_xn + 8 * TW
    o_zb = o_xz + 4 * TW
    o_cv = o_zb + TW
    o_scr = o_cv + TC1
    o_xld = o_scr + 1024
    o_xp = o_xld + 2 * 1024
    o_xnorm = o_xp + 2 * TC1
    o_scr2 = o_xnorm + 2 * 1024
    o_win = o_scr2 + 1024
    o_end1 = o_win + 8 * 1024
    assert o_end1 <= AW, o_end1
    o_dtb, o_dtx, o_x4 = 0, 4 * TC2, 8 * TC2
    o_bb = 12 * TC2
    o_cb = o_bb + 2 * NBLK * TC2
    o_dA = o_cb + 2 * NBLK * TC2
    o_u = o_dA + NDA * TC2
    o_st2 = o_u + 4 * TC2
    o_w = o_st2 + NST * TC2
    o_dx4 = o_w + NWB * 2 * TC2
    o_yev = o_dx4 + TC2
    o_end2 = o_yev + 2 * TC2
    assert o_end2 <= AW, o_end2
    o_xpl = o_dA
    o_dto = o_st2
    o3_y, o3_z, o3_g = 0, 4 * TC2, 8 * TC2
    o3_mu, o3_rs = 12 * TC2, 13 * TC2
    o3_wo = 14 * TC2
    o3_s = o3_wo + 4 * 1024
    o3_sr = o3_s + 2 * 1024
    o3_end = o3_sr + T + 2048
    assert o3_end <= AW, o3_end
    o3_y2 = o3_end
    o3_z2 = o3_y2 + 4 * TC2
    o3_mu2 = o3_z2 + 4 * TC2
    o3_rs2 = o3_mu2 + TC2
    assert o3_rs2 + TC2 <= AW
    o4_xnT = 0
    o4_eo = o4_xnT + 8 * QT
    o4_we = o4_eo + 4 * 1024
    o4_s = o4_we + 8 * 1024
    o4_x = o4_s + 2 * 1024
    o4_be = o4_x + 2 * 1024
    o4_bs = o4_be + 1024
    o4_lt = o4_bs + 8
    o4_xq = o4_lt + 16 * E
    o4_g2 = o4_xq + 4 * 1024
    o4_scr = o4_g2 + 2 * 1024
    o4_end = o4_scr + 2 * 1024
    assert o4_end <= AW, o4_end

    for fn in (
        lambda e: e.dma_start(out=t_eye[:], in_=eye),
        lambda e: e.dma_start(out=A(o_win, 8 * 1024).rearrange("p (k c) -> p k c", k=8), in_=w_inT.rearrange("(k p) c -> p k c", p=128)),
        lambda e: e.dma_start(out=t_wdtT[:].rearrange("p (k c) -> p k c", k=4), in_=w_dtT.rearrange("(k p) c -> p k c", p=128)),
        lambda e: e.dma_start(out=t_wcv[:], in_=w_conv),
        lambda e: e.dma_start(out=t_bcv[:], in_=b_conv),
        lambda e: e.dma_start(out=t_a[:], in_=a_log),
        lambda e: e.dma_start(out=t_d[:], in_=d_col),
        lambda e: e.dma_start(out=t_gb[:, 0:HPC], in_=gamma),
        lambda e: e.dma_start(out=t_gb[:, HPC:2 * HPC], in_=beta),
        lambda e: e.dma_start(out=t_dtb[:], in_=dtbias),
        lambda e: e.dma_start(out=t_bdt[:, 0:1], in_=b_dt[0:128, :]),
        lambda e: e.dma_start(out=t_bdt[0:16, 1:2], in_=b_dt[128:144, :]),
        lambda e: e.dma_start(out=t_wg[:].rearrange("p (k c) -> p k c", k=8), in_=wg.rearrange("(k p) c -> p k c", p=128)),
        lambda e: e.dma_start(out=t_qsel[:], in_=qsel),
        lambda e: e.dma_start(out=t_sel16[:], in_=sel16),
        lambda e: e.dma_start(out=t_step8[:], in_=step8),
    ):
        S.dma("sync", fn, g["s_ld"])
    all_ld = S.tok(g["s_ld"])

    t_a2 = A(o_yev, HPC * DS)
    ta = S.step("scalar", lambda e: e.activation(t_a2, t_a[:], AF.Exp),
                waits=[all_ld], inc=g["s_pre"])
    a_ready = S.step("scalar", lambda e: e.mul(t_a[:], t_a2, -1.0),
                     waits=[ta], inc=g["s_pre"])
    S.step("vector", lambda e: e.memset(t_ones[:], 1.0))
    S.step("vector", lambda e: e.memset(t_eps[:], EPS))
    S.step("vector", lambda e: e.memset(t_carry[:], 0.0))
    S.step("vector", lambda e: e.memset(t_mask[:], 0.0))
    S.step("vector", lambda e: e.memset(t_gate[:], 0.0))

    def ln_tile(tl_tok, xb, scr, scrB, xnd, dbase, dstride, jcol, tpbanks,
                stats_wait, first_pe_wait=None, xnd_wait=None):
        t1 = S.step("scalar", lambda e, xb=xb, scr=scr: e.activation(
            A(scr, 1024), A(xb, 1024), AF.Copy, accum_out=t_small[:, 0:1]),
            waits=[tl_tok, stats_wait], inc=g["s_stat"], chain=False)
        t2 = S.step("scalar", lambda e, xb=xb, scrB=scrB: e.activation(
            A(scrB, 1024), A(xb, 1024), AF.Square, accum_out=t_small[:, 1:2]),
            waits=[t1], inc=g["s_stat"], chain=False)
        S.step("vector", lambda e: e.tensor_scalar_mul(
            t_small[:, 2:3], t_small[:, 0:1], 1.0 / D), waits=[t2], drain=True)
        S.step("vector", lambda e: e.tensor_scalar_mul(
            t_small[:, 3:4], t_small[:, 1:2], 1.0 / D), drain=True)
        S.step("vector", lambda e: e.tensor_tensor(
            t_small[:, 4:5], t_small[:, 2:3], t_small[:, 2:3], OP.mult),
            drain=True)
        t3 = S.step("vector", lambda e: e.tensor_tensor(
            t_small[:, 5:6], t_small[:, 3:4], t_small[:, 4:5], OP.subtract),
            inc=g["s_norm"], drain=True)
        t4 = S.step("scalar", lambda e: e.activation(
            t_small[:, 6:7], t_small[:, 5:6], AF.Sqrt, bias=t_eps[:, 0:1]),
            waits=[t3], inc=g["s_stat"], chain=False)
        S.step("vector", lambda e: e.reciprocal(
            t_small[:, 7:8], t_small[:, 6:7]), waits=[t4], drain=True)
        tn = S.step("vector", lambda e, xb=xb, xnd=xnd: e.tensor_scalar(
            A(xnd, 1024), A(xb, 1024), t_small[:, 2:3], t_small[:, 7:8],
            OP.subtract, OP.mult), waits=[xnd_wait], inc=g["s_norm"])
        last_pe, last_ev = None, None
        for kk in range(8):
            bank = tpbanks[kk % 2]
            wv = [tn]
            if bank_free[bank] is not None:
                wv.append(bank_free[bank])
            if kk == 0 and first_pe_wait is not None:
                wv.append(first_pe_wait)
            last_pe = S.step("tensor", lambda e, xnd=xnd, kk=kk, bank=bank: e.transpose(
                psb[bank][:, 0:128], A(xnd + kk * 128, 128), t_eye[:]),
                waits=wv, inc=g["s_tp"])
            last_ev = S.step("scalar", lambda e, kk=kk, bank=bank,
                             dbase=dbase, dstride=dstride, jcol=jcol: e.copy(
                arena[:, dbase + kk * dstride + jcol:dbase + kk * dstride + jcol + 128],
                psb[bank][:, 0:128]), waits=[last_pe], inc=g["s_evs"],
                chain=False)
            bank_free[bank] = last_ev
        return tn, last_pe, last_ev

    tp_pe, chunk_pe_done, tn_toks = {}, {}, {}
    x4_sp, xp_sp, z_sp = {}, {}, {}
    conv_done = {}
    dt_pe_last = {}
    z_sp_prev = None
    prev_tn = None
    NCHN = [(0, 512), (512, 512), (1024, 128)]

    for k in range(NK1):
        evs = []
        for j in range(NT1):
            tok0 = k * TC1 - HALO + j * 128
            xb = o_xld + (j % 2) * 1024
            if tok0 < 0:
                tz = None
                for kk in range(8):
                    tz = S.step("vector", lambda e, kk=kk, j=j: e.memset(
                        arena[:, o_xn + kk * TW + j * 128:o_xn + kk * TW + j * 128 + 128], 0.0),
                        inc=g["s_norm"])
                evs.append(tz)
                tp_pe[(k, j)] = None
                continue
            w = []
            if j >= 2:
                w.append(tn_toks.get((k, j - 2)))
            elif k > 0:
                w.append(tn_toks.get((k - 1, NT1 - 1 - (1 - j))))
            if j >= 2:
                w.append(tp_pe.get((k, j - 2)))
            elif k > 0:
                w.append(tp_pe.get((k - 1, NT1 - 1 - (1 - j))))
            tl = S.dma("sync", lambda e, xb=xb, tok0=tok0: e.dma_start(
                out=A(xb, 1024), in_=x_b[tok0:tok0 + 128, :]),
                g["s_x0" if j % 2 == 0 else "s_x1"], waits=w)
            fpw = chunk_pe_done.get(k - 1) if (k > 0 and j == 0) else None
            tn, lpe, lev = ln_tile(tl, xb, o_scr, o_scr2,
                                   o_xnorm + (j % 2) * 1024,
                                   o_xn, TW, j * 128, (6, 7),
                                   prev_tn, first_pe_wait=fpw)
            prev_tn = tn
            tn_toks[(k, j)] = tn
            tp_pe[(k, j)] = lpe
            evs.append(lev)

        pe_last = None
        for m in range(8):
            z_ev_group = []
            for (ni, (noff, nw)) in enumerate(NCHN):
                bank = ni
                mms = []
                for kd in range(8):
                    def fmms(e, m=m, kd=kd, noff=noff, nw=nw, bank=bank):
                        return e.matmul(
                            psb[bank][:, 0:nw],
                            arena[:, o_win + kd * 1024 + m * 128:o_win + kd * 1024 + (m + 1) * 128],
                            arena[:, o_xn + kd * TW + noff:o_xn + kd * TW + noff + nw],
                            start=(kd == 0), stop=(kd == 7))
                    mms.append(fmms)
                extra_ev = []
                if m < 4:
                    def fev(e, m=m, noff=noff, nw=nw, bank=bank):
                        return e.tensor_copy(
                            arena[:, o_xz + m * TW + noff:o_xz + m * TW + noff + nw],
                            psb[bank][:, 0:nw])
                    if k > 0:
                        extra_ev = [dt_pe_last.get(k - 1), x4_sp.get((k - 1, 3)),
                                    conv_done.get((k - 1, m))]
                else:
                    def fev(e, noff=noff, nw=nw, bank=bank):
                        return e.tensor_copy(
                            arena[:, o_zb + noff:o_zb + noff + nw],
                            psb[bank][:, 0:nw])
                    extra_ev = [z_sp_prev]
                ev, mmt = mm_group(bank, mms, fev, "vector",
                                   extra_mm_waits=evs + [all_ld],
                                   extra_evac_waits=extra_ev)
                pe_last = mmt
                if m >= 4:
                    z_ev_group.append(ev)
            if m >= 4:
                z_sp_prev = S.dma("sync", lambda e, k=k, m=m: e.dma_start(
                    out=z_hbm[(m - 4) * 128:(m - 3) * 128, k * TC1:(k + 1) * TC1],
                    in_=A(o_zb + HALO, TC1)), g["s_spz"], waits=z_ev_group)
                z_sp[(k, m - 4)] = z_sp_prev

        for ct in range(4):
            xzoff = o_xz + ct * TW
            cw = [conv_done.get((k, ct - 1)) if ct > 0 else conv_done.get((k - 1, 3))]
            t1 = S.step("vector", lambda e, xzoff=xzoff, ct=ct: e.tensor_scalar(
                A(o_cv, TC1), arena[:, xzoff + HALO - 3:xzoff + HALO - 3 + TC1],
                t_wcv[:, ct * 4:ct * 4 + 1], t_bcv[:, ct:ct + 1],
                OP.mult, OP.add), waits=cw)
            for tap in range(1, 4):
                t1 = S.step("vector", lambda e, xzoff=xzoff, ct=ct, tap=tap: e.scalar_tensor_tensor(
                    A(o_cv, TC1),
                    arena[:, xzoff + HALO - 3 + tap:xzoff + HALO - 3 + tap + TC1],
                    t_wcv[:, ct * 4 + tap:ct * 4 + tap + 1],
                    A(o_cv, TC1), OP.mult, OP.add),
                    inc=(g["s_norm"] if tap == 3 else None))
            tsl = S.step("scalar", lambda e, xzoff=xzoff: e.activation(
                arena[:, xzoff + HALO:xzoff + HALO + TC1], A(o_cv, TC1), AF.Silu),
                waits=[t1], inc=g["s_stat"])
            conv_done[(k, ct)] = tsl
            x4_sp[(k, ct)] = S.dma("sync", lambda e, k=k, ct=ct, xzoff=xzoff: e.dma_start(
                out=x4_hbm[ct * 128:(ct + 1) * 128, k * TC1:(k + 1) * TC1],
                in_=arena[:, xzoff + HALO:xzoff + HALO + TC1]), g["s_spx4"],
                waits=[tsl])

        dt_ev = []
        for (mi, (p0, pw)) in enumerate([(0, 128), (128, 16)]):
            for (ni, noff) in enumerate(range(0, TC1, 512)):
                bank = 5
                mms = []
                for kd in range(4):
                    def fdt(e, mi=mi, p0=p0, pw=pw, kd=kd, noff=noff, bank=bank):
                        return e.matmul(
                            psb[bank][0:pw, 0:512],
                            t_wdtT[:, kd * 144 + p0:kd * 144 + p0 + pw],
                            arena[:, o_xz + kd * TW + HALO + noff:
                                  o_xz + kd * TW + HALO + noff + 512],
                            start=(kd == 0), stop=(kd == 3))
                    mms.append(fdt)
                def fdtev(e, mi=mi, pw=pw, noff=noff, bank=bank):
                    col = t_bdt[0:128, 0:1] if mi == 0 else t_bdt[0:16, 1:2]
                    return e.tensor_scalar(
                        arena[0:pw, o_xp + mi * TC1 + noff:o_xp + mi * TC1 + noff + 512],
                        psb[bank][0:pw, 0:512], col, None, OP.add)
                exw = [xp_sp.get((k - 1, 1))] if k > 0 else []
                ev, mmt = mm_group(bank, mms, fdtev, "vector",
                                   extra_mm_waits=[conv_done[(k, c2)] for c2 in range(4)],
                                   extra_evac_waits=exw)
                dt_ev.append(ev)
                dt_pe_last[k] = mmt
        xp_sp[(k, 0)] = S.dma("sync", lambda e, k=k: e.dma_start(
            out=xp_in[0:128, k * TC1:(k + 1) * TC1], in_=A(o_xp, TC1)),
            g["s_spxp"], waits=dt_ev)
        xp_sp[(k, 1)] = S.dma("sync", lambda e, k=k: e.dma_start(
            out=xp_in[128:144, k * TC1:(k + 1) * TC1],
            in_=arena[0:16, o_xp + TC1:o_xp + 2 * TC1]), g["s_spxp"], waits=dt_ev)

        rt_prev_ev = None
        rpe = pe_last
        for jt in range(TC1 // 128):
            m = (k % 2) * 8 + jt
            bank = 5
            mms = []
            for kd in range(8):
                def frt(e, jt=jt, kd=kd, bank=bank):
                    return e.matmul(
                        psb[bank][:, 0:E],
                        arena[:, o_xn + kd * TW + HALO + jt * 128:
                              o_xn + kd * TW + HALO + (jt + 1) * 128],
                        t_wg[:, kd * E:(kd + 1) * E],
                        start=(kd == 0), stop=(kd == 7))
                mms.append(frt)
            def frtev(e):
                return e.tensor_copy(t_small[:, 8:16], psb[5][:, 0:E])
            ev, mmt = mm_group(bank, mms, frtev, "vector",
                               extra_mm_waits=evs + [all_ld],
                               extra_evac_waits=[rt_prev_ev], evac_chain=True)
            rpe = mmt
            if _DEBUG and k == 0 and jt == 0:
                if "dbg_lg" not in _dbg_t:
                    _dbg_t["dbg_lg"] = dd("dbg_lg", [128, 64], "ExternalOutput")
                S.dma("sync", lambda e: e.dma_start(
                    out=_dbg_t["dbg_lg"][:, 0:8], in_=t_small[:, 8:16]),
                    g["s_dbg"], waits=[ev])
            S.step("vector", lambda e: e.reduce_max(
                t_small[:, 16:17], t_small[:, 8:16], axis=AX.X), drain=True)
            t1 = S.step("vector", lambda e: e.tensor_scalar_mul(
                t_small[:, 17:18], t_small[:, 16:17], -1.0), inc=g["s_norm"],
                drain=True)
            t2 = S.step("scalar", lambda e: e.activation(
                t_small[:, 24:32], t_small[:, 8:16], AF.Exp,
                bias=t_small[:, 17:18]), waits=[t1], inc=g["s_stat"])
            t3 = S.step("vector", lambda e: e.reduce_sum(
                t_small[:, 18:19], t_small[:, 24:32], axis=AX.X), waits=[t2],
                drain=True)
            S.step("vector", lambda e: e.reciprocal(
                t_small[:, 19:20], t_small[:, 18:19]), drain=True)
            S.step("vector", lambda e, m=m, k=k: e.scalar_tensor_tensor(
                t_gate[:, m:m + 1], t_small[:, 19:20], t_qsel[:, k:k + 1],
                t_gate[:, m:m + 1], OP.mult, OP.add), drain=True)
            S.step("vector", lambda e: e.tensor_scalar(
                t_small[:, 32:40], t_small[:, 8:16], t_small[:, 16:17], None,
                OP.is_equal), drain=True)
            S.step("vector", lambda e: e.tensor_tensor_scan(
                t_small[:, 40:48], t_ones[:, 0:8], t_small[:, 32:40], 0.0,
                OP.mult, OP.add), drain=True)
            S.step("vector", lambda e: e.tensor_scalar(
                t_small[:, 56:64], t_small[:, 40:48], 1.0, None, OP.is_equal),
                drain=True)
            S.step("vector", lambda e: e.tensor_tensor(
                t_small[:, 48:56], t_small[:, 32:40], t_small[:, 56:64],
                OP.mult), drain=True)
            rt_prev_ev = S.step("vector", lambda e, m=m, k=k: e.scalar_tensor_tensor(
                t_mask[:, m * E:(m + 1) * E], t_small[:, 48:56],
                t_qsel[:, k:k + 1], t_mask[:, m * E:(m + 1) * E],
                OP.mult, OP.add), inc=g["s_norm"], drain=True)
        chunk_pe_done[k] = rpe

    mlast = S.tok(g["s_norm"])
    mt_prev = None
    for m in range(16):
        wv = [mlast]
        if bank_free[6] is not None:
            wv.append(bank_free[6])
        pt = S.step("tensor", lambda e, m=m: e.transpose(
            psb[6][0:E, 0:128], t_mask[:, m * E:(m + 1) * E], t_eye[:]),
            waits=wv, inc=g["s_tp"])
        mt_prev = S.step("vector", lambda e, m=m: e.tensor_copy(
            t_maskT[:, m * 128:(m + 1) * 128], psb[6][0:E, 0:128]),
            waits=[pt], inc=g["s_ev"], drain=True)
        bank_free[6] = mt_prev
    if _DEBUG:
        dbg_mask = dd("dbg_mask", [128, 16 * E], "ExternalOutput")
        dbg_maskT = dd("dbg_maskT", [8, QT], "ExternalOutput")
        dbg_gate = dd("dbg_gate", [128, 16], "ExternalOutput")
        S.dma("sync", lambda e: e.dma_start(out=dbg_mask, in_=t_mask[:]),
              g["s_dbg"], waits=[mlast, mt_prev])
        S.dma("sync", lambda e: e.dma_start(out=dbg_maskT, in_=t_maskT[:, :]),
              g["s_dbg"], waits=[mt_prev])
        S.dma("sync", lambda e: e.dma_start(out=dbg_gate, in_=t_gate[:]),
              g["s_dbg"], waits=[mlast])
    S.step("vector", lambda e: e.tensor_tensor_scan(
        t_loc[:, :], t_maskT[:, :], t_maskT[:, :], 0.0, OP.add, OP.bypass),
        waits=[mt_prev], drain=True)
    ttot = S.step("vector", lambda e: e.tensor_copy(
        t_tot[:, 0:1], t_loc[:, QT - 1:QT]), inc=g["s_rtr"], drain=True)
    S.dma("sync", lambda e: e.dma_start(out=loc_dram, in_=t_loc[:, :]),
          g["s_sprt"], waits=[ttot])
    S.dma("sync", lambda e: e.dma_start(
        out=tot_in.rearrange("one e -> e one"), in_=t_tot[:, 0:1],
        allow_slow_non_contiguous=True), g["s_sprt"], waits=[ttot])
    ld1 = ld2 = S.tok(g["s_sprt"])

    if _PHASES < 2:
        return _finish(nc, S, ctx)

    grp_quad = [[0, 1, 2, 3], [4, 5, 6, 7]]
    grp_all = [[0, 1, 2, 3, 4, 5, 6, 7]]
    xp_ready = [xp_sp[(k, i)] for k in range(NK1) for i in range(2)]
    ar1 = S.step("gpsimd", lambda e: e.collective_compute(
        "AllReduce", OP.add, ins=[xp_in], outs=[xp_out],
        replica_groups=grp_quad), waits=xp_ready, inc=g["s_cc"])
    if _DEBUG:
        dbg_xpo = dd("dbg_xpo", [144, T], "ExternalOutput")
        S.dma("sync", lambda e: e.dma_start(out=dbg_xpo, in_=xp_out),
              g["s_dbg"], waits=[ar1])
    ag1 = S.step("gpsimd", lambda e: e.collective_compute(
        "AllGather", OP.bypass, ins=[tot_in], outs=[tot_out],
        replica_groups=grp_all), waits=[ld2], inc=g["s_cc"])

    stage_toks = []
    prev_sp = []
    prev_e2 = None
    for kk in range(4):
        c0 = kk * 2048
        wv = [ar1] + prev_sp
        l1 = S.dma("sync", lambda e, c0=c0: e.dma_start(
            out=arena[0:16, o_xpl:o_xpl + 2048], in_=xp_out[0:16, c0:c0 + 2048]),
            g["s_p2k"], waits=wv)
        selev = []
        for (ni, noff) in enumerate(range(0, 2048, 512)):
            def fsel(e, noff=noff):
                return e.matmul(
                    psb[7][0:4, 0:512], t_sel16[:, :],
                    arena[0:16, o_xpl + noff:o_xpl + noff + 512],
                    start=True, stop=True)
            def fselev(e, noff=noff):
                return e.tensor_copy(
                    arena[0:4, o_dto + noff:o_dto + noff + 512],
                    psb[7][0:4, 0:512])
            ev, _ = mm_group(7, [fsel], fselev, "vector",
                             extra_mm_waits=[l1, all_ld],
                             extra_evac_waits=[prev_e2])
            selev.append(ev)
        e1 = S.step("scalar", lambda e: e.activation(
            arena[0:4, o_dto:o_dto + 2048], arena[0:4, o_dto:o_dto + 2048],
            AF.Exp, bias=t_dtb[:, 0:1]), waits=selev, inc=g["s_stat"])
        prev_e2 = S.step("scalar", lambda e: e.activation(
            arena[0:4, o_dto:o_dto + 2048], arena[0:4, o_dto:o_dto + 2048],
            AF.Ln, bias=1.0), inc=g["s_stat"])
        d1 = S.dma("sync", lambda e, c0=c0: e.dma_start(
            out=dt_dram[:, c0:c0 + 2048], in_=arena[0:4, o_dto:o_dto + 2048]),
            g["s_spd1"], waits=[prev_e2])
        prev_sp = [d1]
        stage_toks += [d1]
    stage_toks.append(ar1)

    dA_toks, scan_toks, w_toks, mm_toks = {}, {}, {}, {}
    dump_toks = {}
    x4r, dtbr, dtxr = {}, {}, {}
    yev_toks, ysp_toks = {}, {}

    def itix(k, nb, n, h):
        return ((k * NNB + nb) * NBLK + n) * 4 + h

    I_LAST = NK2 * NNB * NBLK * 4 - 1

    x4_all = S.tok(g["s_spx4"])
    z_all0 = S.tok(g["s_spz"])
    x_all0 = [S.tok(g["s_x0"]), S.tok(g["s_x1"])]

    NIT = NNB * NBLK * 4
    bc_tok = {}
    wpair = {}
    u_tok, cy_tok = {}, {}

    def gi_parts(gi):
        k, r = divmod(gi, NIT)
        nb, r2 = divmod(r, 4 * NBLK)
        h, n = divmod(r2, NBLK)
        return k, nb, n, h

    def emit_bc(k, nb):
        par = nb % 2
        gnb = k * NNB + nb
        if gnb < 2:
            wv = list(stage_toks) + [z_all0, x4_all] + x_all0
        else:
            pgi = (gnb - 2) * 4 * NBLK + 4 * NBLK - 1
            wv = [u_tok[pgi], wpair[pgi]]
        bcs = g["s_bc0" if par == 0 else "s_bc1"]
        S.dma("sync", lambda e, k=k, nb=nb, par=par: e.dma_start(
            out=A(o_bb + par * NBLK * TC2, NBLK * TC2).rearrange(
                "p (n t) -> p n t", n=NBLK),
            in_=xp_out[16 + nb * NBLK:16 + (nb + 1) * NBLK,
                       k * TC2:(k + 1) * TC2][None, :, :]
            .broadcast_to([128, NBLK, TC2])), bcs, waits=wv)
        bc_tok[gnb] = S.dma("sync", lambda e, k=k, nb=nb, par=par: e.dma_start(
            out=A(o_cb + par * NBLK * TC2, NBLK * TC2).rearrange(
                "p (n t) -> p n t", n=NBLK),
            in_=xp_out[80 + nb * NBLK:80 + (nb + 1) * NBLK,
                       k * TC2:(k + 1) * TC2][None, :, :]
            .broadcast_to([128, NBLK, TC2])), bcs, waits=wv)

    def emit_u(gi):
        k, nb, n, h = gi_parts(gi)
        par = nb % 2
        u_tok[gi] = S.step("vector", lambda e, h=h, n=n, par=par, gi=gi: e.tensor_tensor(
            A(o_u + (gi % 4) * TC2, TC2), A(o_dtx + h * TC2, TC2),
            A(o_bb + par * NBLK * TC2 + n * TC2, TC2), OP.mult),
            waits=[bc_tok[k * NNB + nb], dtxr[(k, 3)],
                   scan_toks.get(gi - 4)], inc=g["s_u"], chain=False)

    pend_carry = []
    for k in range(NK2):
        x4h, dth = {}, {}
        for h in range(4):
            wvx = list(stage_toks) + [x4_all]
            wvd = list(stage_toks) + [x4_all]
            if k > 0:
                wvx.append(yev_toks[(k - 1, h)])
                wvd.append(dA_toks[k * NIT - 1])
            S.dma("gpsimd", lambda e, k=k, h=h: e.dma_start(
                out=A(o_x4 + h * TC2, TC2),
                in_=x4_hbm[h * 128:(h + 1) * 128, k * TC2:(k + 1) * TC2]),
                g["s_x4h%d" % h], waits=wvx)
            x4h[h] = S.tok(g["s_x4h%d" % h])
            S.dma("scalar", lambda e, k=k, h=h: e.dma_start(
                out=A(o_dtb + h * TC2, TC2),
                in_=dt_dram[h:h + 1, k * TC2:(k + 1) * TC2].broadcast_to([128, TC2])),
                g["s_dth%d" % h], waits=wvd)
            dth[h] = S.tok(g["s_dth%d" % h])
        for h in range(4):
            dtxr[(k, h)] = S.step("vector", lambda e, h=h: e.tensor_tensor(
                A(o_dtx + h * TC2, TC2), A(o_dtb + h * TC2, TC2),
                A(o_x4 + h * TC2, TC2), OP.mult),
                waits=[x4h[h], dth[h]], inc=g["s_norm"], chain=False)
            dtbr[(k, h)] = dth[h]
        emit_bc(k, 0)
        emit_bc(k, 1)
        for (pc_h, pc_nn, pc_gi) in pend_carry:
            cy_tok[pc_gi] = S.step("vector", lambda e, pc_h=pc_h, pc_nn=pc_nn, pc_gi=pc_gi: e.tensor_copy(
                t_carry[:, pc_h * DS + pc_nn:pc_h * DS + pc_nn + 1],
                arena[:, o_st2 + (pc_gi % NST) * TC2 + TC2 - 1:
                      o_st2 + (pc_gi % NST) * TC2 + TC2]),
                waits=[scan_toks[pc_gi]], inc=g["s_cy"], chain=False)
        pend_carry = []
        emit_u(k * NIT + 0)
        emit_u(k * NIT + 1)
        emit_u(k * NIT + 2)
        for i in range(NIT):
            gi = k * NIT + i
            _, nb, n, h = gi_parts(gi)
            nn = nb * NBLK + n
            if i % (4 * NBLK) == 0 and nb >= 1 and nb + 1 < NNB:
                emit_bc(k, nb + 1)
            wv = [dth[h], a_ready]
            if gi >= NDA:
                wv.append(scan_toks.get(gi - NDA))
            dA_toks[gi] = S.step("scalar", lambda e, h=h, nn=nn, gi=gi: e.activation(
                A(o_dA + (gi % NDA) * TC2, TC2), A(o_dtb + h * TC2, TC2),
                AF.Exp, scale=t_a[:, h * DS + nn:h * DS + nn + 1]),
                waits=wv, inc=g["s_dA"], chain=False)
            wv = [dA_toks[gi], u_tok[gi], cy_tok.get(gi - NST)]
            if gi >= NST:
                wv.append(wpair.get((gi - NST) | 1))
            scan_toks[gi] = S.step("vector", lambda e, h=h, nn=nn, gi=gi: e.tensor_tensor_scan(
                A(o_st2 + (gi % NST) * TC2, TC2),
                A(o_dA + (gi % NDA) * TC2, TC2),
                A(o_u + (gi % 4) * TC2, TC2),
                t_carry[:, h * DS + nn:h * DS + nn + 1],
                OP.mult, OP.add), waits=wv, inc=g["s_scan"], chain=False)
            if i + 3 < NIT:
                emit_u(gi + 3)
            if i >= 1:
                _, pnb, pn, ph = gi_parts(gi - 1)
                pnn = pnb * NBLK + pn
                cy_tok[gi - 1] = S.step("vector", lambda e, ph=ph, pnn=pnn, gi=gi: e.tensor_copy(
                    t_carry[:, ph * DS + pnn:ph * DS + pnn + 1],
                    arena[:, o_st2 + ((gi - 1) % NST) * TC2 + TC2 - 1:
                          o_st2 + ((gi - 1) % NST) * TC2 + TC2]),
                    waits=[scan_toks[gi - 1]], inc=g["s_cy"], chain=False)
            if i == NIT - 1:
                pend_carry.append((h, nn, gi))
            if i % 2 == 1:
                pbase = gi - 1
                slot = pbase % NST
                wbuf = (pbase // 2) % NWB
                par = nb % 2
                wv = [scan_toks[gi], bc_tok[k * NNB + nb]]
                if pbase >= 2 * NWB:
                    wv.append(mm_toks.get((pbase - 2 * NWB) | 1))
                wpair[gi] = S.step("vector", lambda e, par=par, slot=slot, wbuf=wbuf: e.tensor_tensor(
                    A(o_w + wbuf * 2 * TC2, 2 * TC2),
                    A(o_st2 + slot * TC2, 2 * TC2),
                    A(o_cb + par * NBLK * TC2, 2 * TC2),
                    OP.mult), waits=wv, inc=g["s_w"], chain=False)
                first = (nb == 0 and n == 1)
                last = (nb == NNB - 1 and n == 1)
                for sub in range(4):
                    bank = 2 * h + (sub % 2)
                    wv = [wpair[gi], all_ld]
                    if first and sub < 2:
                        if k == 0:
                            wv.append(bank_free[bank])
                        else:
                            wv.append(yev_toks[(k - 1, h)])
                    mm_toks[gi] = S.step(
                        "tensor", lambda e, sub=sub, bank=bank, wbuf=wbuf,
                        first=first, last=last: e.matmul(
                            psb[bank][:, 0:512], t_eye[:],
                            arena[:, o_w + wbuf * 2 * TC2 + sub * 512:
                                  o_w + wbuf * 2 * TC2 + sub * 512 + 512],
                            start=(first and sub < 2), stop=(last and sub >= 2)),
                        waits=wv, inc=(g["s_ymm"] if sub == 3 else None))
        for h in range(4):
            jj = k * 4 + h
            tok_i = k * NIT + ((NNB - 1) * 4 + h) * NBLK + 1
            S.step("vector", lambda e, h=h: e.tensor_scalar(
                A(o_dx4, TC2), A(o_x4 + h * TC2, TC2), t_d[:, h:h + 1], None,
                OP.mult), waits=[mm_toks[tok_i]])
            wv = [ysp_toks.get(jj - 2)]
            S.step("vector", lambda e, h=h, jj=jj: e.tensor_tensor(
                A(o_yev + (jj % 2) * TC2, 512), psb[2 * h][:, 0:512],
                A(o_dx4, 512), OP.add), waits=wv)
            y2 = S.step("vector", lambda e, h=h, jj=jj: e.tensor_tensor(
                arena[:, o_yev + (jj % 2) * TC2 + 512:o_yev + (jj % 2) * TC2 + 1024],
                psb[2 * h + 1][:, 0:512],
                arena[:, o_dx4 + 512:o_dx4 + 1024], OP.add), inc=g["s_yev"])
            yev_toks[(k, h)] = y2
            ysp_toks[jj] = S.dma("sync", lambda e, h=h, k=k, jj=jj: e.dma_start(
                out=y_hbm[h * 128:(h + 1) * 128, k * TC2:(k + 1) * TC2],
                in_=A(o_yev + (jj % 2) * TC2, TC2)),
                g["s_spy" if jj % 2 == 0 else "s_spy1"], waits=[y2])

    if _PHASES < 3:
        return _finish(nc, S, ctx)

    p2_drain = [scan_toks[I_LAST], wpair[I_LAST]] + \
        [yev_toks[(NK2 - 1, hh)] for hh in range(4)]
    y_all_sp = S.tok(g["s_spy"])
    y_all_sp1 = S.tok(g["s_spy1"])

    st_ev = {}
    sp_done = {}
    for k in range(NK2):
        pay = o3_y if k % 2 == 0 else o3_y2
        pag = o3_g if k % 2 == 0 else o3_z
        spay = g["s_pay" if k % 2 == 0 else "s_pay1"]
        yld = {}
        for h in range(4):
            wv = [y_all_sp, y_all_sp1]
            if k < 2:
                wv += p2_drain
            else:
                wv.append(st_ev[(k - 2, 1, 1)])
            yld[h] = S.dma("sync", lambda e, k=k, h=h, pay=pay: e.dma_start(
                out=A(pay + h * TC2, TC2),
                in_=y_hbm[h * 128:(h + 1) * 128, k * TC2:(k + 1) * TC2]),
                spay, waits=wv)
        yall = S.tok(spay)
        yld = {h: yall for h in range(4)}
        sq = {}
        for h in range(4):
            sqw = [yld[h]]
            if k < 2:
                sqw += p2_drain
            else:
                sqw.append(st_ev[(k - 2, 1, 1)])
            sq[h] = S.step("scalar", lambda e, h=h, pay=pay, pag=pag: e.activation(
                A(pag + h * TC2, TC2), A(pay + h * TC2, TC2), AF.Square),
                waits=sqw, inc=g["s_stat"], chain=False)
        for (si, src_off) in ((0, pay), (1, pag)):
            for half in range(2):
                bank = 4 + si * 2 + half
                mms = []
                for h in range(4):
                    def fst(e, h=h, src_off=src_off, half=half, bank=bank):
                        return e.matmul(
                            psb[bank][0:1, 0:512], t_ones[:, 0:1],
                            A(src_off + h * TC2 + half * 512, 512),
                            start=(h == 0), stop=(h == 3))
                    mms.append(fst)
                def fstev(e, si=si, half=half, k=k, bank=bank):
                    off = o3_sr + si * 4096 + (k % 4) * 1024 + half * 512
                    return e.tensor_copy(arena[0:1, off:off + 512],
                                         psb[bank][0:1, 0:512])
                exw = [sp_done[(k - 4, si)]] if k >= 4 else []
                st_ev[(k, si, half)], _ = mm_group(
                    bank, mms, fstev, "vector",
                    extra_mm_waits=[sq[h2] for h2 in range(4)] + list(yld.values()),
                    extra_evac_waits=exw)
        if k % 4 == 3:
            deps = [st_ev[(kk2, si, hh)] for kk2 in range(k - 3, k + 1)
                    for si in range(2) for hh in range(2)]
            for si in range(2):
                S.dma("sync", lambda e, k=k, si=si: e.dma_start(
                    out=sta_in[si:si + 1, (k - 3) * TC2:(k + 1) * TC2],
                    in_=arena[0:1, o3_sr + si * 4096:o3_sr + si * 4096 + 4096]),
                    g["s_spst"], waits=deps)
            tpair = S.tok(g["s_spst"])
            for kk2 in range(k - 3, k + 1):
                for si in range(2):
                    sp_done[(kk2, si)] = tpair

    ar2 = S.step("gpsimd", lambda e: e.collective_compute(
        "AllReduce", OP.add, ins=[sta_in], outs=[sta_out],
        replica_groups=grp_quad),
        waits=[sp_done[(NK2 - 1, 0)], sp_done[(NK2 - 1, 1)]], inc=g["s_cc"])
    m1 = m2 = None
    for c4 in range(4):
        c0 = c4 * 2048
        oA, oB, oC, oD, oE = (o3_sr, o3_sr + 2048, o3_sr + 4096,
                              o3_sr + 6144, o3_sr + 8192)
        la = S.dma("sync", lambda e, c0=c0, oA=oA: e.dma_start(
            out=arena[0:1, oA:oA + 2048], in_=sta_out[0:1, c0:c0 + 2048]),
            g["s_pam"], waits=[ar2, st_ev[(NK2 - 1, 1, 1)], m1, m2])
        lb = S.dma("sync", lambda e, c0=c0, oB=oB: e.dma_start(
            out=arena[0:1, oB:oB + 2048], in_=sta_out[1:2, c0:c0 + 2048]),
            g["s_pam"], waits=[ar2, m1, m2])
        la = S.tok(g["s_pam"])
        lb = la
        S.step("vector", lambda e, oA=oA, oD=oD: e.tensor_scalar_mul(
            arena[0:1, oD:oD + 2048], arena[0:1, oA:oA + 2048], 1.0 / DI),
            waits=[la])
        S.step("vector", lambda e, oB=oB, oC=oC: e.tensor_scalar_mul(
            arena[0:1, oC:oC + 2048], arena[0:1, oB:oB + 2048], 1.0 / DI),
            waits=[lb], drain=True)
        S.step("vector", lambda e, oA=oA, oD=oD: e.tensor_tensor(
            arena[0:1, oA:oA + 2048], arena[0:1, oD:oD + 2048],
            arena[0:1, oD:oD + 2048], OP.mult), drain=True)
        v1 = S.step("vector", lambda e, oA=oA, oB=oB, oC=oC: e.tensor_tensor(
            arena[0:1, oB:oB + 2048], arena[0:1, oC:oC + 2048],
            arena[0:1, oA:oA + 2048], OP.subtract), inc=g["s_norm"],
            drain=True)
        v2 = S.step("scalar", lambda e, oB=oB, oC=oC: e.activation(
            arena[0:1, oC:oC + 2048], arena[0:1, oB:oB + 2048], AF.Sqrt,
            bias=t_eps[0:1, 0:1]), waits=[v1], inc=g["s_stat"])
        v3 = S.step("vector", lambda e, oC=oC, oE=oE: e.reciprocal(
            arena[0:1, oE:oE + 2048], arena[0:1, oC:oC + 2048]),
            waits=[v2], inc=g["s_norm"], drain=True)
        m1 = S.dma("sync", lambda e, c0=c0, oD=oD: e.dma_start(
            out=mu_dram[0:1, c0:c0 + 2048], in_=arena[0:1, oD:oD + 2048]),
            g["s_spmu"], waits=[v3])
        m2 = S.dma("sync", lambda e, c0=c0, oE=oE: e.dma_start(
            out=rs_dram[0:1, c0:c0 + 2048], in_=arena[0:1, oE:oE + 2048]),
            g["s_spmu"], waits=[v3])

    wol = S.dma("sync", lambda e: e.dma_start(
        out=A(o3_wo, 4 * 1024).rearrange("p (g c) -> p g c", g=4),
        in_=w_outT.rearrange("(g p) c -> p g c", p=128)), g["s_p4w"])
    z_all = S.tok(g["s_spz"])
    mu_all = S.tok(g["s_spmu"])
    sev = {}
    s_sp_tok = {}
    pb_done = {}
    gt_hist = {}
    wel_a = wel_b = None
    for k in range(NK2):
        if k == 2:
            wel_a = S.dma("gpsimd", lambda e: e.dma_start(
                out=A(o4_we, 4 * 1024).rearrange("p (k c) -> p k c", k=4),
                in_=we[0:512, :].rearrange("(k p) c -> p k c", p=128)),
                g["s_wek"], waits=[m1, m2])
            wel_b = S.dma("gpsimd", lambda e: e.dma_start(
                out=A(o4_we + 4 * 1024, 4 * 1024).rearrange("p (k c) -> p k c", k=4),
                in_=we[512:1024, :].rearrange("(k p) c -> p k c", p=128)),
                g["s_wek1"], waits=[m1, m2])
        oy = o3_y if k % 2 == 0 else o3_y2
        oz = o3_z if k % 2 == 0 else o3_z2
        om = o3_mu if k % 2 == 0 else o3_mu2
        ors = o3_rs if k % 2 == 0 else o3_rs2
        sm = g["s_pbm" if k % 2 == 0 else "s_pbm1"]
        sy = g["s_pby" if k % 2 == 0 else "s_pby1"]
        sz = g["s_pbz" if k % 2 == 0 else "s_pbz1"]
        mw = [mu_all, gt_hist.get(k - 2)]
        mub = S.dma("sync", lambda e, k=k, om=om: e.dma_start(
            out=A(om, TC2),
            in_=mu_dram[0:1, k * TC2:(k + 1) * TC2].broadcast_to([128, TC2])),
            sm, waits=mw)
        S.dma("sync", lambda e, k=k, ors=ors: e.dma_start(
            out=A(ors, TC2),
            in_=rs_dram[0:1, k * TC2:(k + 1) * TC2].broadcast_to([128, TC2])),
            sm, waits=mw)
        mub = S.tok(sm)
        rsb = mub
        gtok = {}
        for h in range(4):
            S.dma("gpsimd", lambda e, k=k, h=h, oy=oy: e.dma_start(
                out=A(oy + h * TC2, TC2),
                in_=y_hbm[h * 128:(h + 1) * 128, k * TC2:(k + 1) * TC2]),
                sy, waits=[y_all_sp, y_all_sp1, pb_done.get(k - 2)])
            S.dma("scalar", lambda e, k=k, h=h, oz=oz: e.dma_start(
                out=A(oz + h * TC2, TC2),
                in_=z_hbm[h * 128:(h + 1) * 128, k * TC2:(k + 1) * TC2]),
                sz, waits=[z_all, gt_hist.get(k - 2)])
        pby_tok = S.tok(sy)
        pbz_tok = S.tok(sz)
        for h in range(4):
            gw = [pby_tok, mub, pb_done.get(k - 2)]
            S.step("vector", lambda e, h=h, oy=oy, om=om: e.tensor_tensor(
                A(o3_g + h * TC2, TC2), A(oy + h * TC2, TC2), A(om, TC2),
                OP.subtract), waits=gw)
            S.step("vector", lambda e, h=h, oy=oy, ors=ors: e.tensor_tensor(
                A(oy + h * TC2, TC2), A(o3_g + h * TC2, TC2), A(ors, TC2),
                OP.mult), waits=[rsb])
            S.step("vector", lambda e, h=h, oy=oy: e.tensor_scalar(
                A(o3_g + h * TC2, TC2), A(oy + h * TC2, TC2),
                t_gb[:, h:h + 1], t_gb[:, HPC + h:HPC + h + 1],
                OP.mult, OP.add), inc=g["s_norm"])
            sg = S.step("scalar", lambda e, h=h, oz=oz: e.activation(
                A(oz + h * TC2, TC2), A(oz + h * TC2, TC2), AF.Sigmoid),
                waits=[pbz_tok], inc=g["s_stat"], chain=False)
            gtok[h] = S.step("vector", lambda e, h=h, oy=oy, oz=oz: e.tensor_tensor(
                A(oy + h * TC2, TC2), A(o3_g + h * TC2, TC2),
                A(oz + h * TC2, TC2), OP.mult), waits=[sg], inc=g["s_norm"])
        gt_hist[k] = gtok[3]
        for m in range(TC2 // 128):
            for half in range(2):
                bank = half + 2 * (m % 2)
                mms = []
                for cg in range(4):
                    def fop(e, m=m, cg=cg, half=half, bank=bank, oy=oy):
                        return e.matmul(
                            psb[bank][:, 0:512],
                            arena[:, oy + cg * TC2 + m * 128:
                                  oy + cg * TC2 + (m + 1) * 128],
                            A(o3_wo + cg * 1024 + half * 512, 512),
                            start=(cg == 0), stop=(cg == 3))
                    mms.append(fop)
                def fopev(e, m=m, half=half, bank=bank):
                    return e.tensor_copy(
                        A(o3_s + (m % 2) * 1024 + half * 512, 512),
                        psb[bank][:, 0:512])
                ms = k * (TC2 // 128) + m
                sev[(k, m, half)], _ = mm_group(
                    bank, mms, fopev, "vector",
                    extra_mm_waits=[gtok[h2] for h2 in range(4)] + [wol],
                    extra_evac_waits=[s_sp_tok.get(ms - 2)])
            s_sp_tok[ms] = S.dma("sync", lambda e, k=k, m=m: e.dma_start(
                out=s_in[k * TC2 + m * 128:k * TC2 + (m + 1) * 128, :],
                in_=A(o3_s + (m % 2) * 1024, 1024)),
                g["s_sps0" if m % 2 == 0 else "s_sps1"],
                waits=[sev[(k, m, 0)], sev[(k, m, 1)]])
        pb_done[k] = sev[(k, TC2 // 128 - 1, 1)]

    p3_last_ev = sev[(NK2 - 1, TC2 // 128 - 1, 1)]
    rs1 = S.step("gpsimd", lambda e: e.collective_compute(
        "ReduceScatter", OP.add, ins=[s_in], outs=[s_out],
        replica_groups=grp_quad),
        waits=[S.tok(g["s_sps0"]), S.tok(g["s_sps1"])], inc=g["s_cc"])
    if _DEBUG:
        dbg_so = dd("dbg_so", [QT, D], "ExternalOutput")
        S.dma("sync", lambda e: e.dma_start(out=dbg_so, in_=s_out),
              g["s_dbg"], waits=[rs1])
        dbg_to = dd("dbg_to", [8, E], "ExternalOutput")
        S.dma("sync", lambda e: e.dma_start(out=dbg_to, in_=tot_out),
              g["s_dbg"], waits=[ag1])

    if _PHASES < 4:
        return _finish(nc, S, ctx)

    sps0_tok = S.tok(g["s_sps0"])
    sps1_tok = S.tok(g["s_sps1"])
    bel = S.dma("sync", lambda e: e.dma_start(
        out=A(o4_be, 1024), in_=be.broadcast_to([128, 1024])), g["s_p4b"],
        waits=[p3_last_ev])
    prev_tn4 = None
    tn4, lpe4, xn_lev = {}, {}, {}
    for j in range(16):
        xb = o4_xq + (j % 4) * 1024
        w = [tn4.get(j - 4)] if j >= 4 else [p3_last_ev]
        tl = S.dma("sync", lambda e, xb=xb, j=j: e.dma_start(
            out=A(xb, 1024), in_=x_q[j * 128:(j + 1) * 128, :]),
            g[("s_xq0", "s_xq1", "s_xq2", "s_xq3")[j % 4]], waits=w)
        fpw = p3_last_ev if j == 0 else None
        tn, lpe, lev = ln_tile(tl, xb, o4_scr, o4_scr + 1024,
                               o4_g2 + (j % 2) * 1024,
                               o4_xnT, QT, j * 128, (6, 7),
                               prev_tn4, first_pe_wait=fpw,
                               xnd_wait=lpe4.get(j - 2))
        prev_tn4 = tn
        tn4[j], lpe4[j], xn_lev[j] = tn, lpe, lev

    ttl = S.dma("sync", lambda e: e.dma_start(out=t_totL[:], in_=tot_out),
                g["s_p4t"], waits=[ag1])
    def fb(e):
        return e.matmul(psb[5][0:1, 0:E], t_step8[:], t_totL[:],
                        start=True, stop=True)
    def fbev(e):
        inst = e.tensor_copy(t_small[0:1, 36:44], psb[5][0:1, 0:E])
        return inst
    bev, _ = mm_group(5, [fb], fbev, "vector", extra_mm_waits=[ttl, all_ld])
    bd = S.dma("sync", lambda e: e.dma_start(
        out=base_dram, in_=t_small[0:1, 36:44]), g["s_p4bd"], waits=[bev])
    bb2 = S.dma("sync", lambda e: e.dma_start(
        out=A(o4_bs, 8), in_=base_dram.broadcast_to([128, 8])), g["s_p4bs"],
        waits=[bd])
    ltl = None
    for m in range(16):
        wv = [ld1]
        if bank_free[5] is not None:
            wv.append(bank_free[5])
        pt = S.step("tensor", lambda e, m=m: e.transpose(
            psb[5][:, 0:E], t_loc[:, m * 128:(m + 1) * 128], t_eye[0:E, 0:E]),
            waits=wv, inc=g["s_tp"])
        ltl = S.step("vector", lambda e, m=m: e.tensor_copy(
            arena[:, o4_lt + m * E:o4_lt + (m + 1) * E], psb[5][:, 0:E]),
            waits=[pt], inc=g["s_ev"], drain=True)
        bank_free[5] = ltl
    sc_tok = {}
    for m in range(16):
        S.step("vector", lambda e, m=m: e.tensor_tensor(
            t_small[:, 24:32], arena[:, o4_lt + m * 8:o4_lt + (m + 1) * 8],
            A(o4_bs, 8), OP.add), waits=[bb2, ltl], drain=True)
        S.step("vector", lambda e: e.tensor_scalar(
            t_small[:, 44:52], t_small[:, 24:32], float(CAP), None, OP.is_le),
            drain=True)
        S.step("vector", lambda e, m=m: e.tensor_tensor(
            t_small[:, 24:32], t_small[:, 44:52],
            t_mask[:, m * E:(m + 1) * E], OP.mult), drain=True)
        S.step("vector", lambda e: e.reduce_sum(
            t_small[:, 32:33], t_small[:, 24:32], axis=AX.X), drain=True)
        sc_tok[m] = S.step("vector", lambda e, m=m: e.tensor_tensor(
            t_scale[:, m:m + 1], t_small[:, 32:33], t_gate[:, m:m + 1],
            OP.mult), inc=g["s_norm"], drain=True)

    out_hist = {}
    lpe_last = lpe4[15]
    for m in range(16):
        par = m % 2
        par4 = m % 4
        evs4 = []
        for half in range(2):
            bank = half + 2 * (m % 2)
            mm_tok = None
            for kd in range(8):
                w = [wel_a if kd < 4 else wel_b]
                if kd == 0:
                    w.append(xn_lev[m])
                    if bank_free[bank] is not None:
                        w.append(bank_free[bank])
                mm_tok = S.step(
                    "tensor", lambda e, m=m, kd=kd, half=half, bank=bank: e.matmul(
                        psb[bank][:, 0:512],
                        arena[:, o4_xnT + kd * QT + m * 128:
                              o4_xnT + kd * QT + (m + 1) * 128],
                        A(o4_we + kd * 1024 + half * 512, 512),
                        start=(kd == 0), stop=(kd == 7)),
                    waits=w, inc=(g["s_mm"] if kd == 7 else None))
            def fev(e, par4=par4, half=half, bank=bank):
                return e.tensor_copy(
                    A(o4_eo + par4 * 1024 + half * 512, 512), psb[bank][:, 0:512])
            ev = S.step("vector", fev,
                        waits=[mm_tok, out_hist.get(m - 4),
                               sps0_tok, sps1_tok], inc=g["s_ev"])
            bank_free[bank] = ev
            evs4.append(ev)
        t1w = [bel] + evs4 + ([lpe_last] if m < 2 else [])
        t1 = S.step("vector", lambda e, par=par, par4=par4: e.tensor_tensor(
            A(o4_g2 + par * 1024, 1024), A(o4_eo + par4 * 1024, 1024),
            A(o4_be, 1024), OP.add), waits=t1w, inc=g["s_norm"])
        t2 = S.step("scalar", lambda e, m=m, par=par, par4=par4: e.activation(
            A(o4_eo + par4 * 1024, 1024), A(o4_g2 + par * 1024, 1024), AF.Silu,
            scale=t_scale[:, m:m + 1]), waits=[t1, sc_tok[m]], inc=g["s_stat"],
            chain=False)
        sl = S.dma("gpsimd", lambda e, m=m, par=par: e.dma_start(
            out=A(o4_s + par * 1024, 1024), in_=s_out[m * 128:(m + 1) * 128, :]),
            g["s_p4s" if par == 0 else "s_p4s1"], waits=[rs1, out_hist.get(m - 2)])
        xlw = [out_hist.get(m - 2)] + ([p3_last_ev] if m < 2 else [])
        xl = S.dma("scalar", lambda e, m=m, par=par: e.dma_start(
            out=A(o4_x + par * 1024, 1024), in_=x_q[m * 128:(m + 1) * 128, :]),
            g["s_p4x" if par == 0 else "s_p4x1"], waits=xlw)
        t3 = S.step("vector", lambda e, par=par, par4=par4: e.tensor_tensor(
            A(o4_g2 + par * 1024, 1024), A(o4_eo + par4 * 1024, 1024),
            A(o4_s + par * 1024, 1024), OP.mult), waits=[t2, sl], inc=g["s_norm"])
        t4 = S.step("vector", lambda e, par=par, par4=par4: e.tensor_tensor(
            A(o4_eo + par4 * 1024, 1024), A(o4_g2 + par * 1024, 1024),
            A(o4_x + par * 1024, 1024), OP.add), waits=[xl], inc=g["s_norm"])
        out_hist[m] = S.dma("sync", lambda e, m=m, par4=par4: e.dma_start(
            out=out_c[m * 128:(m + 1) * 128, :], in_=A(o4_eo + par4 * 1024, 1024)),
            g["s_out" if par == 0 else "s_out1"], waits=[t4])

    S.step("vector", lambda e: e.engine_nop(),
           waits=[out_hist[14], out_hist[15]])
    return _finish(nc, S, ctx)


def _finish(nc, S, ctx):
    with nc.Block() as block:
        @block.sync
        def _(eng):
            S.replay("sync", eng)

        @block.scalar
        def _(eng):
            S.replay("scalar", eng)

        @block.vector
        def _(eng):
            S.replay("vector", eng)

        @block.tensor
        def _(eng):
            S.replay("tensor", eng)

        @block.gpsimd
        def _(eng):
            S.replay("gpsimd", eng)

    for cm in reversed(ctx):
        cm.__exit__(None, None, None)
    return nc


def host_inputs(core, inputs):
    b, q = core // 4, core % 4
    ch0 = CPC * q
    W_in = np.asarray(inputs["L0_attn_W_in"], np.float32)
    m = {}
    xb = np.ascontiguousarray(np.asarray(
        inputs["L0_attn_hidden_states"], np.float32)[b])
    m["x_b"] = xb
    m["x_q"] = np.ascontiguousarray(xb[q * QT:(q + 1) * QT])
    wx = W_in[ch0:ch0 + CPC]
    wz = W_in[DI + ch0:DI + ch0 + CPC]
    m["w_inT"] = np.ascontiguousarray(np.concatenate([wx, wz], 0).T)
    m["w_dtT"] = np.ascontiguousarray(
        np.asarray(inputs["L0_attn_W_dt"], np.float32)[:, ch0:ch0 + CPC].T)
    m["w_conv"] = np.ascontiguousarray(
        np.asarray(inputs["L0_attn_W_conv"], np.float32).reshape(DI, 4)[ch0:ch0 + CPC]
        .reshape(4, 128, 4).transpose(1, 0, 2).reshape(128, 16))
    m["b_conv"] = np.ascontiguousarray(
        np.asarray(inputs["L0_attn_b_conv"], np.float32)[ch0:ch0 + CPC].reshape(4, 128).T)
    al = np.asarray(inputs["L0_attn_A_log"], np.float32).reshape(HN, HD, DS)[4 * q:4 * q + 4]
    m["a_log"] = np.ascontiguousarray(al.transpose(1, 0, 2).reshape(HD, HPC * DS))
    m["d_col"] = np.ascontiguousarray(
        np.asarray(inputs["L0_attn_D"], np.float32).reshape(HN, HD)[4 * q:4 * q + 4].T)
    m["dtbias"] = np.ascontiguousarray(
        np.asarray(inputs["L0_attn_dt_bias"], np.float32)[4 * q:4 * q + 4].reshape(HPC, 1))
    bdt = np.asarray(inputs["L0_attn_b_dt"], np.float32).reshape(144, 1)
    m["b_dt"] = np.ascontiguousarray(bdt if q == 0 else np.zeros_like(bdt))
    m["gamma"] = np.ascontiguousarray(
        np.asarray(inputs["L0_attn_ln_gamma"], np.float32).reshape(HN, HD)[4 * q:4 * q + 4].T)
    m["beta"] = np.ascontiguousarray(
        np.asarray(inputs["L0_attn_ln_beta"], np.float32).reshape(HN, HD)[4 * q:4 * q + 4].T)
    m["w_outT"] = np.ascontiguousarray(
        np.asarray(inputs["L0_attn_W_out"], np.float32)[:, ch0:ch0 + CPC].T)
    m["we"] = np.ascontiguousarray(np.asarray(inputs["L0_ffn_We"], np.float32))
    m["be"] = np.ascontiguousarray(
        np.asarray(inputs["L0_ffn_be"], np.float32).reshape(1, D))
    m["wg"] = np.ascontiguousarray(np.asarray(inputs["L0_ffn_Wg"], np.float32))
    m["eye"] = np.eye(128, dtype=np.float32)
    qs = np.zeros((128, NK1), np.float32)
    qs[:, 2 * q] = 1.0
    qs[:, 2 * q + 1] = 1.0
    m["qsel"] = qs
    s16 = np.zeros((16, 4), np.float32)
    for hh in range(4):
        s16[4 * q + hh, hh] = 1.0
    m["sel16"] = s16
    st8 = np.zeros((8, 1), np.float32)
    st8[0:core, 0] = 1.0
    m["step8"] = st8
    return m


def kernel(**inputs):
    nc = build()
    in_maps = [host_inputs(c, inputs) for c in range(8)]
    res = run_bass_kernel_spmd(nc, in_maps, core_ids=list(range(8)))
    out = np.empty((B, T, D), np.float32)
    for c in range(8):
        b, q = c // 4, c % 4
        out[b, q * QT:(q + 1) * QT] = np.asarray(res.results[c]["out_c"])
    return out
